# version 35
# baseline (speedup 1.0000x reference)
import contextlib
import math
import numpy as np

import concourse.bass as bass
import concourse.mybir as mybir
from concourse.bass_utils import run_bass_kernel_spmd

F32 = mybir.dt.float32
AF = mybir.ActivationFunctionType
ALU = mybir.AluOpType
AX = mybir.AxisListType

HALF_PI = math.pi / 2.0


class CFG:
    B = 2
    S = 8192
    D = 1024
    H = 16
    HD = 64
    I_FFN = 2816
    N_CORES = 8
    C = 128
    LN_EPS = 1e-5
    RMS_EPS = 1e-6
    L2_EPS = 1e-6

    def __init__(self, **kw):
        for k, v in kw.items():
            setattr(self, k, v)
        self.SPANS = self.N_CORES // self.B
        self.NT = self.B * self.S // self.N_CORES
        self.NTILE = self.NT // self.C
        self.NG = (self.D // 2) // 128
        self.QCH = self.D // 128
        self.ICH = self.I_FFN // 128
        self.HP = self.H // 2


ENGINES = ("tensor", "vector", "scalar", "gpsimd", "sync")


class Op:
    __slots__ = ("eng", "fn", "reads", "writes", "inc", "need_inc",
                 "cum_after", "deps", "tag", "sem", "lane_wait", "gself",
                 "lwg")

    def __init__(self, eng, fn, reads, writes, inc, tag):
        self.eng, self.fn = eng, fn
        self.reads, self.writes = tuple(reads), tuple(writes)
        self.inc = inc
        self.need_inc = False
        self.cum_after = None
        self.deps = []
        self.tag = tag


class Sched:
    def __init__(self):
        self.ops = []
        self.last_writer = {}
        self.readers = {}

    def add(self, eng, fn, reads=(), writes=(), inc=1, tag=""):
        op = Op(eng, fn, reads, writes, inc, tag)
        deps = []
        for k in op.reads:
            w = self.last_writer.get(k)
            if w is not None:
                deps.append(w)
        for k in op.writes:
            w = self.last_writer.get(k)
            if w is not None:
                deps.append(w)
            deps.extend(self.readers.get(k, ()))
        seen = set()
        for d in deps:
            if id(d) not in seen and d is not op:
                seen.add(id(d))
                op.deps.append(d)
        self.ops.append(op)
        for k in op.reads:
            self.readers.setdefault(k, []).append(op)
        for k in op.writes:
            self.last_writer[k] = op
            self.readers[k] = []
        return op

    N_LANES = 4

    def finalize(self):
        self.cc_idx = 0
        per_eng = {e: [] for e in ENGINES}
        for op in self.ops:
            per_eng[op.eng].append(op)
        for op in self.ops:
            if op.inc == 16 or op.tag.startswith("ag_"):
                op.need_inc = True
            for d in op.deps:
                d.need_inc = True
        for e in ENGINES:
            cum = 0
            lane_rr = 0
            lane_cum = {}
            for op in per_eng[e]:
                if op.tag.startswith("ag_"):
                    ln = "cc%d" % self.cc_idx
                    self.cc_idx += 1
                    op.sem = ln
                    op.lane_wait = 0
                    op.cum_after = op.inc
                elif op.inc == 16:
                    ln = lane_rr % self.N_LANES
                    lane_rr += 1
                    op.sem = (e, ln)
                    op.lane_wait = lane_cum.get(ln, 0)
                    lane_cum[ln] = op.lane_wait + op.inc
                    op.cum_after = lane_cum[ln]
                else:
                    op.sem = e
                    op.lane_wait = None
                    if op.need_inc:
                        cum += op.inc
                    op.cum_after = cum
        eng_g = {}
        lane_g = {}
        for op in self.ops:
            gm = {}
            for d in op.deps:
                for s, v in d.gself.items():
                    if gm.get(s, -1) < v:
                        gm[s] = v
            if op.lane_wait is None and not op.sem.__class__ is tuple \
                    and not str(op.sem).startswith("cc"):
                base = eng_g.setdefault(op.eng, {})
                for s, v in gm.items():
                    if base.get(s, -1) < v:
                        base[s] = v
                if op.need_inc:
                    base[op.sem] = op.cum_after
                op.gself = dict(base)
            elif op.lane_wait is not None and op.sem.__class__ is tuple:
                base = lane_g.setdefault(op.sem, {})
                op.lwg = dict(base)
                for s, v in gm.items():
                    if base.get(s, -1) < v:
                        base[s] = v
                base[op.sem] = op.cum_after
                op.gself = dict(base)
            else:
                gm[op.sem] = op.cum_after
                op.gself = gm
        return per_eng


class Arena:

    def __init__(self, b, name, cols):
        self.name = name
        self.t = b.ctx.enter_context(b.nc.sbuf_tensor(name, [128, cols], F32))
        self.cols = cols

    def view(self, c0, ncols, p0=0, np_=128):
        ap = self.t[p0:p0 + np_, c0:c0 + ncols]
        r0, r1 = c0 // 1024, (c0 + ncols - 1) // 1024
        keys = [f"{self.name}.{r}" for r in range(r0, r1 + 1)]
        return ap, keys


class B:
    def __init__(self, nc, cfg):
        self.nc = nc
        self.cfg = cfg
        self.sch = Sched()
        self.ctx = contextlib.ExitStack()
        self.sb = {}
        self.dr = {}
        self.cn = {}
        self.ps = []

    def sbuf(self, name, shape):
        t = self.ctx.enter_context(self.nc.sbuf_tensor("s_" + name, shape, F32))
        self.sb[name] = t.ap()
        return self.sb[name]

    def const(self, name):
        off, w = self.cn[name]
        return self.sb["consts"][:, off:off + w]

    def op(self, eng, fn, r=(), w=(), inc=1, tag=""):
        return self.sch.add(eng, fn, r, w, inc, tag)

    def dma(self, q, out_ap, in_ap, r=(), w=(), tag=""):
        return self.op(q, lambda e, o=out_ap, i=in_ap: e.dma_start(out=o, in_=i),
                       r, w, inc=16, tag=tag)

    def mmg(self, items, r=(), w=(), tag=""):
        def fn(e, items=tuple(items)):
            first = ins = None
            for (o, l, rr, st, sp, tp) in items:
                if tp is None:
                    ins = e.matmul(o, l, rr, start=st, stop=sp)
                else:
                    ins = e.matmul(o, l, rr, start=st, stop=sp, tile_position=tp)
                if first is None:
                    first = ins
            return (first, ins)
        return self.op("tensor", fn, r, w, tag=tag)

    def mm(self, out, lhsT, rhs, start=True, stop=True, r=(), w=(), tp=None, tag=""):
        return self.mmg([(out, lhsT, rhs, start, stop, tp)], r, w, tag)

    def tp(self, out_psum, in_ap, r=(), w=(), ident=None, tpos=None, tag=""):
        if ident is None:
            kk = in_ap.shape[0]
            ident = self.const("ident")[0:kk, 0:kk]
        idap = ident
        if tpos is None:
            return self.op("tensor",
                           lambda e, o=out_psum, i=in_ap, d=idap:
                           e.transpose(o, i, d), r, w, tag=tag)
        return self.op("tensor",
                       lambda e, o=out_psum, i=in_ap, d=idap, t=tpos:
                       e.transpose(o, i, d, tile_position=t), r, w, tag=tag)

    def tt(self, out, a, bb, alu, r=(), w=(), eng="vector", tag=""):
        return self.op(eng, lambda e, o=out, x=a, y=bb, op=alu: e.tensor_tensor(
            out=o, in0=x, in1=y, op=op), r, w, tag=tag)

    def ts(self, out, a, s1, op0, s2=None, op1=None, r=(), w=(), eng="vector", tag=""):
        if op1 is None:
            return self.op(eng, lambda e, o=out, x=a, s=s1, op=op0:
                           e.tensor_single_scalar(out=o, in_=x, scalar=s, op=op),
                           r, w, tag=tag)
        return self.op(eng, lambda e, o=out, x=a, sa=s1, sb=s2, oa=op0, ob=op1:
                       e.tensor_scalar(out=o, in0=x, scalar1=sa, scalar2=sb,
                                       op0=oa, op1=ob), r, w, tag=tag)

    def stt(self, out, in0, s, in1, op0, op1, r=(), w=(), eng="vector", tag=""):
        return self.op(eng, lambda e, o=out, x=in0, sc=s, y=in1, oa=op0, ob=op1:
                       e.scalar_tensor_tensor(out=o, in0=x, scalar=sc, in1=y,
                                              op0=oa, op1=ob), r, w, tag=tag)

    def act(self, out, in_, func, bias=0.0, scale=1.0, r=(), w=(), tag=""):
        return self.op("scalar", lambda e, o=out, i=in_, f=func, bb=bias, ss=scale:
                       e.activation(out=o, in_=i, func=f, bias=bb, scale=ss),
                       r, w, tag=tag)

    def copy(self, out, in_, r=(), w=(), eng="scalar", tag=""):
        if eng == "scalar":
            return self.op(eng, lambda e, o=out, i=in_: e.copy(out=o, in_=i),
                           r, w, tag=tag)
        return self.op(eng, lambda e, o=out, i=in_: e.tensor_copy(out=o, in_=i),
                       r, w, tag=tag)

    def red(self, out, in_, op, absval=False, r=(), w=(), tag=""):
        return self.op("vector", lambda e, o=out, i=in_, oo=op, av=absval:
                       e.tensor_reduce(out=o, in_=i, axis=AX.X, op=oo,
                                       apply_absolute_value=av), r, w, tag=tag)

    def scan(self, out, d0, d1, init, r=(), w=(), tag=""):
        return self.op("vector", lambda e, o=out, a=d0, bb=d1, ii=init:
                       e.tensor_tensor_scan(out=o, data0=a, data1=bb, initial=ii,
                                            op0=ALU.mult, op1=ALU.add),
                       r, w, tag=tag)

    def memset(self, ap, v, r=(), w=(), eng="vector", tag=""):
        return self.op(eng, lambda e, a=ap, vv=v: e.memset(a, vv), r, w, tag=tag)

    def recip(self, out, in_, r=(), w=(), tag=""):
        return self.op("vector", lambda e, o=out, i=in_: e.reciprocal(out=o, in_=i),
                       r, w, tag=tag)

    def allgather(self, in_ap, out_ap, r=(), w=(), tag=""):
        n = self.cfg.N_CORES
        return self.op("gpsimd", lambda e, i=in_ap, o=out_ap, nn=n:
                       e.collective_compute("AllGather", ALU.bypass,
                                            replica_groups=[list(range(nn))],
                                            ins=[i.opt()], outs=[o.opt()]),
                       r, w, inc=1, tag=tag)

    def quant(self, out3, v3, rexp, hexp, lanes, lkeys, rk, wk):
        b = self
        L0, L1, L2, L3 = lanes
        k0, k1, k2, k3 = lkeys
        b.tt(L0, v3, rexp, ALU.mult, r=rk, w=[k0])
        b.act(L1, L0, AF.Abs, r=[k0], w=[k1])
        b.ts(L2, L1, 2.5, ALU.is_gt, r=[k1], w=[k2], eng="vector")
        b.stt(L2, L1, 3.5, L2, ALU.is_gt, ALU.add, r=[k1, k2], w=[k2], eng="vector")
        b.stt(L2, L1, 5.0, L2, ALU.is_gt, ALU.add, r=[k1, k2], w=[k2], eng="vector")
        b.stt(L2, L1, 5.0, L2, ALU.is_gt, ALU.add, r=[k1, k2], w=[k2], eng="vector")
        b.ts(L3, L1, 0.25, ALU.is_gt, r=[k1], w=[k3])
        for thr in (0.75, 1.25, 1.75):
            b.stt(L3, L1, thr, L3, ALU.is_gt, ALU.add, r=[k1, k3], w=[k3])
        b.stt(L3, L2, 2.0, L3, ALU.mult, ALU.add, r=[k2, k3], w=[k3])
        b.ts(L1, L0, 0.0, ALU.is_lt, r=[k0], w=[k1], eng="vector")
        b.ts(L1, L1, -2.0, ALU.mult, 1.0, ALU.add, r=[k1], w=[k1], eng="vector")
        b.tt(L3, L3, L1, ALU.mult, r=[k1, k3], w=[k3])
        b.tt(out3, L3, hexp, ALU.mult, r=[k3] + list(rk), w=wk)



def make_consts(cfg):
    cols, names = [], {}

    def add(arr, name):
        arr = np.asarray(arr, np.float32)
        if arr.ndim == 1:
            arr = arr[:, None]
        pad = np.zeros((128, arr.shape[1]), np.float32)
        pad[: arr.shape[0]] = arr
        names[name] = (sum(c.shape[1] for c in cols), arr.shape[1])
        cols.append(pad)

    add(np.eye(128, dtype=np.float32), "ident")
    add(-np.tril(np.ones((128, 128), np.float32), -1).T, "mask_sT_neg")
    add(-np.tril(np.ones((128, 128), np.float32), -1), "mask_s_neg")
    add(np.tril(np.ones((128, 128), np.float32), 0).T, "mask_iT")
    add(np.ones((128, 128), np.float32), "ones_blk")
    e2 = np.zeros((2, 128), np.float32)
    e2[0, :64] = 1.0
    e2[1, 64:] = 1.0
    add(e2, "E2")
    add(e2 * (cfg.HD ** -0.5), "E2s")
    r2 = np.zeros((128, 2), np.float32)
    r2[:64, 0] = 1.0
    r2[64:, 1] = 1.0
    add(r2, "R2")
    add(np.concatenate([np.eye(64, dtype=np.float32)] * 2, axis=0), "I64pair")
    for p in range(cfg.H // 2):
        ep = np.zeros((16, 128), np.float32)
        ep[2 * p, :64] = 1.0
        ep[2 * p + 1, 64:] = 1.0
        add(ep, f"E16_{p}")
    add(np.full(128, -HALF_PI, np.float32), "neg_half_pi")
    add(np.full(128, cfg.LN_EPS, np.float32), "ln_eps")
    add(np.full(128, cfg.RMS_EPS, np.float32), "rms_eps")
    add(np.full(128, cfg.RMS_EPS * cfg.HD, np.float32), "rms_eps_hd")
    add(np.full(128, cfg.L2_EPS, np.float32), "l2_eps")
    return np.concatenate(cols, axis=1), names


def _fm_cols(v):
    return np.ascontiguousarray(np.asarray(v, np.float32).reshape(-1, 128).T)


def _conv_cols(w):
    w = np.asarray(w, np.float32)
    nf = w.shape[0] // 128
    out = np.zeros((128, nf * 4), np.float32)
    for f in range(nf):
        out[:, 4 * f:4 * f + 4] = w[128 * f:128 * (f + 1)]
    return np.ascontiguousarray(out)


def prep_inputs(cfg, inp):
    perm = np.concatenate([np.arange(0, cfg.D, 2), np.arange(1, cfg.D, 2)])
    x = np.asarray(inp["L0_attn_hidden"], np.float32)
    ctrl_W = np.ascontiguousarray(np.asarray(inp["L0_attn_ctrl_W"], np.float32))
    p20c_W = np.ascontiguousarray(np.asarray(inp["L0_attn_p20c_W"], np.float32)[perm])
    oW = np.asarray(inp["L0_attn_out_W"], np.float32)
    out_W = np.ascontiguousarray(np.concatenate([oW[:cfg.D], oW[cfg.D:][perm]], 0))
    w_gate = np.asarray(inp["L0_ffn_w_gate"], np.float32)
    w_up = np.asarray(inp["L0_ffn_w_up"], np.float32)
    w_down = np.asarray(inp["L0_ffn_w_down"], np.float32)

    consts, cnames = make_consts(cfg)
    cb = np.asarray(inp["L0_attn_ctrl_b"], np.float32)
    ctrl_b = np.zeros((128, 26), np.float32)
    for m in range(24):
        ctrl_b[:, m] = cb[128 * m: 128 * (m + 1)]
    ctrl_b[0:16, 24] = cb[3 * cfg.D: 3 * cfg.D + cfg.H]
    ctrl_b[32:48, 25] = cb[3 * cfg.D + cfg.H: 3 * cfg.D + 2 * cfg.H]
    gate_b = np.ascontiguousarray(cb[3 * cfg.D + 2 * cfg.H:][None, :])
    out_b = np.ascontiguousarray(np.asarray(inp["L0_attn_out_b"], np.float32)[None, :])
    convw = np.concatenate([_conv_cols(inp["L0_attn_qconv_w"]),
                            _conv_cols(inp["L0_attn_kconv_w"]),
                            _conv_cols(inp["L0_attn_vconv_w"])], axis=1)
    vnorm = _fm_cols(np.asarray(inp["L0_attn_vnorm_scale"], np.float32)[perm])
    mnorm = np.ascontiguousarray(
        np.tile(np.asarray(inp["L0_attn_mnorm_scale"], np.float32), 2)[:, None])
    ramp = _fm_cols(inp["L0_attn_ramp_logit"])
    p20_dl = np.ascontiguousarray(
        np.asarray(inp["L0_attn_p20_decay_log"], np.float32).reshape(cfg.NG, 128).T)
    p20_th = np.ascontiguousarray(
        np.asarray(inp["L0_attn_p20_theta"], np.float32).reshape(cfg.NG, 128).T)

    span = cfg.S // cfg.SPANS
    sh = cfg.D // cfg.N_CORES
    si = cfg.I_FFN // cfg.N_CORES
    in_maps = []
    for c in range(cfg.N_CORES):
        bb = c // cfg.SPANS
        s0 = (c % cfg.SPANS) * span
        xs = np.ascontiguousarray(x[bb, s0:s0 + span])
        halo = np.zeros((3, cfg.D), np.float32)
        if s0 > 0:
            halo = np.ascontiguousarray(x[bb, s0 - 3:s0])
        pm = np.zeros((128, cfg.N_CORES), np.float32)
        for a in range(cfg.N_CORES):
            if (a // cfg.SPANS) == (c // cfg.SPANS) and (a % cfg.SPANS) < (c % cfg.SPANS):
                pm[:, a] = 1.0
        in_maps.append({
            "x_tm": xs, "x_halo": halo,
            "ctrl_W": ctrl_W, "p20c_W": p20c_W, "out_W": out_W,
            "wg_shard": np.ascontiguousarray(w_gate[:, c * sh:(c + 1) * sh]),
            "wu_shard": np.ascontiguousarray(w_up[:, c * sh:(c + 1) * sh]),
            "wd_shard": np.ascontiguousarray(w_down[:, c * si:(c + 1) * si]),
            "consts": consts, "ctrl_b": ctrl_b, "gate_b": gate_b, "out_b": out_b,
            "convw": convw, "vnorm": vnorm, "mnorm": mnorm, "ramp": ramp,
            "p20_dl": p20_dl, "p20_th": p20_th, "premask": pm,
        })
    return in_maps, cnames



def build_nc(cfg, cnames, taps=()):
    nc = bass.Bass("TRN2", target_bir_lowering=False)
    b = B(nc, cfg)
    b.cn = cnames
    b.taps = set(taps)
    NT, D, C = cfg.NT, cfg.D, cfg.C
    QCH, ICH, NG, HP, HD = cfg.QCH, cfg.ICH, cfg.NG, cfg.HP, cfg.HD
    sh = D // cfg.N_CORES
    si = cfg.I_FFN // cfg.N_CORES

    dr = b.dr
    def din(name, shape):
        dr[name] = nc.dram_tensor(name, shape, F32, kind="ExternalInput").ap()
    def dint(name, shape):
        dr[name] = nc.dram_tensor(name, shape, F32).ap()
    din("x_tm", [NT, D]); din("x_halo", [3, D])
    din("ctrl_W", [D, 4128]); din("p20c_W", [D, 3 * D]); din("out_W", [2 * D, D])
    din("wg_shard", [cfg.I_FFN, sh]); din("wu_shard", [cfg.I_FFN, sh])
    din("wd_shard", [D, si])
    ncc = sum(w for (_, w) in cnames.values())
    din("consts", [128, ncc]); din("ctrl_b", [128, 26]); din("gate_b", [1, D])
    din("out_b", [1, D]); din("convw", [128, 96]); din("vnorm", [128, QCH])
    din("mnorm", [128, 1]); din("ramp", [128, QCH]); din("p20_dl", [128, NG])
    din("p20_th", [128, NG]); din("premask", [128, cfg.N_CORES])
    dr["y"] = nc.dram_tensor("y", [NT, D], F32, kind="ExternalOutput").ap()
    for nm in b.taps:
        dr["tap_" + nm] = nc.dram_tensor("tap_" + nm, [128, 8192], F32,
                                         kind="ExternalOutput").ap()
    dint("sc_p20", [128, QCH * NT])
    dint("sc_gate", [NT, D])
    dint("sc_O1", [cfg.NTILE, 128, HP * C])
    dint("sc_Qp", [cfg.NTILE, 128, HP * C])
    dint("sc_GH", [cfg.NTILE, 128, H2C(cfg)])
    dint("sc_reads", [128, QCH * NT])
    dint("sc_x1", [NT, D])
    dint("sc_hq", [cfg.NTILE, 128, ICH * 128])
    dint("cc_p20_in", [2, NG * 128]); dint("cc_p20_out", [2 * cfg.N_CORES, NG * 128])
    dint("cc_gh_in", [128, cfg.H * HD]); dint("cc_gh_out", [128 * cfg.N_CORES, cfg.H * HD])
    dint("cc_wg_in", [sh, cfg.I_FFN]); dint("cc_wg_out", [D, cfg.I_FFN])
    dint("cc_wu_in", [sh, cfg.I_FFN]); dint("cc_wu_out", [D, cfg.I_FFN])
    dint("cc_wd_in", [si, D]); dint("cc_wd_out", [cfg.I_FFN, D])

    b.ps = [b.ctx.enter_context(nc.psum_tensor(f"bk{i}", [128, 512], F32))
            for i in range(8)]

    sb = b.sbuf
    sb("consts", [128, ncc])
    for nm, shp in [("ctrl_b", [128, 26]), ("gate_b", [1, D]), ("out_b", [1, D]),
                    ("convw", [128, 96]), ("vnorm", [128, QCH]), ("mnorm", [128, 1]),
                    ("ramp", [128, QCH]), ("p20_dl", [128, NG]), ("p20_th", [128, NG]),
                    ("premask", [128, cfg.N_CORES])]:
        sb(nm, shp)
    sb("trig", [128, 8 * NG])
    sb("p20sm", [128, 16 * NG])
    sb("rsrow", [1, NT])
    sb("tta", [128, 128]); sb("ttb", [128, 128]); sb("ttc", [128, 128])
    sb("ttd", [128, 128]); sb("tte", [128, 16])
    sb("decb", [64, NT + 3])
    sb("g_all", [16, NT]); sb("beta_all", [48, NT]); sb("b_all", [16, NT])
    sb("colb", [128, 48])
    for sfx in ("", "2"):
        sb("ktm" + sfx, [128, 128]); sb("vtm" + sfx, [128, 128])
        sb("khtm" + sfx, [128, 128]); sb("khfm" + sfx, [128, 128])
        sb("Y0" + sfx, [128, 128]); sb("Y1" + sfx, [128, 128])
        sb("SqA" + sfx, [128, 128]); sb("SqB" + sfx, [128, 128])
        sb("OqA" + sfx, [128, 128]); sb("OqB" + sfx, [128, 128])
        sb("MQ" + sfx, [128, 128])
    sb("O1st", [128, HP * C]); sb("Qpst", [128, HP * C]); sb("GHst", [128, cfg.H * HD])
    sb("Gs1", [128, HP * HD]); sb("Gs0", [128, HP * HD])
    sb("Hs", [128, HP * HD]); sb("cmpa", [128, HP * HD]); sb("cmpb", [128, HP * HD])
    sb("S0", [128, HP * HD]); sb("ghin", [128, cfg.H * HD])
    sb("sqb", [128, 1024])
    sb("bpt", [128, 48])
    sb("tails", [128, 72])
    sb("lnst", [128, 8])
    sb("outt", [128, D])
    sb("xcb", [128, 2 * QCH * 259])
    sb("scl", [128, 3 * 176])

    b.A1 = Arena(b, "A1", 8256)
    b.A2 = Arena(b, "A2", 6144)
    b.A3 = Arena(b, "A3", 6144)
    b.sb["subr"] = b.A3.view(512, NG * 128)[0]
    b.sb["subi"] = b.A3.view(512 + NG * 128, NG * 128)[0]
    b.sb["rp"] = b.A3.view(512 + 2 * NG * 128, NG * 128)[0]

    emit(b, cfg)

    per_eng = b.sch.finalize()
    sems = {e: b.ctx.enter_context(nc.semaphore(f"sem_{e}")) for e in ENGINES}
    for e in ("sync", "gpsimd", "scalar", "vector", "tensor"):
        for ln in range(Sched.N_LANES):
            sems[(e, ln)] = b.ctx.enter_context(nc.semaphore(f"sem_{e}_{ln}"))
    for i in range(b.sch.cc_idx):
        sems["cc%d" % i] = b.ctx.enter_context(nc.semaphore(f"sem_cc{i}"))
    with b.ctx:
        with nc.Block() as block:
            for ename in ENGINES:
                ops = per_eng[ename]

                def section(eng, ops=ops, ename=ename):
                    wm = {}

                    def merge(g):
                        for s, v in g.items():
                            if wm.get(s, 0) < v:
                                wm[s] = v

                    for op in ops:
                        pend = []
                        for d in op.deps:
                            if wm.get(d.sem, 0) >= d.cum_after:
                                continue
                            pend.append((d.sem, d.cum_after))
                            merge(d.gself)
                        if op.lane_wait is not None and op.lane_wait > 0 \
                                and wm.get(op.sem, 0) < op.lane_wait:
                            pend.append((op.sem, op.lane_wait))
                            merge(op.lwg)
                            if wm.get(op.sem, 0) < op.lane_wait:
                                wm[op.sem] = op.lane_wait
                        for sm, vv in pend[:-1]:
                            eng.wait_ge(sems[sm], vv)
                        ret = op.fn(eng)
                        first, ins = ret if isinstance(ret, tuple) \
                            else (ret, ret)
                        if ins is None and (pend or op.need_inc
                                            or op.lane_wait is not None):
                            first = ins = eng.nop()
                        if pend:
                            first.wait_op(sems[pend[-1][0]], pend[-1][1],
                                          "sem-ge")
                        if op.need_inc or op.lane_wait is not None:
                            ins.then_inc(sems[op.sem], op.inc)

                getattr(block, ename)(section)
    return nc


def H2C(cfg):
    return cfg.H * cfg.HD



def emit(b, cfg):
    ph_loads(b, cfg)
    ph_p20_setup(b, cfg)
    ph_wquant(b, cfg)
    ph_p20(b, cfg)
    ph_attn(b, cfg)
    ph_gh_ag(b, cfg)
    ph_pass2(b, cfg)
    ph_outproj(b, cfg)
    ph_ffn(b, cfg)
    b.op("gpsimd", lambda e: None, r=["y_out"], tag="fence")


def ph_loads(b, cfg):
    for nm in ("consts", "ctrl_b", "gate_b", "out_b", "convw", "vnorm", "mnorm",
               "ramp", "p20_dl", "p20_th", "premask"):
        b.dma("sync", b.sb[nm][:, :] if b.sb[nm].shape[0] == 128 else b.sb[nm][:, :],
              b.dr[nm], w=[nm], tag=f"ld_{nm}")
    b.act(b.sb["ramp"][:, :], b.sb["ramp"][:, :], AF.Sigmoid, r=["ramp"], w=["ramp"])


def ph_p20_setup(b, cfg):
    NG, NT = cfg.NG, cfg.NT
    sm, tg = b.sb["p20sm"], b.sb["trig"]

    def smc(g, j):
        return sm[:, g * 16 + j: g * 16 + j + 1]

    def tgc(g, j):
        return tg[:, 8 * g + j: 8 * g + j + 1]

    dl, th = b.sb["p20_dl"], b.sb["p20_th"]
    tte = b.sb["tte"]
    for g in range(NG):
        b.act(smc(g, 13), dl[:, g:g + 1], AF.Exp, r=["p20_dl"], w=["p20sm"])
        b.act(smc(g, 0), smc(g, 13), AF.Exp, scale=-1.0, r=["p20sm"], w=["p20sm"])
        b.act(tgc(g, 6), th[:, g:g + 1], AF.Sin,
              bias=b.const("neg_half_pi")[:, 0:1], scale=0.5,
              r=["p20_th", "consts"], w=["trig"])
        b.act(tgc(g, 7), th[:, g:g + 1], AF.Sin, scale=0.5, r=["p20_th"], w=["trig"])
        b.act(tgc(g, 2), tgc(g, 6), AF.Square, r=["trig"], w=["trig"])
        b.ts(tgc(g, 0), tgc(g, 2), 2.0, ALU.mult, -1.0, ALU.add,
             r=["trig"], w=["trig"])
        b.tt(tgc(g, 1), tgc(g, 6), tgc(g, 7), ALU.mult, r=["trig"], w=["trig"])
        b.ts(tgc(g, 1), tgc(g, 1), -2.0, ALU.mult, r=["trig"], w=["trig"])
        b.copy(tgc(g, 2), tgc(g, 0), r=["trig"], w=["trig"], eng="vector")
        b.ts(tgc(g, 3), tgc(g, 1), -1.0, ALU.mult, r=["trig"], w=["trig"])
    subr, subi, rp = b.sb["subr"], b.sb["subi"], b.sb["rp"]
    for g in range(NG):
        b.memset(subr[:, g * 128: g * 128 + 1], 1.0, w=["subr"])
        b.memset(subi[:, g * 128: g * 128 + 1], 0.0, w=["subi"])
        b.memset(rp[:, g * 128: g * 128 + 1], 1.0, w=["rp"])
        b.copy(tgc(g, 4), tgc(g, 2), r=["trig"], w=["trig"], eng="vector")
        b.copy(tgc(g, 5), tgc(g, 3), r=["trig"], w=["trig"], eng="vector")
        b.copy(smc(g, 14), smc(g, 0), r=["p20sm"], w=["p20sm"], eng="vector")
        total = 1
        while total < NT:
            k = total
            fr, fi = tgc(g, 4), tgc(g, 5)
            if k < 128:
                sl = slice(g * 128, g * 128 + k)
                dl_ = slice(g * 128 + k, g * 128 + 2 * k)
                b.ts(tgc(g, 6), fi, -1.0, ALU.mult, r=["trig"], w=["trig"])
                t0 = tte[:, 0:1].broadcast_to([128, k]) if False else None
                ta = b.sb["tta"][:, 0:k]
                b.ts(ta, subr[:, sl], fr[:, 0:1], ALU.mult,
                     r=["subr", "trig"], w=["tta"])
                b.stt(subr[:, dl_], subi[:, sl], tgc(g, 6)[:, 0:1], ta,
                      ALU.mult, ALU.add, r=["subi", "trig", "tta"], w=["subr"])
                b.ts(ta, subr[:, sl], fi[:, 0:1], ALU.mult,
                     r=["subr", "trig"], w=["tta"])
                b.stt(subi[:, dl_], subi[:, sl], fr[:, 0:1], ta, ALU.mult, ALU.add,
                      r=["subi", "trig", "tta"], w=["subi"])
                b.ts(ta, rp[:, sl], smc(g, 14)[:, 0:1], ALU.mult,
                     r=["rp", "p20sm"], w=["tta"])
                b.copy(rp[:, dl_], ta, r=["tta"], w=["rp"], eng="vector")
            if total == 128:
                b.copy(smc(g, 2), tgc(g, 4), r=["trig"], w=["p20sm"], eng="vector")
                b.copy(smc(g, 3), tgc(g, 5), r=["trig"], w=["p20sm"], eng="vector")
                b.copy(smc(g, 1), smc(g, 14), r=["p20sm"], w=["p20sm"], eng="vector")
            b.tt(tgc(g, 7), fr, fi, ALU.mult, r=["trig"], w=["trig"])
            b.tt(tte[:, 0:1], fi, fi, ALU.mult, r=["trig"], w=["tte"])
            b.tt(fr, fr, fr, ALU.mult, r=["trig"], w=["trig"])
            b.tt(fr, fr, tte[:, 0:1], ALU.subtract, r=["trig", "tte"], w=["trig"])
            b.ts(fi, tgc(g, 7), 2.0, ALU.mult, r=["trig"], w=["trig"])
            b.tt(smc(g, 14), smc(g, 14), smc(g, 14), ALU.mult, r=["p20sm"],
                 w=["p20sm"])
            total *= 2
        b.copy(smc(g, 5), tgc(g, 4), r=["trig"], w=["p20sm"], eng="vector")
        b.ts(smc(g, 6), tgc(g, 5), -1.0, ALU.mult, r=["trig"], w=["p20sm"])
        b.copy(smc(g, 4), smc(g, 14), r=["p20sm"], w=["p20sm"], eng="vector")


def ph_wquant(b, cfg):
    D = cfg.D
    sh = D // cfg.N_CORES
    si = cfg.I_FFN // cfg.N_CORES
    wshv, wshk = b.A2.view(0, 512)
    lanes = [b.A2.view(512 + i * 512, 512) for i in range(4)]
    scl = b.sb["scl"]
    jobs = [("wg_shard", "cc_wg_in", cfg.I_FFN, sh),
            ("wu_shard", "cc_wu_in", cfg.I_FFN, sh),
            ("wd_shard", "cc_wd_in", D, si)]
    for wi, (src, dst, rows, colsn) in enumerate(jobs):
        nrt = (rows + 127) // 128
        for rt in range(nrt):
            r0 = rt * 128
            pr = min(128, rows - r0)
            seg = wshv[:pr, :colsn]
            b.dma("sync", seg, b.dr[src][r0:r0 + pr, :], r=wshk, w=wshk,
                  tag=f"wql{wi}_{rt}")
            nblk = colsn // 16
            sc = scl[:pr, :nblk]
            b.red(sc, seg.rearrange("p (g k) -> p g k", k=16), ALU.max,
                  absval=True, r=wshk, w=["scl"])
            b.ts(sc, sc, 1.0 / 6.0, ALU.mult, 1e-12, ALU.max, r=["scl"], w=["scl"])
            rsc = scl[:pr, 176:176 + nblk]
            b.act(rsc, sc, AF.Ln, r=["scl"], w=["scl"])
            b.act(rsc, rsc, AF.Exp, scale=-1.0, r=["scl"], w=["scl"])
            hsc = scl[:pr, 352:352 + nblk]
            b.ts(hsc, sc, 0.5, ALU.mult, r=["scl"], w=["scl"])
            rexp = rsc.unsqueeze(2).broadcast_to([pr, nblk, 16])
            hexp = hsc.unsqueeze(2).broadcast_to([pr, nblk, 16])
            v3 = seg.rearrange("p (g k) -> p g k", k=16)
            lv = [ln[0][:pr, :colsn].rearrange("p (g k) -> p g k", k=16)
                  for ln in lanes]
            lk = [ln[1][0] for ln in lanes]
            b.quant(v3, v3, rexp, hexp, lv, lk, wshk + ["scl"], wshk)
            nct = (colsn + 127) // 128
            for ct in range(nct):
                c0 = ct * 128
                pc = min(128, colsn - c0)
                bank = b.ps[ct % 2]
                bkk = f"bk{ct % 2}"
                b.tp(bank[:pc, :pr], seg[:, c0:c0 + pc], r=wshk, w=[bkk])
                b.copy(b.sb["tta"][:pc, :pr], bank[:pc, :pr], r=[bkk], w=["tta"],
                       eng="scalar")
                b.dma("gpsimd", b.dr[dst][c0:c0 + pc, r0:r0 + pr],
                      b.sb["tta"][:pc, :pr], r=["tta"], w=[dst], tag="wqsp")
                b.sch.last_writer["tta"] = b.sch.ops[-1]
    b.allgather(b.dr["cc_wg_in"], b.dr["cc_wg_out"], r=["cc_wg_in"],
                w=["cc_wg_out"], tag="ag_wg")
    b.allgather(b.dr["cc_wu_in"], b.dr["cc_wu_out"], r=["cc_wu_in"],
                w=["cc_wu_out"], tag="ag_wu")
    b.allgather(b.dr["cc_wd_in"], b.dr["cc_wd_out"], r=["cc_wd_in"],
                w=["cc_wd_out"], tag="ag_wd")


def ph_p20(b, cfg):
    NT, NG = cfg.NT, cfg.NG
    sm, tg = b.sb["p20sm"], b.sb["trig"]
    subr, subi, rp = b.sb["subr"], b.sb["subi"], b.sb["rp"]

    def smc(g, j):
        return sm[:, g * 16 + j: g * 16 + j + 1]

    def tgc(g, j):
        return tg[:, 8 * g + j: 8 * g + j + 1]

    ntile = NT // 128
    urv, urk = b.A1.view(0, NT)
    uiv, uik = b.A1.view(NT, NT)
    wrv, wrk = b.A1.view(2 * NT, NT)
    wiv, wik = b.A1.view(3 * NT, NT)
    tta, ttb = b.sb["tta"], b.sb["ttb"]
    ttc, ttd = b.sb["ttc"], b.sb["ttd"]

    for g in range(NG):
        for t in range(ntile):
            xv, xk = b.A3.view(2048 + (t % 8) * 256, 256)
            b.dma(("sync", "gpsimd", "scalar")[t % 3], xv,
                  b.dr["x_tm"][t * 128:(t + 1) * 128, 256 * g:256 * (g + 1)],
                  r=xk, w=xk, tag=f"p20x{g}_{t}")
            for comp, dst, dk in ((0, urv, urk), (1, uiv, uik)):
                bank = b.ps[comp]
                bkk = f"bk{comp}"
                b.tp(bank[:, 0:128], xv[:, comp:256:2], r=xk, w=[bkk])
                b.copy(dst[:, t * 128:(t + 1) * 128], bank[:, 0:128],
                       r=[bkk], w=dk, eng="scalar")
        bpt = b.sb["bpt"]
        bptr, bpti = bpt[:, 0:16], bpt[:, 16:32]
        _cpow_table(b, g, smc, bptr, bpti, ntile)
        ssl = slice(g * 128, (g + 1) * 128)
        ur3 = urv[:, 0:NT].rearrange("q (t c) -> q t c", c=128)
        ui3 = uiv[:, 0:NT].rearrange("q (t c) -> q t c", c=128)
        wr3 = wrv[:, 0:NT].rearrange("q (t c) -> q t c", c=128)
        wi3 = wiv[:, 0:NT].rearrange("q (t c) -> q t c", c=128)
        srb = subr[:, ssl].unsqueeze(1).broadcast_to([128, ntile, 128])
        sib = subi[:, ssl].unsqueeze(1).broadcast_to([128, ntile, 128])
        b7 = bptr[:, 0:ntile].unsqueeze(2).broadcast_to([128, ntile, 128])
        b8 = bpti[:, 0:ntile].unsqueeze(2).broadcast_to([128, ntile, 128])
        b.tt(wr3, ur3, srb, ALU.mult, r=urk + ["subr"], w=wrk)
        b.tt(wi3, ui3, sib, ALU.mult, r=uik + ["subi"], w=wik)
        b.tt(wr3, wr3, wi3, ALU.subtract, r=wrk + wik, w=wrk)
        b.tt(wi3, ui3, srb, ALU.mult, r=uik + ["subr"], w=wik)
        b.tt(ui3, ur3, sib, ALU.mult, r=urk + ["subi"], w=uik)
        b.tt(ui3, ui3, wi3, ALU.add, r=uik + wik, w=uik)
        b.tt(ur3, ui3, b8, ALU.mult, r=uik + ["bpt"], w=urk)
        b.tt(wi3, ui3, b7, ALU.mult, r=uik + ["bpt"], w=wik)
        b.tt(ui3, wr3, b8, ALU.mult, r=wrk + ["bpt"], w=uik)
        b.tt(wr3, wr3, b7, ALU.mult, r=wrk + ["bpt"], w=wrk)
        b.tt(wr3, wr3, ur3, ALU.subtract, r=wrk + urk, w=wrk)
        b.tt(wi3, wi3, ui3, ALU.add, r=wik + uik, w=wik)
        rho_b = smc(g, 0).broadcast_to([128, NT])
        b.scan(urv[:, 0:NT], rho_b, wrv[:, 0:NT], 0.0, r=wrk + ["p20sm"], w=urk)
        b.scan(uiv[:, 0:NT], rho_b, wiv[:, 0:NT], 0.0, r=wik + ["p20sm"], w=uik)
        cosc, sinc = tgc(g, 0), tgc(g, 1)
        tte = b.sb["tte"]
        b.tt(tte[:, 0:1], smc(g, 7), cosc, ALU.mult, r=["p20sm", "trig"], w=["tte"])
        b.tt(tte[:, 1:2], smc(g, 8), sinc, ALU.mult, r=["p20sm", "trig"], w=["tte"])
        b.tt(tte[:, 2:3], smc(g, 7), sinc, ALU.mult, r=["p20sm", "trig"], w=["tte"])
        b.tt(tte[:, 3:4], smc(g, 8), cosc, ALU.mult, r=["p20sm", "trig"], w=["tte"])
        b.tt(tte[:, 4:5], tte[:, 0:1], tte[:, 1:2], ALU.subtract, r=["tte"], w=["tte"])
        b.tt(tte[:, 5:6], tte[:, 2:3], tte[:, 3:4], ALU.add, r=["tte"], w=["tte"])
        b.ts(tte[:, 5:6], tte[:, 5:6], -1.0, ALU.mult, r=["tte"], w=["tte"])
        ver = urv[:, NT - 1:NT]
        vei = uiv[:, NT - 1:NT]
        b.tt(tte[:, 0:1], tte[:, 4:5], ver, ALU.mult, r=["tte"] + urk, w=["tte"])
        b.tt(tte[:, 1:2], tte[:, 5:6], vei, ALU.mult, r=["tte"] + uik, w=["tte"])
        b.tt(tte[:, 2:3], tte[:, 4:5], vei, ALU.mult, r=["tte"] + uik, w=["tte"])
        b.tt(tte[:, 3:4], tte[:, 5:6], ver, ALU.mult, r=["tte"] + urk, w=["tte"])
        b.tt(smc(g, 11), tte[:, 0:1], tte[:, 1:2], ALU.subtract, r=["tte"],
             w=["p20sm"])
        b.tt(smc(g, 12), tte[:, 2:3], tte[:, 3:4], ALU.add, r=["tte"], w=["p20sm"])
        b.dma("gpsimd", b.dr["sc_p20"][:, g * NT:(g + 1) * NT],
              urv[:, 0:NT], r=urk, w=["sc_p20"], tag=f"p20vr{g}")
        b.dma("scalar", b.dr["sc_p20"][:, (NG + g) * NT:(NG + g + 1) * NT],
              uiv[:, 0:NT], r=uik, w=["sc_p20"], tag=f"p20vi{g}")
        b.dma("gpsimd", b.dr["cc_p20_in"][0:1, g * 128:(g + 1) * 128],
              smc(g, 11), r=["p20sm"], w=["cc_p20_in"], tag="p20st")
        b.dma("gpsimd", b.dr["cc_p20_in"][1:2, g * 128:(g + 1) * 128],
              smc(g, 12), r=["p20sm"], w=["cc_p20_in"], tag="p20st")
    b.allgather(b.dr["cc_p20_in"], b.dr["cc_p20_out"], r=["cc_p20_in"],
                w=["cc_p20_out"], tag="ag_p20")
    agv, agk = b.A1.view(4 * NT, 64)
    agst = b.sb["sqb"][0:16, 0:NG * 128]
    b.dma("sync", agst, b.dr["cc_p20_out"], r=["sqb", "cc_p20_out"], w=["sqb"], tag="p20agl")
    for g in range(NG):
        bank = b.ps[g % 2]
        b.tp(bank[:, 0:16], agst[:, g * 128:(g + 1) * 128], r=["sqb"],
             w=[f"bk{g % 2}"])
        b.copy(agv[:, g * 16:(g + 1) * 16], bank[:, 0:16], r=[f"bk{g % 2}"],
               w=agk, eng="scalar")

    def agc(a, comp, g):
        j = g * 16 + a * 2 + comp
        return agv[:, j:j + 1]

    pm = b.sb["premask"]
    tte = b.sb["tte"]
    ta = b.sb["tta"]
    e_ = 16 * (NG - 1) + 1

    def smv(j):
        return sm[:, j:j + e_:16]

    cr, ci_ = smv(11), smv(12)
    b.memset(cr, 0.0, w=["p20sm"])
    b.memset(ci_, 0.0, w=["p20sm"])
    for a in range(cfg.N_CORES):
        mc = pm[:, a:a + 1]
        agr = agv[:, a * 2:a * 2 + 16 * (NG - 1) + 1:16]
        agi = agv[:, a * 2 + 1:a * 2 + 1 + 16 * (NG - 1) + 1:16]
        b.tt(tte[:, 0:NG], smv(4), smv(5), ALU.mult, r=["p20sm"], w=["tte"])
        b.tt(tte[:, 4:4 + NG], smv(4), smv(6), ALU.mult, r=["p20sm"],
             w=["tte"])
        b.ts(tte[:, 0:NG], tte[:, 0:NG], -1.0, ALU.add, r=["tte"], w=["tte"])
        b.ts(tte[:, 0:NG], tte[:, 0:NG], mc[:, 0:1], ALU.mult,
             r=["tte", "premask"], w=["tte"])
        b.ts(tte[:, 0:NG], tte[:, 0:NG], 1.0, ALU.add, r=["tte"], w=["tte"])
        b.ts(tte[:, 4:4 + NG], tte[:, 4:4 + NG], mc[:, 0:1], ALU.mult,
             r=["tte", "premask"], w=["tte"])
        b.tt(tte[:, 8:8 + NG], cr, tte[:, 0:NG], ALU.mult,
             r=["p20sm", "tte"], w=["tte"])
        b.tt(tte[:, 12:12 + NG], ci_, tte[:, 4:4 + NG], ALU.mult,
             r=["p20sm", "tte"], w=["tte"])
        b.tt(ta[:, 0:NG], cr, tte[:, 4:4 + NG], ALU.mult,
             r=["p20sm", "tte"], w=["tta"])
        b.tt(ta[:, 4:4 + NG], ci_, tte[:, 0:NG], ALU.mult,
             r=["p20sm", "tte"], w=["tta"])
        b.tt(cr, tte[:, 8:8 + NG], tte[:, 12:12 + NG], ALU.subtract,
             r=["tte"], w=["p20sm"])
        b.tt(ci_, ta[:, 0:NG], ta[:, 4:4 + NG], ALU.add, r=["tta"],
             w=["p20sm"])
        b.stt(cr, agr, mc[:, 0:1], cr, ALU.mult, ALU.add,
              r=agk + ["premask", "p20sm"], w=["p20sm"])
        b.stt(ci_, agi, mc[:, 0:1], ci_, ALU.mult, ALU.add,
              r=agk + ["premask", "p20sm"], w=["p20sm"])
    e8 = 8 * (NG - 1) + 1
    cosc, sinc = tg[:, 0:e8:8], tg[:, 1:1 + e8:8]
    b.tt(tte[:, 0:NG], cr, cosc, ALU.mult, r=["p20sm", "trig"], w=["tte"])
    b.tt(tte[:, 4:4 + NG], ci_, sinc, ALU.mult, r=["p20sm", "trig"],
         w=["tte"])
    b.tt(tte[:, 8:8 + NG], cr, sinc, ALU.mult, r=["p20sm", "trig"],
         w=["tte"])
    b.tt(tte[:, 12:12 + NG], ci_, cosc, ALU.mult, r=["p20sm", "trig"],
         w=["tte"])
    b.tt(smv(9), tte[:, 0:NG], tte[:, 4:4 + NG], ALU.subtract, r=["tte"],
         w=["p20sm"])
    b.tt(smv(10), tte[:, 8:8 + NG], tte[:, 12:12 + NG], ALU.add, r=["tte"],
         w=["p20sm"])
    b.tt(smv(9), smv(9), smv(0), ALU.mult, r=["p20sm"], w=["p20sm"])
    b.tt(smv(10), smv(10), smv(0), ALU.mult, r=["p20sm"], w=["p20sm"])
    for g in range(NG):
        b.dma("sync", urv[:, 0:NT], b.dr["sc_p20"][:, g * NT:(g + 1) * NT],
              r=["sc_p20"], w=urk, tag=f"p20rr{g}")
        b.dma("sync", uiv[:, 0:NT],
              b.dr["sc_p20"][:, (NG + g) * NT:(NG + g + 1) * NT],
              r=["sc_p20"], w=uik, tag=f"p20ri{g}")
        bpt = b.sb["bpt"]
        bptr, bpti, rt = bpt[:, 0:16], bpt[:, 16:32], bpt[:, 32:48]
        _rpow_table(b, g, smc, rt, ntile)
        ta2, tb2 = b.sb["tta"], b.sb["ttb"]
        b.ts(ta2[:, 0:ntile], rt[:, 0:ntile], smc(g, 9)[:, 0:1], ALU.mult,
             r=["bpt", "p20sm"], w=["tta"])
        b.ts(tb2[:, 0:ntile], rt[:, 0:ntile], smc(g, 10)[:, 0:1], ALU.mult,
             r=["bpt", "p20sm"], w=["ttb"])
        ssl = slice(g * 128, (g + 1) * 128)
        ur3 = urv[:, 0:NT].rearrange("q (t c) -> q t c", c=128)
        ui3 = uiv[:, 0:NT].rearrange("q (t c) -> q t c", c=128)
        wr3 = wrv[:, 0:NT].rearrange("q (t c) -> q t c", c=128)
        wi3 = wiv[:, 0:NT].rearrange("q (t c) -> q t c", c=128)
        rpb = rp[:, ssl].unsqueeze(1).broadcast_to([128, ntile, 128])
        zzr = ta2[:, 0:ntile].unsqueeze(2).broadcast_to([128, ntile, 128])
        zzi = tb2[:, 0:ntile].unsqueeze(2).broadcast_to([128, ntile, 128])
        b.tt(wr3, rpb, zzr, ALU.mult, r=["rp", "tta"], w=wrk)
        b.tt(ur3, ur3, wr3, ALU.add, r=urk + wrk, w=urk)
        b.tt(wi3, rpb, zzi, ALU.mult, r=["rp", "ttb"], w=wik)
        b.tt(ui3, ui3, wi3, ALU.add, r=uik + wik, w=uik)
        _cpow_table(b, g, smc, bptr, bpti, ntile)
        srb = subr[:, ssl].unsqueeze(1).broadcast_to([128, ntile, 128])
        sib = subi[:, ssl].unsqueeze(1).broadcast_to([128, ntile, 128])
        b7 = bptr[:, 0:ntile].unsqueeze(2).broadcast_to([128, ntile, 128])
        b8 = bpti[:, 0:ntile].unsqueeze(2).broadcast_to([128, ntile, 128])
        b.tt(wr3, ur3, srb, ALU.mult, r=urk + ["subr"], w=wrk)
        b.tt(wi3, ui3, sib, ALU.mult, r=uik + ["subi"], w=wik)
        b.tt(wr3, wr3, wi3, ALU.add, r=wrk + wik, w=wrk)
        b.tt(wi3, ui3, srb, ALU.mult, r=uik + ["subr"], w=wik)
        b.tt(ui3, ur3, sib, ALU.mult, r=urk + ["subi"], w=uik)
        b.tt(wi3, wi3, ui3, ALU.subtract, r=wik + uik, w=wik)
        b.tt(ur3, wi3, b8, ALU.mult, r=wik + ["bpt"], w=urk)
        b.tt(ui3, wr3, b8, ALU.mult, r=wrk + ["bpt"], w=uik)
        b.tt(wr3, wr3, b7, ALU.mult, r=wrk + ["bpt"], w=wrk)
        b.tt(wi3, wi3, b7, ALU.mult, r=wik + ["bpt"], w=wik)
        b.tt(ur3, wr3, ur3, ALU.add, r=urk + wrk, w=urk)
        b.tt(ui3, wi3, ui3, ALU.subtract, r=uik + wik, w=uik)
        sq = b.sb["sqb"]
        nss = max(1, NT // 512)
        ssz = min(512, NT)
        for comp, buf, kk in ((0, urv, urk), (1, uiv, uik)):
            for s in range(nss):
                b.act(sq[:, 0:ssz], buf[:, s * ssz:(s + 1) * ssz], AF.Square,
                      r=kk, w=["sqb"])
                bank = b.ps[4 + s % 4]
                b.mm(bank[0:1, 0:ssz], b.const("ones_blk")[:, 0:1],
                     sq[:, 0:ssz],
                     start=(g == 0 and comp == 0),
                     stop=(g == NG - 1 and comp == 1),
                     r=["sqb", "consts"], w=[f"bk{4 + s % 4}"], tag="p20ssq")
        b.dma("gpsimd", b.dr["sc_p20"][:, g * NT:(g + 1) * NT],
              urv[:, 0:NT], r=urk, w=["sc_p20"], tag=f"p20yr{g}")
        b.dma("scalar", b.dr["sc_p20"][:, (NG + g) * NT:(NG + g + 1) * NT],
              uiv[:, 0:NT], r=uik, w=["sc_p20"], tag=f"p20yi{g}")
    rs = b.sb["rsrow"]
    nss = max(1, NT // 512)
    ssz = min(512, NT)
    for s in range(nss):
        bank = b.ps[4 + s % 4]
        bkk = f"bk{4 + s % 4}"
        b.act(rs[0:1, s * ssz:(s + 1) * ssz], bank[0:1, 0:ssz], AF.Ln,
              bias=b.const("rms_eps")[0:1, 0:1], scale=1.0 / cfg.D,
              r=[bkk, "consts"], w=["rsrow"])
    b.act(rs[0:1, 0:NT], rs[0:1, 0:NT], AF.Exp, scale=-0.5, r=["rsrow"],
          w=["rsrow"])


def _cpow_table(b, g, smc, tr, ti, n):
    tte = b.sb["tte"]
    ta, tb = b.sb["tta"], b.sb["ttb"]
    b.memset(tr[:, 0:1], 1.0, w=["bpt"])
    b.memset(ti[:, 0:1], 0.0, w=["bpt"])
    b.copy(tte[:, 8:9], smc(g, 2), r=["p20sm"], w=["tte"], eng="vector")
    b.copy(tte[:, 9:10], smc(g, 3), r=["p20sm"], w=["tte"], eng="vector")
    s = 1
    while s < n:
        cr, ci_ = tte[:, 8:9][:, 0:1], tte[:, 9:10][:, 0:1]
        b.ts(ta[:, 0:s], tr[:, 0:s], cr, ALU.mult, r=["bpt", "tte"],
             w=["tta"])
        b.ts(tb[:, 0:s], ti[:, 0:s], ci_, ALU.mult, r=["bpt", "tte"],
             w=["ttb"])
        b.tt(tr[:, s:2 * s], ta[:, 0:s], tb[:, 0:s], ALU.subtract,
             r=["tta", "ttb"], w=["bpt"])
        b.ts(ta[:, 0:s], ti[:, 0:s], cr, ALU.mult, r=["bpt", "tte"],
             w=["tta"])
        b.ts(tb[:, 0:s], tr[:, 0:s], ci_, ALU.mult, r=["bpt", "tte"],
             w=["ttb"])
        b.tt(ti[:, s:2 * s], ta[:, 0:s], tb[:, 0:s], ALU.add,
             r=["tta", "ttb"], w=["bpt"])
        b.tt(tte[:, 10:11], cr, cr, ALU.mult, r=["tte"], w=["tte"])
        b.tt(tte[:, 11:12], ci_, ci_, ALU.mult, r=["tte"], w=["tte"])
        b.tt(tte[:, 14:15], cr, ci_, ALU.mult, r=["tte"], w=["tte"])
        b.tt(tte[:, 8:9], tte[:, 10:11], tte[:, 11:12], ALU.subtract,
             r=["tte"], w=["tte"])
        b.tt(tte[:, 9:10], tte[:, 14:15], tte[:, 14:15], ALU.add,
             r=["tte"], w=["tte"])
        s *= 2
    b.copy(smc(g, 7), tte[:, 8:9], r=["tte"], w=["p20sm"], eng="vector")
    b.copy(smc(g, 8), tte[:, 9:10], r=["tte"], w=["p20sm"], eng="vector")


def _rpow_table(b, g, smc, rt, n):
    tte = b.sb["tte"]
    b.memset(rt[:, 0:1], 1.0, w=["bpt"])
    b.copy(tte[:, 8:9], smc(g, 1), r=["p20sm"], w=["tte"], eng="vector")
    s = 1
    while s < n:
        b.ts(rt[:, s:2 * s], rt[:, 0:s], tte[:, 8:9][:, 0:1], ALU.mult,
             r=["bpt", "tte"], w=["bpt"])
        b.tt(tte[:, 8:9], tte[:, 8:9], tte[:, 8:9], ALU.mult, r=["tte"],
             w=["tte"])
        s *= 2



CGW = 2
SEGW = CGW * 128 + 3


def ph_attn(b, cfg):
    b.memset(b.sb["tte"][:, 15:16], 0.0, r=["subr", "subi", "rp"],
             w=["A3.0", "A3.1", "tte"], tag="tbl_fence")
    for cg in range(cfg.NTILE // CGW):
        ph_ctrl_chunk(b, cfg, cg)
        for sub in range(CGW):
            ph_delta1(b, cfg, cg * CGW + sub)


def _ci(b, m):
    v, k = b.A1.view(m * SEGW, SEGW)
    return v, k


def ph_ctrl_chunk(b, cfg, cg):
    NT, D, C, QCH, NG = cfg.NT, cfg.D, cfg.C, cfg.QCH, cfg.NG
    W = CGW * 128
    t0 = cg * W
    xcb = b.sb["xcb"]
    par = cg % 2
    xck = f"xcb{par}"
    segw = QCH * (W + 3)
    xc = xcb[:, par * segw:(par + 1) * segw]
    CW = W + 3

    for tt_ in range(CGW):
        xt, xtk = b.A2.view(3072 + (tt_ % 2) * 1024, 1024)
        b.dma("sync", xt[:, 0:D],
              b.dr["x_tm"][t0 + tt_ * 128: t0 + (tt_ + 1) * 128, :], r=xtk,
              w=xtk, tag=f"xld{cg}_{tt_}")
        for f in range(QCH):
            bank = b.ps[(tt_ + f) % 2]
            bkk = f"bk{(tt_ + f) % 2}"
            b.tp(bank[:, 0:128], xt[:, f * 128:(f + 1) * 128], r=xtk, w=[bkk])
            b.copy(xc[:, f * CW + 3 + tt_ * 128: f * CW + 3 + (tt_ + 1) * 128],
                   bank[:, 0:128], r=[bkk], w=[xck], eng="scalar")
    xh, xhk = b.A2.view(5120, 1024)
    if cg == 0:
        b.dma("sync", xh[:3, 0:D], b.dr["x_halo"], r=xhk, w=xhk, tag="xhalo")
    else:
        b.dma("sync", xh[:3, 0:D], b.dr["x_tm"][t0 - 3:t0, :], r=xhk, w=xhk,
              tag=f"xhalo{cg}")
    for f in range(QCH):
        bank = b.ps[f % 2]
        bkk = f"bk{f % 2}"
        b.tp(bank[:, 0:3], xh[:3, f * 128:(f + 1) * 128], r=xhk, w=[bkk])
        b.copy(xc[:, f * CW: f * CW + 3], bank[:, 0:3], r=[bkk], w=[xck],
               eng="scalar")

    n = W + 3
    tails = b.sb["tails"]
    for m in range(26):
        if m < 24:
            mc, w0, wn, pbase = 128, 128 * m, 128, 0
        elif m == 24:
            mc, w0, wn, pbase = 16, 3072, 16, 0
        else:
            mc, w0, wn, pbase = 16, 3088, 16, 32
        wv, wk0 = b.A3.view((m % 2) * 1024, 1024)
        b.dma("sync", wv[:, 0:QCH * mc],
              b.dr["ctrl_W"][:, w0:w0 + wn]
              .rearrange("(k p) n -> p k n", p=128),
              r=wk0, w=wk0, tag=f"cw{cg}_{m}")
        bank = b.ps[2 + m % 4]
        bkk = f"bk{2 + m % 4}"
        items = []
        for kc in range(QCH):
            items.append((bank[pbase:pbase + mc, :n], wv[:, kc * mc:(kc + 1) * mc],
                          xc[:, kc * CW: kc * CW + n], kc == 0, kc == QCH - 1,
                          None if pbase == 0 else (0, pbase)))
        b.mmg(items, r=wk0 + [xck], w=[bkk], tag=f"cmm{cg}_{m}")
        if m < 24:
            seg, segk = _ci(b, m)
            if cg > 0:
                b.copy(seg[:, 0:3], tails[:, m * 3:(m + 1) * 3],
                       r=["tails"], w=segk, eng="vector")
            b.act(seg[:, 3:3 + W] if cg > 0 else seg[:, 0:n],
                  bank[:, 3:3 + W] if cg > 0 else bank[:, 0:n], AF.Identity,
                  bias=b.sb["ctrl_b"][:, m:m + 1], r=[bkk, "ctrl_b"], w=segk)
            b.copy(tails[:, m * 3:(m + 1) * 3], seg[:, W:W + 3], r=segk,
                   w=["tails"], eng="vector")
        else:
            b.act(b.sb["decb"][pbase:pbase + 16, t0:t0 + W],
                  bank[pbase:pbase + 16, 3:3 + W], AF.Identity,
                  bias=b.sb["ctrl_b"][pbase:pbase + 16, m:m + 1],
                  r=[bkk, "ctrl_b"], w=["decb"])
    dsl = slice(t0, t0 + W)
    gt = b.sb["sqb"]
    b.act(gt[:16, 0:W], b.sb["decb"][0:16, dsl], AF.Exp, scale=-1.0,
          r=["decb"], w=["sqb"])
    b.act(b.sb["g_all"][:16, dsl], gt[:16, 0:W], AF.Ln, bias=1.0, r=["sqb"],
          w=["g_all"])
    b.ts(b.sb["g_all"][:16, dsl], b.sb["g_all"][:16, dsl], -1.0, ALU.mult,
         r=["g_all"], w=["g_all"])
    b.act(b.sb["beta_all"][32:48, dsl], b.sb["decb"][32:48, dsl], AF.Sigmoid,
          r=["decb"], w=["beta_all"])

    gout = b.sb["outt"]
    for q4 in range(4):
        gwv, gwk = b.A3.view(2048 + (q4 % 2) * 2048, 2048)
        b.dma("sync", gwv[:, 0:QCH * 256],
              b.dr["ctrl_W"][:, 3104 + q4 * 256:3104 + (q4 + 1) * 256]
              .rearrange("(k p) n -> p k n", p=128),
              r=gwk, w=gwk, tag=f"gw{cg}_{q4}")
        for tt_ in range(CGW):
            bank = b.ps[4 + tt_ % 2]
            bkk = f"bk{4 + tt_ % 2}"
            items = [(bank[:, q4 % 2 * 256:(q4 % 2) * 256 + 256],
                      xc[:, kc * CW + 3 + tt_ * 128: kc * CW + 3 + (tt_ + 1) * 128],
                      gwv[:, kc * 256:(kc + 1) * 256], kc == 0, False, None)
                     for kc in range(QCH)]
            items.append((bank[:, q4 % 2 * 256:(q4 % 2) * 256 + 256],
                          b.const("ones_blk")[0:1, :],
                          b.sb["gate_b"][0:1, q4 * 256:(q4 + 1) * 256],
                          False, True, None))
            b.mmg(items, r=gwk + [xck, "gate_b", "consts"], w=[bkk],
                  tag=f"gmm{cg}_{q4}_{tt_}")
            b.copy(gout[:, (tt_ % 2) * 512 + (q4 % 2) * 256:
                        (tt_ % 2) * 512 + (q4 % 2) * 256 + 256],
                   bank[:, q4 % 2 * 256:(q4 % 2) * 256 + 256],
                   r=[bkk], w=["outt"], eng="scalar")
            b.dma("gpsimd",
                  b.dr["sc_gate"][t0 + tt_ * 128: t0 + (tt_ + 1) * 128,
                                  q4 * 256:(q4 + 1) * 256],
                  gout[:, (tt_ % 2) * 512 + (q4 % 2) * 256:
                       (tt_ % 2) * 512 + (q4 % 2) * 256 + 256],
                  r=["outt"], w=["sc_gate"], tag=f"gsp{cg}_{q4}_{tt_}")

    npv, npk = b.A2.view(0, QCH * 256)
    b.dma("sync", npv,
          b.dr["sc_p20"].rearrange("p (k t) -> p k t", t=NT)[:, :, t0:t0 + W],
          r=["sc_p20"], w=npk, tag=f"npl{cg}")
    bank_rs = b.ps[6]
    b.mm(bank_rs[:, 0:W], b.const("ones_blk")[0:1, :],
         b.sb["rsrow"][0:1, t0:t0 + W], r=["rsrow", "consts"],
         w=["bk6"], tag="rse")
    for f in range(QCH):
        vn = b.sb["vnorm"][:, f:f + 1]
        b.stt(npv[:, f * W:(f + 1) * W], npv[:, f * W:(f + 1) * W], vn[:, 0:1],
              bank_rs[:, 0:W], ALU.mult, ALU.mult,
              r=npk + ["vnorm", "bk6"], w=npk)
    convw, ramp = b.sb["convw"], b.sb["ramp"]
    for m in range(24):
        wpv, wpk = b.A3.view((m % 2) * 1024, 1024)
        b.dma("sync", wpv[:, 0:QCH * 128],
              b.dr["p20c_W"][:, 128 * m:128 * (m + 1)]
              .rearrange("(k p) n -> p k n", p=128),
              r=wpk, w=wpk, tag=f"pw{cg}_{m}")
        bank = b.ps[2 + m % 4]
        bkk = f"bk{2 + m % 4}"
        items = []
        for kc in range(QCH):
            items.append((bank[:, :W], wpv[:, kc * 128:(kc + 1) * 128],
                          npv[:, kc * W:(kc + 1) * W], kc == 0, kc == QCH - 1,
                          None))
        b.mmg(items, r=wpk + npk, w=[bkk], tag=f"pmm{cg}_{m}")
        seg, segk = _ci(b, m)
        tmp = b.sb["sqb"][:, (m % 2) * 512:(m % 2) * 512 + W]
        b.ts(tmp, seg[:, 0:W], convw[:, 4 * m:4 * m + 1][:, 0:1], ALU.mult,
             r=segk + ["convw"], w=["sqb"])
        for j in range(1, 4):
            b.stt(tmp, seg[:, j:j + W], convw[:, 4 * m + j:4 * m + j + 1][:, 0:1],
                  tmp, ALU.mult, ALU.add, r=segk + ["convw", "sqb"], w=["sqb"])
        b.act(tmp, tmp, AF.Silu, r=["sqb"], w=["sqb"])
        rcol = ramp[:, m % QCH: m % QCH + 1]
        b.stt(seg[:, 3:3 + W], bank[:, :W], rcol[:, 0:1], tmp,
              ALU.mult, ALU.add, r=[bkk, "ramp", "sqb"], w=segk)
    for part in range(2):
        sq = b.sb["sqb"]
        bank = b.ps[5 + part]
        bkk = f"bk{5 + part}"
        for f in range(QCH):
            m = part * QCH + f
            seg, segk = _ci(b, m)
            b.act(sq[:, 0:W], seg[:, 3:3 + W], AF.Square, r=segk, w=["sqb"])
            b.mm(bank[0:2, (f % 2) * W:(f % 2) * W + W],
                 b.const("R2"), sq[:, 0:W], r=["sqb", "consts"], w=[bkk],
                 tag="l2s")
            if f % 2 == 1 or f == QCH - 1:
                n4 = (f % 2) + 1
                lrs = b.sb["sqb"][:2, 512:512 + n4 * W]
                b.act(lrs, bank[0:2, 0:n4 * W], AF.Ln,
                      bias=b.const("l2_eps")[:2, 0:1], r=[bkk, "consts"],
                      w=["sqb"])
                b.act(lrs, lrs, AF.Exp, scale=-0.5, r=["sqb"], w=["sqb"])
                for ff in range(f - n4 + 1, f + 1):
                    mm_ = part * QCH + ff
                    seg2, seg2k = _ci(b, mm_)
                    bank2 = b.ps[7]
                    b.mm(bank2[:, 0:W],
                         b.const("E2")[0:2, :],
                         b.sb["sqb"][0:2, 512 + (ff % 2) * W:512 + (ff % 2) * W + W],
                         r=["sqb", "consts"], w=["bk7"], tag="l2e")
                    b.tt(seg2[:, 3:3 + W], seg2[:, 3:3 + W], bank2[:, 0:W],
                         ALU.mult, r=seg2k + ["bk7"], w=seg2k)


NEUM = 6


def ph_delta1(b, cfg, c):
    C, HP, HD = cfg.C, cfg.HP, cfg.HD
    colb, tte = b.sb["colb"], b.sb["tte"]
    sub = c % CGW
    o0 = 3 + sub * 128
    dsl = slice(c * C, (c + 1) * C)
    gc = b.sb["sqb"][:16, 0:C]
    ones_b = b.const("ones_blk")[:16, 0:1].broadcast_to([16, C])
    b.scan(gc, ones_b, b.sb["g_all"][:16, dsl], 0.0, r=["g_all", "consts"],
           w=["sqb"], tag="gcum")
    bv = b.sb["sqb"][:16, 2 * C:3 * C]
    stk = b.sb["sqb"][:48, 3 * C:4 * C]
    b.act(stk[0:16, :], gc, AF.Exp, r=["sqb"], w=["sqb"])
    b.act(bv, gc, AF.Exp, scale=-1.0, r=["sqb"], w=["sqb"])
    b.copy(b.sb["b_all"][:16, dsl], stk[0:16, :], r=["sqb"], w=["b_all"],
           eng="vector")
    b.dma("gpsimd", stk[16:32, :], bv, r=["sqb"], w=["sqb"], tag="binvmv")
    b.copy(stk[32:48, :], b.sb["beta_all"][32:48, dsl], r=["beta_all"], w=["sqb"],
           eng="vector")
    bank = b.ps[4]
    b.tp(bank[:, 0:48], stk, r=["sqb"], w=["bk4"])
    b.copy(colb[:, 0:48], bank[:, 0:48], r=["bk4"], w=["colb"], eng="scalar")

    GH = b.sb["GHst"]
    for pp in range(HP // 4):
        pinf = []
        for pi in range(4):
            p = 4 * pp + pi
            if pi < 2:
                ps_ = "" if p % 2 == 0 else "2"
                ktm, vtm = b.sb["ktm" + ps_], b.sb["vtm" + ps_]
                ktmk, vtmk = "ktm" + ps_, "vtm" + ps_
                kh, khk = b.sb["khtm" + ps_], "khtm" + ps_
                khfm, khfk = b.sb["khfm" + ps_], "khfm" + ps_
            elif pi == 2:
                ktm, k1_ = b.A3.view(4096, 128)
                vtm = b.A3.view(4224, 128)[0]
                kh = b.A3.view(4352, 128)[0]
                khfm = b.A3.view(4480, 128)[0]
                ktmk = vtmk = khk = khfk = k1_[0]
            else:
                ktm, k1_ = b.A2.view(3072, 128)
                vtm = b.A2.view(3200, 128)[0]
                kh = b.A2.view(3328, 128)[0]
                khfm = b.A2.view(3456, 128)[0]
                ktmk = vtmk = khk = khfk = k1_[0]
            mK, mV, mQ = 8 + p, 16 + p, p
            ksg, ksgk = _ci(b, mK)
            vsg, vsgk = _ci(b, mV)
            qsg, qsgk = _ci(b, mQ)
            bank = b.ps[5]
            b.tp(bank[:, 0:128], ksg[:, o0:o0 + C], r=ksgk, w=["bk5"])
            b.copy(ktm, bank[:, 0:128], r=["bk5"], w=[ktmk], eng="scalar")
            bank = b.ps[6]
            b.tp(bank[:, 0:128], vsg[:, o0:o0 + C], r=vsgk, w=["bk6"])
            b.copy(vtm, bank[:, 0:128], r=["bk6"], w=[vtmk], eng="scalar")
            for hh in range(2):
                h = 2 * p + hh
                b.ts(kh[:, hh * 64:(hh + 1) * 64], ktm[:, hh * 64:(hh + 1) * 64],
                     colb[:, 32 + h:32 + h + 1][:, 0:1], ALU.mult,
                     r=[ktmk, "colb"], w=[khk])
            bank = b.ps[7]
            b.tp(bank[:, 0:128], kh, r=[khk], w=["bk7"])
            b.copy(khfm, bank[:, 0:128], r=["bk7"], w=[khfk], eng="scalar")
            pinf.append(dict(p=p, ksg=ksg, ksgk=ksgk, qsg=qsg, qsgk=qsgk,
                             ktm=ktm, ktmk=ktmk, vtm=vtm, vtmk=vtmk,
                             kh=kh, khk=khk, khfm=khfm, khfk=khfk))
        ap_banks = ((b.ps[5], "bk5"), (b.ps[6], "bk6"),
                    (b.ps[7], "bk7"), (b.ps[4], "bk4"),
                    (b.ps[0], "bk0"), (b.ps[1], "bk1"),
                    (b.ps[2], "bk2"), (b.ps[3], "bk3"))
        cxs = []
        for pi in range(4):
            inf = pinf[pi]
            for hh in range(2):
                ci4 = 2 * pi + hh
                base = hh * 64
                if pi == 0:
                    s_ = "" if hh == 0 else "2"
                    bufs = {nm: (b.sb[nm + s_], nm + s_) for nm in
                            ("Y0", "Y1", "SqA", "SqB", "OqA", "OqB", "MQ")}
                elif pi == 1:
                    bufs = {}
                    for i3, nm in enumerate(
                            ("Y0", "Y1", "SqA", "SqB", "OqA", "OqB", "MQ")):
                        v, kk = b.A3.view(hh * 1024 + i3 * 128, 128)
                        bufs[nm] = (v, kk[0])
                elif pi == 2:
                    bufs = {}
                    for i3, nm in enumerate(
                            ("Y0", "Y1", "SqA", "SqB", "OqA", "OqB", "MQ")):
                        v, kk = b.A3.view(2048 + hh * 1024 + i3 * 128, 128)
                        bufs[nm] = (v, kk[0])
                else:
                    bufs = {}
                    for i3, nm in enumerate(
                            ("Y0", "Y1", "SqA", "SqB", "OqA", "OqB", "MQ")):
                        v, kk = b.A2.view(hh * 1024 + i3 * 128, 128)
                        bufs[nm] = (v, kk[0])
                cx = dict(
                    pi=pi, hh=hh, base=base, p=inf["p"], inf=inf,
                    kf=inf["ksg"][base:base + 64, o0:o0 + C],
                    qf=inf["qsg"][base:base + 64, o0:o0 + C],
                    khf=inf["khfm"][base:base + 64, :],
                    Y=[bufs["Y0"][0], bufs["Y1"][0]],
                    ky=[bufs["Y0"][1], bufs["Y1"][1]], cur=0,
                    MQ=bufs["MQ"][0], MQk=bufs["MQ"][1],
                    SqA=bufs["SqA"], SqB=bufs["SqB"],
                    OqA=bufs["OqA"], OqB=bufs["OqB"],
                    apb=ap_banks[ci4][0], apk=ap_banks[ci4][1],
                    sqb=ap_banks[ci4][0], sqk=ap_banks[ci4][1],
                    tp1=None if hh == 0 else (0, 64), tcol=8 + ci4,
                )
                cxs.append(cx)
        for cx in cxs:
            h = 2 * cx["p"] + cx["hh"]
            b.tt(tte[:, cx["tcol"]:cx["tcol"] + 1], colb[:, 32 + h:33 + h],
                 colb[:, 16 + h:17 + h], ALU.mult, r=["colb"], w=["tte"])
        for cx in cxs:
            ba = cx["base"]
            b.ts(cx["Y"][0][:, 0:64], cx["inf"]["vtm"][:, ba:ba + 64],
                 tte[:, cx["tcol"]:cx["tcol"] + 1][:, 0:1], ALU.mult,
                 r=[cx["inf"]["vtmk"], "tte"], w=[cx["ky"][0]])
        for cx in cxs:
            ba = cx["base"]
            b.copy(cx["Y"][0][:, 64:128], cx["inf"]["kh"][:, ba:ba + 64],
                   r=[cx["inf"]["khk"]], w=[cx["ky"][0]], eng="vector")
        for tag, lf, mname, dst in (
                ("grS", lambda c_: (c_["kf"], c_["khf"], [c_["inf"]["khfk"]]),
                 "mask_sT_neg", "SqA"),
                ("grO", lambda c_: (c_["khf"], c_["kf"], [c_["inf"]["khfk"]]),
                 "mask_s_neg", "OqA"),
                ("grMQ", lambda c_: (c_["kf"], c_["qf"], c_["inf"]["qsgk"]),
                 "mask_iT", "MQ")):
            for cx in cxs:
                l_, r_, xk = lf(cx)
                b.mm(cx["apb"][:, 0:128], l_, r_, r=cx["inf"]["ksgk"] + xk,
                     w=[cx["apk"]], tag=tag)
            for cx in cxs:
                dv, dk = (cx["MQ"], cx["MQk"]) if dst == "MQ" else cx[dst]
                b.tt(dv, cx["apb"][:, 0:128], b.const(mname),
                     ALU.mult, r=[cx["apk"], "consts"], w=[dk])
        for cx in cxs:
            cx["Sq"], cx["Sqk"] = cx["SqA"]
            cx["Oq"], cx["Oqk"] = cx["OqA"]
            cx["Sqn"], cx["Sqnk"] = cx["SqB"]
            cx["Oqn"], cx["Oqnk"] = cx["OqB"]
        for lev in range(NEUM + 1):
            for cx in cxs:
                cur = cx["cur"]
                b.mm(cx["apb"][:, 0:128], cx["Sq"], cx["Y"][cur],
                     r=[cx["ky"][cur], cx["Sqk"]], w=[cx["apk"]],
                     tag=f"ap{lev}")
            for cx in cxs:
                cur = cx["cur"]
                b.tt(cx["Y"][1 - cur], cx["apb"][:, 0:128], cx["Y"][cur],
                     ALU.add, r=[cx["apk"], cx["ky"][cur]],
                     w=[cx["ky"][1 - cur]], eng="vector")
                cx["cur"] = 1 - cur
            if lev == NEUM:
                break
            for cx in cxs:
                b.mm(cx["sqb"][:, 128:256], cx["Oq"], cx["Sq"],
                     r=[cx["Sqk"], cx["Oqk"]], w=[cx["sqk"]], tag="sqS")
            for cx in cxs:
                b.copy(cx["Sqn"], cx["sqb"][:, 128:256], r=[cx["sqk"]],
                       w=[cx["Sqnk"]], eng="scalar")
            if lev < NEUM - 1:
                for cx in cxs:
                    b.mm(cx["sqb"][:, 256:384], cx["Sq"], cx["Oq"],
                         r=[cx["Sqk"], cx["Oqk"]], w=[cx["sqk"]], tag="sqO")
                for cx in cxs:
                    b.copy(cx["Oqn"], cx["sqb"][:, 256:384], r=[cx["sqk"]],
                           w=[cx["Oqnk"]], eng="scalar")
                    cx["Oq"], cx["Oqn"], cx["Oqk"], cx["Oqnk"] = \
                        cx["Oqn"], cx["Oq"], cx["Oqnk"], cx["Oqk"]
            for cx in cxs:
                cx["Sq"], cx["Sqn"], cx["Sqk"], cx["Sqnk"] = \
                    cx["Sqn"], cx["Sq"], cx["Sqnk"], cx["Sqk"]
        for cx in cxs:
            cx["Yf"], cx["kyf"] = cx["Y"][cx["cur"]], cx["ky"][cx["cur"]]
            cx["fb"], cx["fbk"] = cx["apb"], cx["apk"]
            cx["kcol"] = cx["inf"]["ktm"][:, cx["base"]:cx["base"] + 64]

        def fod(cx, c0, c1):
            return (cx["fb"][0:64, c0:c1] if cx["hh"] == 0
                    else cx["fb"][64:128, c0:c1])
        for cx in cxs:
            cx["od"] = fod(cx, 0, 128)
            b.mm(cx["od"], cx["Yf"][:, 0:64], cx["MQ"],
                 r=[cx["kyf"], cx["MQk"]], w=[cx["fbk"]], tp=cx["tp1"],
                 tag="O1")
        for cx in cxs:
            ba, p = cx["base"], cx["p"]
            b.copy(b.sb["O1st"][ba:ba + 64, p * C:(p + 1) * C], cx["od"],
                   r=[cx["fbk"]], w=["O1st"], eng="scalar")
        for cx in cxs:
            b.mm(cx["od"], cx["Yf"][:, 64:128], cx["MQ"],
                 r=[cx["kyf"], cx["MQk"]], w=[cx["fbk"]], tp=cx["tp1"],
                 tag="W3")
        for cx in cxs:
            ba, p = cx["base"], cx["p"]
            b.tt(b.sb["Qpst"][ba:ba + 64, p * C:(p + 1) * C], cx["qf"],
                 cx["od"], ALU.subtract,
                 r=[cx["fbk"]] + cx["inf"]["qsgk"], w=["Qpst"])
        for cx in cxs:
            cx["od"] = fod(cx, 128, 192)
            b.mm(cx["od"], cx["Yf"][:, 64:128], cx["kcol"],
                 r=[cx["kyf"], cx["inf"]["ktmk"]], w=[cx["fbk"]],
                 tp=cx["tp1"], tag="Gt")
        for cx in cxs:
            ba, p = cx["base"], cx["p"]
            b.tt(GH[ba:ba + 64, p * 64:(p + 1) * 64],
                 b.const("I64pair")[ba:ba + 64, 0:64], cx["od"], ALU.subtract,
                 r=[cx["fbk"], "consts"], w=["GHst"])
        for cx in cxs:
            cx["od"] = fod(cx, 192, 256)
            b.mm(cx["od"], cx["kcol"], cx["Yf"][:, 64:128],
                 r=[cx["kyf"], cx["inf"]["ktmk"]], w=[cx["fbk"]],
                 tp=cx["tp1"], tag="Gn")
        for cx in cxs:
            ba, p = cx["base"], cx["p"]
            b.tt(b.sb["cmpb"][ba:ba + 64, p * 64:(p + 1) * 64],
                 b.const("I64pair")[ba:ba + 64, 0:64], cx["od"], ALU.subtract,
                 r=[cx["fbk"], "consts"], w=["cmpb"])
        for cx in cxs:
            cx["od"] = fod(cx, 256, 320)
            b.mm(cx["od"], cx["kcol"], cx["Yf"][:, 0:64],
                 r=[cx["kyf"], cx["inf"]["ktmk"]], w=[cx["fbk"]],
                 tp=cx["tp1"], tag="Hm")
        for cx in cxs:
            ba, p = cx["base"], cx["p"]
            b.copy(GH[ba:ba + 64, (HP + p) * 64:(HP + p + 1) * 64], cx["od"],
                   r=[cx["fbk"]], w=["GHst"], eng="scalar")
        for pi in range(4):
            p = 4 * pp + pi
            bank = b.ps[6]
            b.mm(bank[:, 0:1], b.const(f"E16_{p}")[0:16, :],
                 b.sb["b_all"][0:16, c * C + C - 1:c * C + C],
                 r=["b_all", "consts"], w=["bk6"], tag="bl")
            for which in ("gt", "gn", "h"):
                if which == "gt":
                    dst, dk = GH[:, p * 64:(p + 1) * 64], "GHst"
                elif which == "gn":
                    dst, dk = b.sb["cmpb"][:, p * 64:(p + 1) * 64], "cmpb"
                else:
                    dst, dk = GH[:, (HP + p) * 64:(HP + p + 1) * 64], "GHst"
                b.ts(dst, dst, bank[:, 0:1], ALU.mult, r=[dk, "bk6"], w=[dk])
            ph_compose(b, cfg, c, p)
    b.dma("gpsimd", b.dr["sc_O1"][c], b.sb["O1st"][:, :], r=["O1st"],
          w=["sc_O1"], tag=f"sO1_{c}")
    b.dma("gpsimd", b.dr["sc_Qp"][c], b.sb["Qpst"][:, :], r=["Qpst"],
          w=["sc_Qp"], tag=f"sQp_{c}")
    b.dma("gpsimd", b.dr["sc_GH"][c], b.sb["GHst"][:, :], r=["GHst"],
          w=["sc_GH"], tag=f"sGH_{c}")


def ph_compose(b, cfg, c, p):
    HP = cfg.HP
    GH = b.sb["GHst"]
    sl = slice(p * 64, (p + 1) * 64)
    gt_c = GH[:, sl]
    gn_c = b.sb["cmpb"][:, sl]
    h_c = GH[:, (HP + p) * 64:(HP + p + 1) * 64]
    Gs0, Gs1, Hs = b.sb["Gs0"], b.sb["Gs1"], b.sb["Hs"]
    if c == 0:
        b.copy(Gs0[:, sl], gt_c, r=["GHst"], w=["Gs0"], eng="vector")
        b.copy(Gs1[:, sl], gn_c, r=["cmpb"], w=["Gs1"], eng="vector")
        b.copy(Hs[:, sl], h_c, r=["GHst"], w=["Hs"], eng="vector")
        return
    cmpa = b.sb["cmpa"]
    for hh in range(2):
        base = hh * 64
        tp1 = None if hh == 0 else (hh * 64, 64)
        bank = b.ps[6]
        od = bank[0:64, 192:256] if hh == 0 else bank[64:128, 192:256]
        b.mm(od, Gs1[base:base + 64, sl], gt_c[base:base + 64, :],
             r=["Gs1", "GHst"], w=["bk6"], tp=tp1, tag="cGt")
        b.copy(cmpa[base:base + 64, sl], od, r=["bk6"], w=["cmpa"], eng="scalar")
        bank = b.ps[7]
        od = bank[0:64, 64:128] if hh == 0 else bank[64:128, 64:128]
        b.mm(od, gt_c[base:base + 64, :], Gs1[base:base + 64, sl],
             r=["Gs1", "GHst"], w=["bk7"], tp=None if hh == 0 else (64, 64),
             tag="cGn")
        bank = b.ps[4]
        od2 = bank[0:64, 192:256] if hh == 0 else bank[64:128, 192:256]
        b.mm(od2, gt_c[base:base + 64, :], Hs[base:base + 64, sl],
             r=["Hs", "GHst"], w=["bk4"], tp=None if hh == 0 else (64, 64),
             tag="cH")
        b.tt(Hs[base:base + 64, sl], od2, h_c[base:base + 64, :], ALU.add,
             r=["bk4", "GHst"], w=["Hs"])
        b.copy(Gs1[base:base + 64, sl], od, r=["bk7"], w=["Gs1"], eng="scalar")
    b.copy(Gs0[:, sl], cmpa[:, sl], r=["cmpa"], w=["Gs0"], eng="vector")


def ph_gh_ag(b, cfg):
    HP = cfg.HP
    gh = b.sb["ghin"]
    b.copy(gh[:, 0:HP * 64], b.sb["Gs0"][:, :], r=["Gs0"], w=["ghin"],
           eng="vector")
    b.copy(gh[:, HP * 64:], b.sb["Hs"][:, :], r=["Hs"], w=["ghin"], eng="vector")
    b.dma("gpsimd", b.dr["cc_gh_in"], gh[:, :], r=["ghin"], w=["cc_gh_in"],
          tag="ghst")
    b.allgather(b.dr["cc_gh_in"], b.dr["cc_gh_out"], r=["cc_gh_in"],
                w=["cc_gh_out"], tag="ag_gh")
    S0 = b.sb["S0"]
    b.memset(S0[:, :], 0.0, w=["S0"] + [f"S0.{p}" for p in range(HP)])
    pm = b.sb["premask"]
    cmpa, cmpb = b.sb["cmpa"], b.sb["cmpb"]
    for a in range(cfg.N_CORES):
        b.dma("sync", gh[:, :], b.dr["cc_gh_out"][a * 128:(a + 1) * 128, :],
              r=["cc_gh_out"], w=["ghin"], tag=f"ghl{a}")
        for p in range(HP):
            sl = slice(p * 64, (p + 1) * 64)
            gt_a = gh[:, sl]
            h_a = gh[:, (HP + p) * 64:(HP + p + 1) * 64]
            b.ts(cmpa[:, sl], gt_a, pm[:, a:a + 1][:, 0:1], ALU.mult,
                 r=["ghin", "premask"], w=["cmpa"])
            b.ts(cmpb[:, sl], b.const("I64pair")[:, 0:64],
                 pm[:, a:a + 1][:, 0:1], ALU.mult, r=["consts", "premask"],
                 w=["cmpb"])
            b.tt(cmpb[:, sl], b.const("I64pair")[:, 0:64], cmpb[:, sl],
                 ALU.subtract, r=["cmpb", "consts"], w=["cmpb"])
            b.tt(cmpa[:, sl], cmpa[:, sl], cmpb[:, sl], ALU.add,
                 r=["cmpa", "cmpb"], w=["cmpa"])
            for hh in range(2):
                base = hh * 64
                bank = b.ps[6]
                od = bank[0:64, 256:320] if hh == 0 else bank[64:128, 256:320]
                b.mm(od, cmpa[base:base + 64, sl], S0[base:base + 64, sl],
                     r=["cmpa", f"S0.{p}"], w=["bk6"],
                     tp=None if hh == 0 else (64, 64), tag="s0c")
                b.stt(S0[base:base + 64, sl], h_a[base:base + 64, :],
                      pm[base:base + 64, a:a + 1][:, 0:1], od,
                      ALU.mult, ALU.add, r=["ghin", "premask", "bk6"],
                      w=[f"S0.{p}"])


def ph_pass2(b, cfg):
    C, HP, HD = cfg.C, cfg.HP, cfg.HD
    S0 = b.sb["S0"]
    sets = []
    for par in range(2):
        O1v, O1k = b.A2.view(par * 3072, 1024)
        Qpv, Qpk = b.A2.view(par * 3072 + 1024, 1024)
        GHv, GHk = b.A2.view(par * 3072 + 2048, 1024)
        sets.append((O1v, O1k, Qpv, Qpk, GHv, GHk))
    for c in range(cfg.NTILE):
        O1, O1k, Qp, Qpk, GH, GHk = sets[c % 2]
        b.dma("sync", O1[:, :], b.dr["sc_O1"][c], r=O1k + ["sc_O1"], w=O1k,
              tag=f"lO1_{c}")
        b.dma("sync", Qp[:, :], b.dr["sc_Qp"][c], r=Qpk + ["sc_Qp"], w=Qpk,
              tag=f"lQp_{c}")
        b.dma("gpsimd", GH[:, :], b.dr["sc_GH"][c], r=GHk + ["sc_GH"], w=GHk,
              tag=f"lGH_{c}")
        pcx = []
        for p in range(HP):
            sc, sck = b.A1.view(p * 1024, 384)
            pcx.append((p, b.ps[p], f"bk{p}", sc, sck,
                        slice(p * 64, (p + 1) * 64)))
        for p, bank, kb, sc, sck, sl in pcx:
            b.mm(bank[:, 0:C], b.const(f"E16_{p}")[0:16, :],
                 b.sb["b_all"][0:16, c * C:(c + 1) * C],
                 r=["b_all", "consts"], w=[kb], tag="be")
        for p, bank, kb, sc, sck, sl in pcx:
            items = []
            for hh in range(2):
                base = hh * 64
                od = bank[0:64, 128:128 + C] if hh == 0 \
                    else bank[64:128, 128:128 + C]
                items.append((od, S0[base:base + 64, sl],
                              Qp[base:base + 64, p * C:(p + 1) * C],
                              True, True, None if hh == 0 else (64, 64)))
            b.mmg(items, r=[f"S0.{p}"] + Qpk, w=[kb], tag="op2")
        for p, bank, kb, sc, sck, sl in pcx:
            ot = sc[:, 0:C]
            b.tt(ot, bank[:, 128:128 + C], O1[:, p * C:(p + 1) * C], ALU.add,
                 r=[kb] + O1k, w=sck)
            b.tt(ot, ot, bank[:, 0:C], ALU.mult, r=sck + [kb], w=sck)
            b.act(sc[:, C:2 * C], ot, AF.Square, r=sck, w=sck)
        for p, bank, kb, sc, sck, sl in pcx:
            b.mm(bank[0:2, 256:256 + C], b.const("R2"), sc[:, C:2 * C],
                 r=sck + ["consts"], w=[kb], tag="on")
        for p, bank, kb, sc, sck, sl in pcx:
            ors = sc[:2, 2 * C:3 * C]
            b.act(ors, bank[0:2, 256:256 + C], AF.Ln,
                  bias=b.const("rms_eps_hd")[:2, 0:1], scale=1.0 / HD,
                  r=[kb, "consts"], w=sck)
            b.act(ors, ors, AF.Exp, scale=-0.5, r=sck, w=sck)
        for p, bank, kb, sc, sck, sl in pcx:
            b.mm(bank[:, 256:256 + C], b.const("E2")[0:2, :],
                 sc[0:2, 2 * C:3 * C], r=sck + ["consts"], w=[kb], tag="oe")
        for p, bank, kb, sc, sck, sl in pcx:
            ot = sc[:, 0:C]
            b.stt(ot, ot, b.sb["mnorm"][:, 0:1], bank[:, 256:256 + C],
                  ALU.mult, ALU.mult, r=sck + ["mnorm", kb], w=sck)
            b.dma("gpsimd",
                  b.dr["sc_reads"][:, (c * HP + p) * C:(c * HP + p + 1) * C],
                  ot, r=sck, w=["sc_reads"], tag="rsp")
        for p, bank, kb, sc, sck, sl in pcx:
            items = []
            for hh in range(2):
                base = hh * 64
                od = bank[0:64, 384:448] if hh == 0 else bank[64:128, 448:512]
                items.append((od, GH[base:base + 64, sl],
                              S0[base:base + 64, sl],
                              True, True, None if hh == 0 else (64, 64)))
            b.mmg(items, r=GHk + [f"S0.{p}"], w=[kb], tag="sw")
        for p, bank, kb, sc, sck, sl in pcx:
            for hh in range(2):
                base = hh * 64
                od = bank[0:64, 384:448] if hh == 0 else bank[64:128, 448:512]
                b.tt(S0[base:base + 64, sl], od,
                     GH[base:base + 64, (HP + p) * 64:(HP + p + 1) * 64],
                     ALU.add, r=[kb] + GHk, w=[f"S0.{p}"])


def ph_outproj(b, cfg):
    NT, D, C, QCH = cfg.NT, cfg.D, cfg.C, cfg.QCH
    gsl = b.sb["outt"]
    lnst = b.sb["lnst"]
    for nh in range(2):
        wob, wobk = b.A1.view(0, 2 * QCH * 512)
        for half in range(2):
            b.dma("sync", wob[:, half * QCH * 512:(half + 1) * QCH * 512],
                  b.dr["out_W"][half * D:(half + 1) * D, nh * 512:(nh + 1) * 512]
                  .rearrange("(k p) n -> p k n", p=128),
                  r=wobk, w=wobk, tag=f"owl{nh}_{half}")
        for t in range(cfg.NTILE):
            fuse, fusek = b.A2.view((t % 2) * 2 * QCH * 128, 2 * QCH * 128)
            b.dma("sync", fuse[:, 0:QCH * 128],
                  b.dr["sc_reads"][:, t * QCH * 128:(t + 1) * QCH * 128],
                  r=["sc_reads"] + fusek, w=fusek, tag=f"fr{t}_{nh}")
            b.dma("gpsimd", fuse[:, QCH * 128:2 * QCH * 128],
                  b.dr["sc_p20"].rearrange("p (k n) -> p k n", n=NT)[:, :, t * C:(t + 1) * C],
                  r=["sc_p20"] + fusek, w=fusek, tag=f"fp{t}_{nh}")
            bank_rs = b.ps[4 + t % 2]
            krs = f"bk{4 + t % 2}"
            b.mm(bank_rs[:, 0:C], b.const("ones_blk")[0:1, :],
                 b.sb["rsrow"][0:1, t * C:(t + 1) * C], r=["rsrow", "consts"],
                 w=[krs], tag="rse2")
            for f in range(QCH):
                vn = b.sb["vnorm"][:, f:f + 1]
                sl_ = slice(QCH * 128 + f * 128, QCH * 128 + (f + 1) * 128)
                b.stt(fuse[:, sl_], fuse[:, sl_], vn[:, 0:1], bank_rs[:, 0:C],
                      ALU.mult, ALU.mult, r=fusek + ["vnorm", krs], w=fusek)
            if nh == 0:
                gpart = gsl[:, (t % 2) * 512:(t % 2) * 512 + 512]
            else:
                gpart = gsl[:, 512:1024]
                b.dma("scalar", gsl[:, 0:512],
                      b.dr["sc_x1"][t * C:(t + 1) * C, 0:512],
                      r=["sc_x1", "outt"], w=["outt"], tag=f"lnl{t}")
            b.dma("scalar", gpart,
                  b.dr["sc_gate"][t * C:(t + 1) * C, nh * 512:(nh + 1) * 512],
                  r=["sc_gate", "outt"], w=["outt"], tag=f"gl{t}_{nh}")
            b.act(gpart, gpart, AF.Sigmoid, r=["outt"], w=["outt"])
            xt, xtk = b.A2.view(4096 + (t % 2) * 1024, 512)
            b.dma("scalar", xt,
                  b.dr["x_tm"][t * C:(t + 1) * C, nh * 512:(nh + 1) * 512],
                  r=xtk, w=xtk, tag=f"xr{t}_{nh}")
            bank = b.ps[6 + t % 2]
            bkk = f"bk{6 + t % 2}"
            items = []
            for half in range(2):
                woff = half * QCH * 512
                items += [(bank[:, 0:512],
                           fuse[:, half * QCH * 128 + kc * 128:
                                half * QCH * 128 + (kc + 1) * 128],
                           wob[:, woff + kc * 512: woff + (kc + 1) * 512],
                           (half == 0 and kc == 0), False, None)
                          for kc in range(QCH)]
            items.append((bank[:, 0:512], b.const("ones_blk")[0:1, :],
                          b.sb["out_b"][0:1, nh * 512:(nh + 1) * 512],
                          False, True, None))
            b.mmg(items, r=fusek + wobk + ["out_b", "consts"], w=[bkk],
                  tag=f"om{t}_{nh}")
            b.tt(gpart, gpart, bank[:, 0:512], ALU.mult, r=["outt", bkk],
                 w=["outt"])
            b.tt(gpart, gpart, xt, ALU.add, r=["outt"] + xtk, w=["outt"])
            if nh == 0:
                b.dma("sync",
                      b.dr["sc_x1"][t * C:(t + 1) * C, 0:512],
                      gpart, r=["outt"], w=["sc_x1"], tag=f"x1s{t}_{nh}")
            else:
                _layernorm_tm(b, cfg, gsl[:, 0:D], lnst, 0)
                b.dma("sync", b.dr["sc_x1"][t * C:(t + 1) * C, :],
                      gsl[:, 0:D], r=["outt"], w=["sc_x1"], tag=f"x1s{t}")


def _layernorm_tm(b, cfg, row, lnst, base, rowk=("outt",)):
    D = cfg.D
    rk = list(rowk)
    sq = b.sb["sqb"]
    b.red(lnst[:, base:base + 1], row, ALU.add, r=rk, w=["lnst"])
    b.act(sq[:, 0:D], row, AF.Square, r=rk, w=["sqb"])
    b.red(lnst[:, base + 1:base + 2], sq[:, 0:D], ALU.add, r=["sqb"], w=["lnst"])
    b.ts(lnst[:, base:base + 1], lnst[:, base:base + 1], 1.0 / D, ALU.mult,
         r=["lnst"], w=["lnst"])
    b.ts(lnst[:, base + 1:base + 2], lnst[:, base + 1:base + 2], 1.0 / D,
         ALU.mult, r=["lnst"], w=["lnst"])
    b.tt(lnst[:, base + 2:base + 3], lnst[:, base:base + 1],
         lnst[:, base:base + 1], ALU.mult, r=["lnst"], w=["lnst"])
    b.tt(lnst[:, base + 2:base + 3], lnst[:, base + 1:base + 2],
         lnst[:, base + 2:base + 3], ALU.subtract, r=["lnst"], w=["lnst"])
    b.act(lnst[:, base + 3:base + 4], lnst[:, base + 2:base + 3], AF.Ln,
          bias=b.const("ln_eps")[:, 0:1], r=["lnst", "consts"], w=["lnst"])
    b.act(lnst[:, base + 3:base + 4], lnst[:, base + 3:base + 4], AF.Exp,
          scale=-0.5, r=["lnst"], w=["lnst"])
    b.ts(row, row, lnst[:, base:base + 1], ALU.subtract, r=rk + ["lnst"],
         w=rk)
    b.ts(row, row, lnst[:, base + 3:base + 4], ALU.mult, r=rk + ["lnst"],
         w=rk)


def ph_ffn(b, cfg):
    D, C, QCH, ICH = cfg.D, cfg.C, cfg.QCH, cfg.ICH
    I = cfg.I_FFN
    scl = b.sb["scl"]
    lnst = b.sb["lnst"]
    outt = b.sb["outt"]
    TG = min(4, cfg.NTILE)
    lanes = [b.A2.view(i * 512, 512) for i in range(4)]
    lk = [ln[1][0] for ln in lanes]

    def quant_cols(out_ap, in_ap, ncols, rkeys, wkeys, tag):
        nblk = ncols // 16
        sc = scl[:, 0:nblk]
        b.red(sc, in_ap.rearrange("p (g k) -> p g k", k=16), ALU.max,
              absval=True, r=rkeys, w=["scl"], tag=tag)
        b.ts(sc, sc, 1.0 / 6.0, ALU.mult, 1e-12, ALU.max, r=["scl"], w=["scl"])
        rsc = scl[:, 176:176 + nblk]
        b.act(rsc, sc, AF.Ln, r=["scl"], w=["scl"])
        b.act(rsc, rsc, AF.Exp, scale=-1.0, r=["scl"], w=["scl"])
        hsc = scl[:, 352:352 + nblk]
        b.ts(hsc, sc, 0.5, ALU.mult, r=["scl"], w=["scl"])
        rexp = rsc.unsqueeze(2).broadcast_to([128, nblk, 16])
        hexp = hsc.unsqueeze(2).broadcast_to([128, nblk, 16])
        lv = [ln[0][:, 0:ncols].rearrange("p (g k) -> p g k", k=16)
              for ln in lanes]
        b.quant(out_ap.rearrange("p (g k) -> p g k", k=16),
                in_ap.rearrange("p (g k) -> p g k", k=16), rexp, hexp, lv, lk,
                list(rkeys) + ["scl"], wkeys)

    NSL2 = (I + 511) // 512
    wslots = [b.A3.view(0, 2048), b.A3.view(2048, 2048),
              b.A3.view(4096, 2048), b.A2.view(2048, 2048),
              b.A2.view(4096, 2048)]
    wcnt = 0
    for tg in range(cfg.NTILE // TG):
        for j in range(TG):
            t = tg * TG + j
            x1v, x1k = b.A1.view(5120, 1024)
            b.dma("sync" if j % 2 == 0 else "gpsimd", x1v,
                  b.dr["sc_x1"][t * C:(t + 1) * C, :],
                  r=x1k + ["sc_x1"], w=x1k, tag=f"x1l{t}")
            xq, xqk = b.A1.view(0, 1024)
            for hx in range(2):
                quant_cols(xq[:, hx * 512:(hx + 1) * 512],
                           x1v[:, hx * 512:(hx + 1) * 512], 512, x1k, xqk,
                           f"xq{t}_{hx}")
            xqf, xqfk = b.A1.view(1024 + j * 1024, 1024)
            for f in range(QCH):
                bank = b.ps[6 + f % 2]
                bkk = f"bk{6 + f % 2}"
                b.tp(bank[:, 0:128], xq[:, f * 128:(f + 1) * 128], r=xqk, w=[bkk])
                b.copy(xqf[:, f * 128:(f + 1) * 128], bank[:, 0:128], r=[bkk],
                       w=xqfk, eng="scalar")
        for n2 in range(NSL2):
            w2 = min(512, I - n2 * 512)
            nsl = w2 // 256
            gsl_, usl_ = [], []
            for sh in range(nsl):
                ns = n2 * 2 + sh
                wgv, wgk = wslots[wcnt % 5]
                wcnt += 1
                b.dma("sync", wgv[:, 0:QCH * 256],
                      b.dr["cc_wg_out"][:, ns * 256:(ns + 1) * 256]
                      .rearrange("(k p) n -> p k n", p=128),
                      r=wgk + ["cc_wg_out"], w=wgk, tag=f"fwg{tg}_{ns}")
                gsl_.append((wgv, wgk))
            for sh in range(nsl):
                ns = n2 * 2 + sh
                wuv, wuk = wslots[wcnt % 5]
                wcnt += 1
                b.dma("gpsimd", wuv[:, 0:QCH * 256],
                      b.dr["cc_wu_out"][:, ns * 256:(ns + 1) * 256]
                      .rearrange("(k p) n -> p k n", p=128),
                      r=wuk + ["cc_wu_out"], w=wuk, tag=f"fwu{tg}_{ns}")
                usl_.append((wuv, wuk))
            gks = sum((g[1] for g in gsl_), [])
            uks = sum((u[1] for u in usl_), [])
            for j in range(TG):
                t = tg * TG + j
                xqf, xqfk = b.A1.view(1024 + j * 1024, 1024)
                bkg = b.ps[j]
                bku = b.ps[4 + j]
                kg, ku = f"bk{j}", f"bk{4 + j}"
                b.mmg([(bkg[:, sh * 256:(sh + 1) * 256],
                        xqf[:, kc * 128:(kc + 1) * 128],
                        gsl_[sh][0][:, kc * 256:(kc + 1) * 256],
                        kc == 0, kc == QCH - 1, None)
                       for sh in range(nsl) for kc in range(QCH)],
                      r=xqfk + gks, w=[kg], tag=f"fmg{t}_{n2}")
                b.mmg([(bku[:, sh * 256:(sh + 1) * 256],
                        xqf[:, kc * 128:(kc + 1) * 128],
                        usl_[sh][0][:, kc * 256:(kc + 1) * 256],
                        kc == 0, kc == QCH - 1, None)
                       for sh in range(nsl) for kc in range(QCH)],
                      r=xqfk + uks, w=[ku], tag=f"fmu{t}_{n2}")
            for j in range(TG):
                t = tg * TG + j
                bkg, bku = b.ps[j], b.ps[4 + j]
                kg, ku = f"bk{j}", f"bk{4 + j}"
                jo = (0, 5120, 6144, 7168)[j]
                hsl, hslk = b.A1.view(jo, 512)
                b.act(hsl[:, 0:w2], bkg[:, 0:w2], AF.Silu, r=[kg], w=hslk)
                b.tt(hsl[:, 0:w2], hsl[:, 0:w2], bku[:, 0:w2], ALU.mult,
                     r=hslk + [ku], w=hslk)
                hqs, hqsk = b.A1.view(jo + 512, 512)
                quant_cols(hqs[:, 0:w2], hsl[:, 0:w2], w2, hslk, hqsk,
                           f"hq{t}_{n2}")
            for j in range(TG):
                t = tg * TG + j
                bku, ku = b.ps[4 + j], f"bk{4 + j}"
                jo = (0, 5120, 6144, 7168)[j]
                hsl, hslk = b.A1.view(jo, 512)
                hqs, hqsk = b.A1.view(jo + 512, 512)
                stg, stgk = hsl, hslk
                for k2 in range(w2 // 128):
                    b.tp(bku[:, k2 * 128:(k2 + 1) * 128],
                         hqs[:, k2 * 128:(k2 + 1) * 128], r=hqsk, w=[ku])
                for k2 in range(nsl):
                    b.copy(stg[:, k2 * 256:(k2 + 1) * 256],
                           bku[:, k2 * 256:(k2 + 1) * 256],
                           r=[ku], w=stgk, eng="scalar")
                b.dma("sync" if j % 2 == 0 else "gpsimd",
                      b.dr["sc_hq"][t][:, n2 * 512:n2 * 512 + w2],
                      stg[:, 0:w2], r=stgk, w=[f"sc_hq{t}"],
                      tag=f"hqsp{t}_{n2}")
    hqslots = [b.A1.view(0, ICH * 128), b.A1.view(ICH * 128, ICH * 128),
               b.A2.view(0, ICH * 128)]
    x1v, x1k = b.A2.view(3072, 2048)
    outb, outbk = b.A2.view(5120, 1024)
    for pp in range(cfg.NTILE // 2):
        ta, tb = 2 * pp, 2 * pp + 1
        hqA, hqAk = hqslots[(2 * pp) % 3]
        hqB, hqBk = hqslots[(2 * pp + 1) % 3]
        b.dma("scalar", hqA, b.dr["sc_hq"][ta], r=hqAk + [f"sc_hq{ta}"],
              w=hqAk, tag=f"hql{ta}")
        b.dma("gpsimd", hqB, b.dr["sc_hq"][tb], r=hqBk + [f"sc_hq{tb}"],
              w=hqBk, tag=f"hql{tb}")
        b.dma("scalar", x1v[:, 0:1024], b.dr["sc_x1"][ta * C:(ta + 1) * C, :],
              r=x1k + ["sc_x1"], w=x1k, tag=f"x1r{ta}")
        b.dma("sync", x1v[:, 1024:2048],
              b.dr["sc_x1"][tb * C:(tb + 1) * C, :],
              r=x1k + ["sc_x1"], w=x1k, tag=f"x1r{tb}")
        for nh in range(4):
            bkA, bkAk = b.ps[4 + nh % 2], f"bk{4 + nh % 2}"
            bkB, bkBk = b.ps[6 + nh % 2], f"bk{6 + nh % 2}"
            for hf in range(2):
                k0 = hf * 11
                k1 = 11 if hf == 0 else ICH - 11
                wdv, wdk = b.A3.view(hf * 2816, 2816)
                b.dma("sync" if hf == 0 else "gpsimd", wdv[:, 0:k1 * 256],
                      b.dr["cc_wd_out"][k0 * 128:(k0 + k1) * 128,
                                        nh * 256:(nh + 1) * 256]
                      .rearrange("(k p) n -> p k n", p=128),
                      r=wdk + ["cc_wd_out"], w=wdk, tag=f"dwl{pp}_{nh}_{hf}")
                b.mmg([(bkA[:, 0:256],
                        hqA[:, (k0 + kc) * 128:(k0 + kc + 1) * 128],
                        wdv[:, kc * 256:(kc + 1) * 256],
                        (hf == 0 and kc == 0), (hf == 1 and kc == k1 - 1),
                        None) for kc in range(k1)],
                      r=hqAk + wdk, w=[bkAk], tag=f"dmA{pp}_{nh}_{hf}")
                b.mmg([(bkB[:, 0:256],
                        hqB[:, (k0 + kc) * 128:(k0 + kc + 1) * 128],
                        wdv[:, kc * 256:(kc + 1) * 256],
                        (hf == 0 and kc == 0), (hf == 1 and kc == k1 - 1),
                        None) for kc in range(k1)],
                      r=hqBk + wdk, w=[bkBk], tag=f"dmB{pp}_{nh}_{hf}")
            b.tt(outt[:, nh * 256:(nh + 1) * 256], bkA[:, 0:256],
                 x1v[:, nh * 256:(nh + 1) * 256], ALU.add, r=[bkAk] + x1k,
                 w=["outt"], tag=f"dresA{pp}_{nh}")
            b.tt(outb[:, nh * 256:(nh + 1) * 256], bkB[:, 0:256],
                 x1v[:, 1024 + nh * 256:1024 + (nh + 1) * 256], ALU.add,
                 r=[bkBk] + x1k, w=outbk, tag=f"dresB{pp}_{nh}")
        _layernorm_tm(b, cfg, outt[:, 0:D], lnst, 4)
        b.dma("gpsimd", b.dr["y"][ta * C:(ta + 1) * C, :], outt[:, 0:D],
              r=["outt"], w=["y_out"], tag=f"yst{ta}")
        _layernorm_tm(b, cfg, outb, lnst, 4, rowk=outbk)
        b.dma("sync", b.dr["y"][tb * C:(tb + 1) * C, :], outb,
              r=outbk, w=["y_out"], tag=f"yst{tb}")



def kernel(**inputs):
    cfg = CFG()
    in_maps, cnames = prep_inputs(cfg, inputs)
    nc = build_nc(cfg, cnames)
    res = run_bass_kernel_spmd(nc, in_maps, core_ids=list(range(cfg.N_CORES)))
    span = cfg.S // cfg.SPANS
    out = np.empty((cfg.B, cfg.S, cfg.D), np.float32)
    for c in range(cfg.N_CORES):
        bb = c // cfg.SPANS
        s0 = (c % cfg.SPANS) * span
        out[bb, s0:s0 + span] = res.results[c]["y"]
    return out



# version 36 (final)
# speedup vs baseline: 1.0190x; 1.0190x over previous
import contextlib
import math
import numpy as np

import concourse.bass as bass
import concourse.mybir as mybir
from concourse.bass_utils import run_bass_kernel_spmd

F32 = mybir.dt.float32
AF = mybir.ActivationFunctionType
ALU = mybir.AluOpType
AX = mybir.AxisListType

HALF_PI = math.pi / 2.0


class CFG:
    B = 2
    S = 8192
    D = 1024
    H = 16
    HD = 64
    I_FFN = 2816
    N_CORES = 8
    C = 128
    LN_EPS = 1e-5
    RMS_EPS = 1e-6
    L2_EPS = 1e-6

    def __init__(self, **kw):
        for k, v in kw.items():
            setattr(self, k, v)
        self.SPANS = self.N_CORES // self.B
        self.NT = self.B * self.S // self.N_CORES
        self.NTILE = self.NT // self.C
        self.NG = (self.D // 2) // 128
        self.QCH = self.D // 128
        self.ICH = self.I_FFN // 128
        self.HP = self.H // 2


ENGINES = ("tensor", "vector", "scalar", "gpsimd", "sync")


class Op:
    __slots__ = ("eng", "fn", "reads", "writes", "inc", "need_inc",
                 "cum_after", "deps", "tag", "sem", "lane_wait", "gself",
                 "lwg")

    def __init__(self, eng, fn, reads, writes, inc, tag):
        self.eng, self.fn = eng, fn
        self.reads, self.writes = tuple(reads), tuple(writes)
        self.inc = inc
        self.need_inc = False
        self.cum_after = None
        self.deps = []
        self.tag = tag


class Sched:
    def __init__(self):
        self.ops = []
        self.last_writer = {}
        self.readers = {}

    def add(self, eng, fn, reads=(), writes=(), inc=1, tag=""):
        op = Op(eng, fn, reads, writes, inc, tag)
        deps = []
        for k in op.reads:
            w = self.last_writer.get(k)
            if w is not None:
                deps.append(w)
        for k in op.writes:
            w = self.last_writer.get(k)
            if w is not None:
                deps.append(w)
            deps.extend(self.readers.get(k, ()))
        seen = set()
        for d in deps:
            if id(d) not in seen and d is not op:
                seen.add(id(d))
                op.deps.append(d)
        self.ops.append(op)
        for k in op.reads:
            self.readers.setdefault(k, []).append(op)
        for k in op.writes:
            self.last_writer[k] = op
            self.readers[k] = []
        return op

    N_LANES = 4

    def finalize(self):
        self.cc_idx = 0
        per_eng = {e: [] for e in ENGINES}
        for op in self.ops:
            per_eng[op.eng].append(op)
        for op in self.ops:
            if op.inc == 16 or op.tag.startswith("ag_"):
                op.need_inc = True
            for d in op.deps:
                d.need_inc = True
        for e in ENGINES:
            cum = 0
            lane_rr = 0
            lane_cum = {}
            for op in per_eng[e]:
                if op.tag.startswith("ag_"):
                    ln = "cc%d" % self.cc_idx
                    self.cc_idx += 1
                    op.sem = ln
                    op.lane_wait = 0
                    op.cum_after = op.inc
                elif op.inc == 16:
                    ln = lane_rr % self.N_LANES
                    lane_rr += 1
                    op.sem = (e, ln)
                    op.lane_wait = lane_cum.get(ln, 0)
                    lane_cum[ln] = op.lane_wait + op.inc
                    op.cum_after = lane_cum[ln]
                else:
                    op.sem = e
                    op.lane_wait = None
                    if op.need_inc:
                        cum += op.inc
                    op.cum_after = cum
        eng_g = {}
        lane_g = {}
        for op in self.ops:
            gm = {}
            for d in op.deps:
                for s, v in d.gself.items():
                    if gm.get(s, -1) < v:
                        gm[s] = v
            if op.lane_wait is None and not op.sem.__class__ is tuple \
                    and not str(op.sem).startswith("cc"):
                base = eng_g.setdefault(op.eng, {})
                for s, v in gm.items():
                    if base.get(s, -1) < v:
                        base[s] = v
                if op.need_inc:
                    base[op.sem] = op.cum_after
                op.gself = dict(base)
            elif op.lane_wait is not None and op.sem.__class__ is tuple:
                base = lane_g.setdefault(op.sem, {})
                op.lwg = dict(base)
                for s, v in gm.items():
                    if base.get(s, -1) < v:
                        base[s] = v
                base[op.sem] = op.cum_after
                op.gself = dict(base)
            else:
                gm[op.sem] = op.cum_after
                op.gself = gm
        return per_eng


class Arena:

    def __init__(self, b, name, cols):
        self.name = name
        self.t = b.ctx.enter_context(b.nc.sbuf_tensor(name, [128, cols], F32))
        self.cols = cols

    def view(self, c0, ncols, p0=0, np_=128):
        ap = self.t[p0:p0 + np_, c0:c0 + ncols]
        r0, r1 = c0 // 1024, (c0 + ncols - 1) // 1024
        keys = [f"{self.name}.{r}" for r in range(r0, r1 + 1)]
        return ap, keys


class B:
    def __init__(self, nc, cfg):
        self.nc = nc
        self.cfg = cfg
        self.sch = Sched()
        self.ctx = contextlib.ExitStack()
        self.sb = {}
        self.dr = {}
        self.cn = {}
        self.ps = []

    def sbuf(self, name, shape):
        t = self.ctx.enter_context(self.nc.sbuf_tensor("s_" + name, shape, F32))
        self.sb[name] = t.ap()
        return self.sb[name]

    def const(self, name):
        off, w = self.cn[name]
        return self.sb["consts"][:, off:off + w]

    def op(self, eng, fn, r=(), w=(), inc=1, tag=""):
        return self.sch.add(eng, fn, r, w, inc, tag)

    def dma(self, q, out_ap, in_ap, r=(), w=(), tag=""):
        return self.op(q, lambda e, o=out_ap, i=in_ap: e.dma_start(out=o, in_=i),
                       r, w, inc=16, tag=tag)

    def mmg(self, items, r=(), w=(), tag=""):
        def fn(e, items=tuple(items)):
            first = ins = None
            for (o, l, rr, st, sp, tp) in items:
                if tp is None:
                    ins = e.matmul(o, l, rr, start=st, stop=sp)
                else:
                    ins = e.matmul(o, l, rr, start=st, stop=sp, tile_position=tp)
                if first is None:
                    first = ins
            return (first, ins)
        return self.op("tensor", fn, r, w, tag=tag)

    def mm(self, out, lhsT, rhs, start=True, stop=True, r=(), w=(), tp=None, tag=""):
        return self.mmg([(out, lhsT, rhs, start, stop, tp)], r, w, tag)

    def tp(self, out_psum, in_ap, r=(), w=(), ident=None, tpos=None, tag=""):
        if ident is None:
            kk = in_ap.shape[0]
            ident = self.const("ident")[0:kk, 0:kk]
        idap = ident
        if tpos is None:
            return self.op("tensor",
                           lambda e, o=out_psum, i=in_ap, d=idap:
                           e.transpose(o, i, d), r, w, tag=tag)
        return self.op("tensor",
                       lambda e, o=out_psum, i=in_ap, d=idap, t=tpos:
                       e.transpose(o, i, d, tile_position=t), r, w, tag=tag)

    def tt(self, out, a, bb, alu, r=(), w=(), eng="vector", tag=""):
        return self.op(eng, lambda e, o=out, x=a, y=bb, op=alu: e.tensor_tensor(
            out=o, in0=x, in1=y, op=op), r, w, tag=tag)

    def ts(self, out, a, s1, op0, s2=None, op1=None, r=(), w=(), eng="vector", tag=""):
        if op1 is None:
            return self.op(eng, lambda e, o=out, x=a, s=s1, op=op0:
                           e.tensor_single_scalar(out=o, in_=x, scalar=s, op=op),
                           r, w, tag=tag)
        return self.op(eng, lambda e, o=out, x=a, sa=s1, sb=s2, oa=op0, ob=op1:
                       e.tensor_scalar(out=o, in0=x, scalar1=sa, scalar2=sb,
                                       op0=oa, op1=ob), r, w, tag=tag)

    def stt(self, out, in0, s, in1, op0, op1, r=(), w=(), eng="vector", tag=""):
        return self.op(eng, lambda e, o=out, x=in0, sc=s, y=in1, oa=op0, ob=op1:
                       e.scalar_tensor_tensor(out=o, in0=x, scalar=sc, in1=y,
                                              op0=oa, op1=ob), r, w, tag=tag)

    def act(self, out, in_, func, bias=0.0, scale=1.0, r=(), w=(), tag=""):
        return self.op("scalar", lambda e, o=out, i=in_, f=func, bb=bias, ss=scale:
                       e.activation(out=o, in_=i, func=f, bias=bb, scale=ss),
                       r, w, tag=tag)

    def copy(self, out, in_, r=(), w=(), eng="scalar", tag=""):
        if eng == "scalar":
            return self.op(eng, lambda e, o=out, i=in_: e.copy(out=o, in_=i),
                           r, w, tag=tag)
        return self.op(eng, lambda e, o=out, i=in_: e.tensor_copy(out=o, in_=i),
                       r, w, tag=tag)

    def red(self, out, in_, op, absval=False, r=(), w=(), tag=""):
        return self.op("vector", lambda e, o=out, i=in_, oo=op, av=absval:
                       e.tensor_reduce(out=o, in_=i, axis=AX.X, op=oo,
                                       apply_absolute_value=av), r, w, tag=tag)

    def scan(self, out, d0, d1, init, r=(), w=(), tag=""):
        return self.op("vector", lambda e, o=out, a=d0, bb=d1, ii=init:
                       e.tensor_tensor_scan(out=o, data0=a, data1=bb, initial=ii,
                                            op0=ALU.mult, op1=ALU.add),
                       r, w, tag=tag)

    def memset(self, ap, v, r=(), w=(), eng="vector", tag=""):
        return self.op(eng, lambda e, a=ap, vv=v: e.memset(a, vv), r, w, tag=tag)

    def recip(self, out, in_, r=(), w=(), tag=""):
        return self.op("vector", lambda e, o=out, i=in_: e.reciprocal(out=o, in_=i),
                       r, w, tag=tag)

    def allgather(self, in_ap, out_ap, r=(), w=(), tag=""):
        n = self.cfg.N_CORES
        return self.op("gpsimd", lambda e, i=in_ap, o=out_ap, nn=n:
                       e.collective_compute("AllGather", ALU.bypass,
                                            replica_groups=[list(range(nn))],
                                            ins=[i.opt()], outs=[o.opt()]),
                       r, w, inc=1, tag=tag)

    def quant(self, out3, v3, rexp, hexp, lanes, lkeys, rk, wk):
        b = self
        L0, L1, L2, L3 = lanes
        k0, k1, k2, k3 = lkeys
        b.tt(L0, v3, rexp, ALU.mult, r=rk, w=[k0])
        b.act(L1, L0, AF.Abs, r=[k0], w=[k1])
        b.ts(L2, L1, 2.5, ALU.is_gt, r=[k1], w=[k2], eng="vector")
        b.stt(L2, L1, 3.5, L2, ALU.is_gt, ALU.add, r=[k1, k2], w=[k2], eng="vector")
        b.stt(L2, L1, 5.0, L2, ALU.is_gt, ALU.add, r=[k1, k2], w=[k2], eng="vector")
        b.stt(L2, L1, 5.0, L2, ALU.is_gt, ALU.add, r=[k1, k2], w=[k2], eng="vector")
        b.ts(L3, L1, 0.25, ALU.is_gt, r=[k1], w=[k3])
        for thr in (0.75, 1.25, 1.75):
            b.stt(L3, L1, thr, L3, ALU.is_gt, ALU.add, r=[k1, k3], w=[k3])
        b.stt(L3, L2, 2.0, L3, ALU.mult, ALU.add, r=[k2, k3], w=[k3])
        b.ts(L1, L0, 0.0, ALU.is_lt, r=[k0], w=[k1], eng="vector")
        b.ts(L1, L1, -2.0, ALU.mult, 1.0, ALU.add, r=[k1], w=[k1], eng="vector")
        b.tt(L3, L3, L1, ALU.mult, r=[k1, k3], w=[k3])
        b.tt(out3, L3, hexp, ALU.mult, r=[k3] + list(rk), w=wk)



def make_consts(cfg):
    cols, names = [], {}

    def add(arr, name):
        arr = np.asarray(arr, np.float32)
        if arr.ndim == 1:
            arr = arr[:, None]
        pad = np.zeros((128, arr.shape[1]), np.float32)
        pad[: arr.shape[0]] = arr
        names[name] = (sum(c.shape[1] for c in cols), arr.shape[1])
        cols.append(pad)

    add(np.eye(128, dtype=np.float32), "ident")
    add(-np.tril(np.ones((128, 128), np.float32), -1).T, "mask_sT_neg")
    add(-np.tril(np.ones((128, 128), np.float32), -1), "mask_s_neg")
    add(np.tril(np.ones((128, 128), np.float32), 0).T, "mask_iT")
    add(np.ones((128, 128), np.float32), "ones_blk")
    e2 = np.zeros((2, 128), np.float32)
    e2[0, :64] = 1.0
    e2[1, 64:] = 1.0
    add(e2, "E2")
    add(e2 * (cfg.HD ** -0.5), "E2s")
    r2 = np.zeros((128, 2), np.float32)
    r2[:64, 0] = 1.0
    r2[64:, 1] = 1.0
    add(r2, "R2")
    add(np.concatenate([np.eye(64, dtype=np.float32)] * 2, axis=0), "I64pair")
    for p in range(cfg.H // 2):
        ep = np.zeros((16, 128), np.float32)
        ep[2 * p, :64] = 1.0
        ep[2 * p + 1, 64:] = 1.0
        add(ep, f"E16_{p}")
    add(np.full(128, -HALF_PI, np.float32), "neg_half_pi")
    add(np.full(128, cfg.LN_EPS, np.float32), "ln_eps")
    add(np.full(128, cfg.RMS_EPS, np.float32), "rms_eps")
    add(np.full(128, cfg.RMS_EPS * cfg.HD, np.float32), "rms_eps_hd")
    add(np.full(128, cfg.L2_EPS, np.float32), "l2_eps")
    return np.concatenate(cols, axis=1), names


def _fm_cols(v):
    return np.ascontiguousarray(np.asarray(v, np.float32).reshape(-1, 128).T)


def _conv_cols(w):
    w = np.asarray(w, np.float32)
    nf = w.shape[0] // 128
    out = np.zeros((128, nf * 4), np.float32)
    for f in range(nf):
        out[:, 4 * f:4 * f + 4] = w[128 * f:128 * (f + 1)]
    return np.ascontiguousarray(out)


def prep_inputs(cfg, inp):
    perm = np.concatenate([np.arange(0, cfg.D, 2), np.arange(1, cfg.D, 2)])
    x = np.asarray(inp["L0_attn_hidden"], np.float32)
    ctrl_W = np.ascontiguousarray(np.asarray(inp["L0_attn_ctrl_W"], np.float32))
    p20c_W = np.ascontiguousarray(np.asarray(inp["L0_attn_p20c_W"], np.float32)[perm])
    oW = np.asarray(inp["L0_attn_out_W"], np.float32)
    out_W = np.ascontiguousarray(np.concatenate([oW[:cfg.D], oW[cfg.D:][perm]], 0))
    w_gate = np.asarray(inp["L0_ffn_w_gate"], np.float32)
    w_up = np.asarray(inp["L0_ffn_w_up"], np.float32)
    w_down = np.asarray(inp["L0_ffn_w_down"], np.float32)

    consts, cnames = make_consts(cfg)
    cb = np.asarray(inp["L0_attn_ctrl_b"], np.float32)
    ctrl_b = np.zeros((128, 26), np.float32)
    for m in range(24):
        ctrl_b[:, m] = cb[128 * m: 128 * (m + 1)]
    ctrl_b[0:16, 24] = cb[3 * cfg.D: 3 * cfg.D + cfg.H]
    ctrl_b[32:48, 25] = cb[3 * cfg.D + cfg.H: 3 * cfg.D + 2 * cfg.H]
    gate_b = np.ascontiguousarray(cb[3 * cfg.D + 2 * cfg.H:][None, :])
    out_b = np.ascontiguousarray(np.asarray(inp["L0_attn_out_b"], np.float32)[None, :])
    convw = np.concatenate([_conv_cols(inp["L0_attn_qconv_w"]),
                            _conv_cols(inp["L0_attn_kconv_w"]),
                            _conv_cols(inp["L0_attn_vconv_w"])], axis=1)
    vnorm = _fm_cols(np.asarray(inp["L0_attn_vnorm_scale"], np.float32)[perm])
    mnorm = np.ascontiguousarray(
        np.tile(np.asarray(inp["L0_attn_mnorm_scale"], np.float32), 2)[:, None])
    ramp = _fm_cols(inp["L0_attn_ramp_logit"])
    p20_dl = np.ascontiguousarray(
        np.asarray(inp["L0_attn_p20_decay_log"], np.float32).reshape(cfg.NG, 128).T)
    p20_th = np.ascontiguousarray(
        np.asarray(inp["L0_attn_p20_theta"], np.float32).reshape(cfg.NG, 128).T)

    span = cfg.S // cfg.SPANS
    sh = cfg.D // cfg.N_CORES
    si = cfg.I_FFN // cfg.N_CORES
    in_maps = []
    for c in range(cfg.N_CORES):
        bb = c // cfg.SPANS
        s0 = (c % cfg.SPANS) * span
        xs = np.ascontiguousarray(x[bb, s0:s0 + span])
        halo = np.zeros((3, cfg.D), np.float32)
        if s0 > 0:
            halo = np.ascontiguousarray(x[bb, s0 - 3:s0])
        pm = np.zeros((128, cfg.N_CORES), np.float32)
        for a in range(cfg.N_CORES):
            if (a // cfg.SPANS) == (c // cfg.SPANS) and (a % cfg.SPANS) < (c % cfg.SPANS):
                pm[:, a] = 1.0
        in_maps.append({
            "x_tm": xs, "x_halo": halo,
            "ctrl_W": ctrl_W, "p20c_W": p20c_W, "out_W": out_W,
            "wg_shard": np.ascontiguousarray(w_gate[:, c * sh:(c + 1) * sh]),
            "wu_shard": np.ascontiguousarray(w_up[:, c * sh:(c + 1) * sh]),
            "wd_shard": np.ascontiguousarray(w_down[:, c * si:(c + 1) * si]),
            "consts": consts, "ctrl_b": ctrl_b, "gate_b": gate_b, "out_b": out_b,
            "convw": convw, "vnorm": vnorm, "mnorm": mnorm, "ramp": ramp,
            "p20_dl": p20_dl, "p20_th": p20_th, "premask": pm,
        })
    return in_maps, cnames



def build_nc(cfg, cnames, taps=()):
    nc = bass.Bass("TRN2", target_bir_lowering=False)
    b = B(nc, cfg)
    b.cn = cnames
    b.taps = set(taps)
    NT, D, C = cfg.NT, cfg.D, cfg.C
    QCH, ICH, NG, HP, HD = cfg.QCH, cfg.ICH, cfg.NG, cfg.HP, cfg.HD
    sh = D // cfg.N_CORES
    si = cfg.I_FFN // cfg.N_CORES

    dr = b.dr
    def din(name, shape):
        dr[name] = nc.dram_tensor(name, shape, F32, kind="ExternalInput").ap()
    def dint(name, shape):
        dr[name] = nc.dram_tensor(name, shape, F32).ap()
    din("x_tm", [NT, D]); din("x_halo", [3, D])
    din("ctrl_W", [D, 4128]); din("p20c_W", [D, 3 * D]); din("out_W", [2 * D, D])
    din("wg_shard", [cfg.I_FFN, sh]); din("wu_shard", [cfg.I_FFN, sh])
    din("wd_shard", [D, si])
    ncc = sum(w for (_, w) in cnames.values())
    din("consts", [128, ncc]); din("ctrl_b", [128, 26]); din("gate_b", [1, D])
    din("out_b", [1, D]); din("convw", [128, 96]); din("vnorm", [128, QCH])
    din("mnorm", [128, 1]); din("ramp", [128, QCH]); din("p20_dl", [128, NG])
    din("p20_th", [128, NG]); din("premask", [128, cfg.N_CORES])
    dr["y"] = nc.dram_tensor("y", [NT, D], F32, kind="ExternalOutput").ap()
    for nm in b.taps:
        dr["tap_" + nm] = nc.dram_tensor("tap_" + nm, [128, 8192], F32,
                                         kind="ExternalOutput").ap()
    dint("sc_p20", [128, QCH * NT])
    dint("sc_gate", [NT, D])
    dint("sc_O1", [cfg.NTILE, 128, HP * C])
    dint("sc_Qp", [cfg.NTILE, 128, HP * C])
    dint("sc_GH", [cfg.NTILE, 128, H2C(cfg)])
    dint("sc_reads", [128, QCH * NT])
    dint("sc_x1", [NT, D])
    dint("sc_hq", [cfg.NTILE, 128, ICH * 128])
    dint("cc_p20_in", [2, NG * 128]); dint("cc_p20_out", [2 * cfg.N_CORES, NG * 128])
    dint("cc_gh_in", [128, cfg.H * HD]); dint("cc_gh_out", [128 * cfg.N_CORES, cfg.H * HD])
    dint("cc_wg_in", [sh, cfg.I_FFN]); dint("cc_wg_out", [D, cfg.I_FFN])
    dint("cc_wu_in", [sh, cfg.I_FFN]); dint("cc_wu_out", [D, cfg.I_FFN])
    dint("cc_wd_in", [si, D]); dint("cc_wd_out", [cfg.I_FFN, D])

    b.ps = [b.ctx.enter_context(nc.psum_tensor(f"bk{i}", [128, 512], F32))
            for i in range(8)]

    sb = b.sbuf
    sb("consts", [128, ncc])
    for nm, shp in [("ctrl_b", [128, 26]), ("gate_b", [1, D]), ("out_b", [1, D]),
                    ("convw", [128, 96]), ("vnorm", [128, QCH]), ("mnorm", [128, 1]),
                    ("ramp", [128, QCH]), ("p20_dl", [128, NG]), ("p20_th", [128, NG]),
                    ("premask", [128, cfg.N_CORES])]:
        sb(nm, shp)
    sb("trig", [128, 8 * NG])
    sb("p20sm", [128, 16 * NG])
    sb("rsrow", [1, NT])
    sb("tta", [128, 128]); sb("ttb", [128, 128]); sb("ttc", [128, 128])
    sb("ttd", [128, 128]); sb("tte", [128, 16])
    sb("decb", [64, NT + 3])
    sb("g_all", [16, NT]); sb("beta_all", [48, NT]); sb("b_all", [16, NT])
    sb("colb", [128, 48])
    for sfx in ("", "2"):
        sb("ktm" + sfx, [128, 128]); sb("vtm" + sfx, [128, 128])
        sb("khtm" + sfx, [128, 128]); sb("khfm" + sfx, [128, 128])
        sb("Y0" + sfx, [128, 128]); sb("Y1" + sfx, [128, 128])
        sb("SqA" + sfx, [128, 128]); sb("SqB" + sfx, [128, 128])
        sb("OqA" + sfx, [128, 128]); sb("OqB" + sfx, [128, 128])
        sb("MQ" + sfx, [128, 128])
    sb("O1st", [128, HP * C]); sb("Qpst", [128, HP * C]); sb("GHst", [128, cfg.H * HD])
    sb("Gs1", [128, HP * HD]); sb("Gs0", [128, HP * HD])
    sb("Hs", [128, HP * HD]); sb("cmpa", [128, HP * HD]); sb("cmpb", [128, HP * HD])
    sb("S0", [128, HP * HD]); sb("ghin", [128, cfg.H * HD])
    sb("sqb", [128, 1024])
    sb("bpt", [128, 48])
    sb("tails", [128, 72])
    sb("lnst", [128, 8])
    sb("outt", [128, D])
    sb("xcb", [128, 2 * QCH * 259])
    sb("scl", [128, 3 * 176])

    b.A1 = Arena(b, "A1", 8256)
    b.A2 = Arena(b, "A2", 6144)
    b.A3 = Arena(b, "A3", 6144)
    b.sb["subr"] = b.A3.view(512, NG * 128)[0]
    b.sb["subi"] = b.A3.view(512 + NG * 128, NG * 128)[0]
    b.sb["rp"] = b.A3.view(512 + 2 * NG * 128, NG * 128)[0]

    emit(b, cfg)

    per_eng = b.sch.finalize()
    sems = {e: b.ctx.enter_context(nc.semaphore(f"sem_{e}")) for e in ENGINES}
    for e in ("sync", "gpsimd", "scalar", "vector", "tensor"):
        for ln in range(Sched.N_LANES):
            sems[(e, ln)] = b.ctx.enter_context(nc.semaphore(f"sem_{e}_{ln}"))
    for i in range(b.sch.cc_idx):
        sems["cc%d" % i] = b.ctx.enter_context(nc.semaphore(f"sem_cc{i}"))
    with b.ctx:
        with nc.Block() as block:
            for ename in ENGINES:
                ops = per_eng[ename]

                def section(eng, ops=ops, ename=ename):
                    wm = {}

                    def merge(g):
                        for s, v in g.items():
                            if wm.get(s, 0) < v:
                                wm[s] = v

                    for op in ops:
                        pend = []
                        for d in op.deps:
                            if wm.get(d.sem, 0) >= d.cum_after:
                                continue
                            pend.append((d.sem, d.cum_after))
                            merge(d.gself)
                        if op.lane_wait is not None and op.lane_wait > 0 \
                                and wm.get(op.sem, 0) < op.lane_wait:
                            pend.append((op.sem, op.lane_wait))
                            merge(op.lwg)
                            if wm.get(op.sem, 0) < op.lane_wait:
                                wm[op.sem] = op.lane_wait
                        for sm, vv in pend[:-1]:
                            eng.wait_ge(sems[sm], vv)
                        ret = op.fn(eng)
                        first, ins = ret if isinstance(ret, tuple) \
                            else (ret, ret)
                        if ins is None and (pend or op.need_inc
                                            or op.lane_wait is not None):
                            first = ins = eng.nop()
                        if pend:
                            first.wait_op(sems[pend[-1][0]], pend[-1][1],
                                          "sem-ge")
                        if op.need_inc or op.lane_wait is not None:
                            ins.then_inc(sems[op.sem], op.inc)

                getattr(block, ename)(section)
    return nc


def H2C(cfg):
    return cfg.H * cfg.HD



def emit(b, cfg):
    ph_loads(b, cfg)
    ph_p20_setup(b, cfg)
    ph_wquant(b, cfg)
    ph_p20(b, cfg)
    ph_attn(b, cfg)
    ph_gh_ag(b, cfg)
    ph_pass2(b, cfg)
    ph_outproj(b, cfg)
    ph_ffn(b, cfg)
    b.op("gpsimd", lambda e: None, r=["y_out"], tag="fence")


def ph_loads(b, cfg):
    for nm in ("consts", "ctrl_b", "gate_b", "out_b", "convw", "vnorm", "mnorm",
               "ramp", "p20_dl", "p20_th", "premask"):
        b.dma("sync", b.sb[nm][:, :] if b.sb[nm].shape[0] == 128 else b.sb[nm][:, :],
              b.dr[nm], w=[nm], tag=f"ld_{nm}")
    b.act(b.sb["ramp"][:, :], b.sb["ramp"][:, :], AF.Sigmoid, r=["ramp"], w=["ramp"])


def ph_p20_setup(b, cfg):
    NG, NT = cfg.NG, cfg.NT
    sm, tg = b.sb["p20sm"], b.sb["trig"]

    def smc(g, j):
        return sm[:, g * 16 + j: g * 16 + j + 1]

    def tgc(g, j):
        return tg[:, 8 * g + j: 8 * g + j + 1]

    dl, th = b.sb["p20_dl"], b.sb["p20_th"]
    tte = b.sb["tte"]
    for g in range(NG):
        b.act(smc(g, 13), dl[:, g:g + 1], AF.Exp, r=["p20_dl"], w=["p20sm"])
        b.act(smc(g, 0), smc(g, 13), AF.Exp, scale=-1.0, r=["p20sm"], w=["p20sm"])
        b.act(tgc(g, 6), th[:, g:g + 1], AF.Sin,
              bias=b.const("neg_half_pi")[:, 0:1], scale=0.5,
              r=["p20_th", "consts"], w=["trig"])
        b.act(tgc(g, 7), th[:, g:g + 1], AF.Sin, scale=0.5, r=["p20_th"], w=["trig"])
        b.act(tgc(g, 2), tgc(g, 6), AF.Square, r=["trig"], w=["trig"])
        b.ts(tgc(g, 0), tgc(g, 2), 2.0, ALU.mult, -1.0, ALU.add,
             r=["trig"], w=["trig"])
        b.tt(tgc(g, 1), tgc(g, 6), tgc(g, 7), ALU.mult, r=["trig"], w=["trig"])
        b.ts(tgc(g, 1), tgc(g, 1), -2.0, ALU.mult, r=["trig"], w=["trig"])
        b.copy(tgc(g, 2), tgc(g, 0), r=["trig"], w=["trig"], eng="vector")
        b.ts(tgc(g, 3), tgc(g, 1), -1.0, ALU.mult, r=["trig"], w=["trig"])
    subr, subi, rp = b.sb["subr"], b.sb["subi"], b.sb["rp"]
    for g in range(NG):
        b.memset(subr[:, g * 128: g * 128 + 1], 1.0, w=["subr"])
        b.memset(subi[:, g * 128: g * 128 + 1], 0.0, w=["subi"])
        b.memset(rp[:, g * 128: g * 128 + 1], 1.0, w=["rp"])
        b.copy(tgc(g, 4), tgc(g, 2), r=["trig"], w=["trig"], eng="vector")
        b.copy(tgc(g, 5), tgc(g, 3), r=["trig"], w=["trig"], eng="vector")
        b.copy(smc(g, 14), smc(g, 0), r=["p20sm"], w=["p20sm"], eng="vector")
        total = 1
        while total < NT:
            k = total
            fr, fi = tgc(g, 4), tgc(g, 5)
            if k < 128:
                sl = slice(g * 128, g * 128 + k)
                dl_ = slice(g * 128 + k, g * 128 + 2 * k)
                b.ts(tgc(g, 6), fi, -1.0, ALU.mult, r=["trig"], w=["trig"])
                t0 = tte[:, 0:1].broadcast_to([128, k]) if False else None
                ta = b.sb["tta"][:, 0:k]
                b.ts(ta, subr[:, sl], fr[:, 0:1], ALU.mult,
                     r=["subr", "trig"], w=["tta"])
                b.stt(subr[:, dl_], subi[:, sl], tgc(g, 6)[:, 0:1], ta,
                      ALU.mult, ALU.add, r=["subi", "trig", "tta"], w=["subr"])
                b.ts(ta, subr[:, sl], fi[:, 0:1], ALU.mult,
                     r=["subr", "trig"], w=["tta"])
                b.stt(subi[:, dl_], subi[:, sl], fr[:, 0:1], ta, ALU.mult, ALU.add,
                      r=["subi", "trig", "tta"], w=["subi"])
                b.ts(ta, rp[:, sl], smc(g, 14)[:, 0:1], ALU.mult,
                     r=["rp", "p20sm"], w=["tta"])
                b.copy(rp[:, dl_], ta, r=["tta"], w=["rp"], eng="vector")
            if total == 128:
                b.copy(smc(g, 2), tgc(g, 4), r=["trig"], w=["p20sm"], eng="vector")
                b.copy(smc(g, 3), tgc(g, 5), r=["trig"], w=["p20sm"], eng="vector")
                b.copy(smc(g, 1), smc(g, 14), r=["p20sm"], w=["p20sm"], eng="vector")
            b.tt(tgc(g, 7), fr, fi, ALU.mult, r=["trig"], w=["trig"])
            b.tt(tte[:, 0:1], fi, fi, ALU.mult, r=["trig"], w=["tte"])
            b.tt(fr, fr, fr, ALU.mult, r=["trig"], w=["trig"])
            b.tt(fr, fr, tte[:, 0:1], ALU.subtract, r=["trig", "tte"], w=["trig"])
            b.ts(fi, tgc(g, 7), 2.0, ALU.mult, r=["trig"], w=["trig"])
            b.tt(smc(g, 14), smc(g, 14), smc(g, 14), ALU.mult, r=["p20sm"],
                 w=["p20sm"])
            total *= 2
        b.copy(smc(g, 5), tgc(g, 4), r=["trig"], w=["p20sm"], eng="vector")
        b.ts(smc(g, 6), tgc(g, 5), -1.0, ALU.mult, r=["trig"], w=["p20sm"])
        b.copy(smc(g, 4), smc(g, 14), r=["p20sm"], w=["p20sm"], eng="vector")


def ph_wquant(b, cfg):
    D = cfg.D
    sh = D // cfg.N_CORES
    si = cfg.I_FFN // cfg.N_CORES
    wshv, wshk = b.A2.view(0, 512)
    lanes = [b.A2.view(512 + i * 512, 512) for i in range(4)]
    scl = b.sb["scl"]
    jobs = [("wg_shard", "cc_wg_in", cfg.I_FFN, sh),
            ("wu_shard", "cc_wu_in", cfg.I_FFN, sh),
            ("wd_shard", "cc_wd_in", D, si)]
    for wi, (src, dst, rows, colsn) in enumerate(jobs):
        nrt = (rows + 127) // 128
        for rt in range(nrt):
            r0 = rt * 128
            pr = min(128, rows - r0)
            seg = wshv[:pr, :colsn]
            b.dma("sync", seg, b.dr[src][r0:r0 + pr, :], r=wshk, w=wshk,
                  tag=f"wql{wi}_{rt}")
            nblk = colsn // 16
            sc = scl[:pr, :nblk]
            b.red(sc, seg.rearrange("p (g k) -> p g k", k=16), ALU.max,
                  absval=True, r=wshk, w=["scl"])
            b.ts(sc, sc, 1.0 / 6.0, ALU.mult, 1e-12, ALU.max, r=["scl"], w=["scl"])
            rsc = scl[:pr, 176:176 + nblk]
            b.act(rsc, sc, AF.Ln, r=["scl"], w=["scl"])
            b.act(rsc, rsc, AF.Exp, scale=-1.0, r=["scl"], w=["scl"])
            hsc = scl[:pr, 352:352 + nblk]
            b.ts(hsc, sc, 0.5, ALU.mult, r=["scl"], w=["scl"])
            rexp = rsc.unsqueeze(2).broadcast_to([pr, nblk, 16])
            hexp = hsc.unsqueeze(2).broadcast_to([pr, nblk, 16])
            v3 = seg.rearrange("p (g k) -> p g k", k=16)
            lv = [ln[0][:pr, :colsn].rearrange("p (g k) -> p g k", k=16)
                  for ln in lanes]
            lk = [ln[1][0] for ln in lanes]
            b.quant(v3, v3, rexp, hexp, lv, lk, wshk + ["scl"], wshk)
            nct = (colsn + 127) // 128
            for ct in range(nct):
                c0 = ct * 128
                pc = min(128, colsn - c0)
                bank = b.ps[ct % 2]
                bkk = f"bk{ct % 2}"
                b.tp(bank[:pc, :pr], seg[:, c0:c0 + pc], r=wshk, w=[bkk])
                b.copy(b.sb["tta"][:pc, :pr], bank[:pc, :pr], r=[bkk], w=["tta"],
                       eng="scalar")
                b.dma("gpsimd", b.dr[dst][c0:c0 + pc, r0:r0 + pr],
                      b.sb["tta"][:pc, :pr], r=["tta"], w=[dst], tag="wqsp")
                b.sch.last_writer["tta"] = b.sch.ops[-1]
    b.allgather(b.dr["cc_wg_in"], b.dr["cc_wg_out"], r=["cc_wg_in"],
                w=["cc_wg_out"], tag="ag_wg")
    b.allgather(b.dr["cc_wu_in"], b.dr["cc_wu_out"], r=["cc_wu_in"],
                w=["cc_wu_out"], tag="ag_wu")
    b.allgather(b.dr["cc_wd_in"], b.dr["cc_wd_out"], r=["cc_wd_in"],
                w=["cc_wd_out"], tag="ag_wd")


def ph_p20(b, cfg):
    NT, NG = cfg.NT, cfg.NG
    sm, tg = b.sb["p20sm"], b.sb["trig"]
    subr, subi, rp = b.sb["subr"], b.sb["subi"], b.sb["rp"]

    def smc(g, j):
        return sm[:, g * 16 + j: g * 16 + j + 1]

    def tgc(g, j):
        return tg[:, 8 * g + j: 8 * g + j + 1]

    ntile = NT // 128
    urv, urk = b.A1.view(0, NT)
    uiv, uik = b.A1.view(NT, NT)
    wrv, wrk = b.A1.view(2 * NT, NT)
    wiv, wik = b.A1.view(3 * NT, NT)
    tta, ttb = b.sb["tta"], b.sb["ttb"]
    ttc, ttd = b.sb["ttc"], b.sb["ttd"]

    for g in range(NG):
        for t in range(ntile):
            xv, xk = b.A3.view(2048 + (t % 8) * 256, 256)
            b.dma("sync" if t % 2 == 0 else "gpsimd", xv,
                  b.dr["x_tm"][t * 128:(t + 1) * 128, 256 * g:256 * (g + 1)],
                  r=xk, w=xk, tag=f"p20x{g}_{t}")
            for comp, dst, dk in ((0, urv, urk), (1, uiv, uik)):
                bank = b.ps[comp]
                bkk = f"bk{comp}"
                b.tp(bank[:, 0:128], xv[:, comp:256:2], r=xk, w=[bkk])
                b.copy(dst[:, t * 128:(t + 1) * 128], bank[:, 0:128],
                       r=[bkk], w=dk, eng="scalar")
        bpt = b.sb["bpt"]
        bptr, bpti = bpt[:, 0:16], bpt[:, 16:32]
        _cpow_table(b, g, smc, bptr, bpti, ntile)
        ssl = slice(g * 128, (g + 1) * 128)
        ur3 = urv[:, 0:NT].rearrange("q (t c) -> q t c", c=128)
        ui3 = uiv[:, 0:NT].rearrange("q (t c) -> q t c", c=128)
        wr3 = wrv[:, 0:NT].rearrange("q (t c) -> q t c", c=128)
        wi3 = wiv[:, 0:NT].rearrange("q (t c) -> q t c", c=128)
        srb = subr[:, ssl].unsqueeze(1).broadcast_to([128, ntile, 128])
        sib = subi[:, ssl].unsqueeze(1).broadcast_to([128, ntile, 128])
        b7 = bptr[:, 0:ntile].unsqueeze(2).broadcast_to([128, ntile, 128])
        b8 = bpti[:, 0:ntile].unsqueeze(2).broadcast_to([128, ntile, 128])
        b.tt(wr3, ur3, srb, ALU.mult, r=urk + ["subr"], w=wrk)
        b.tt(wi3, ui3, sib, ALU.mult, r=uik + ["subi"], w=wik)
        b.tt(wr3, wr3, wi3, ALU.subtract, r=wrk + wik, w=wrk)
        b.tt(wi3, ui3, srb, ALU.mult, r=uik + ["subr"], w=wik)
        b.tt(ui3, ur3, sib, ALU.mult, r=urk + ["subi"], w=uik)
        b.tt(ui3, ui3, wi3, ALU.add, r=uik + wik, w=uik)
        b.tt(ur3, ui3, b8, ALU.mult, r=uik + ["bpt"], w=urk)
        b.tt(wi3, ui3, b7, ALU.mult, r=uik + ["bpt"], w=wik)
        b.tt(ui3, wr3, b8, ALU.mult, r=wrk + ["bpt"], w=uik)
        b.tt(wr3, wr3, b7, ALU.mult, r=wrk + ["bpt"], w=wrk)
        b.tt(wr3, wr3, ur3, ALU.subtract, r=wrk + urk, w=wrk)
        b.tt(wi3, wi3, ui3, ALU.add, r=wik + uik, w=wik)
        rho_b = smc(g, 0).broadcast_to([128, NT])
        b.scan(urv[:, 0:NT], rho_b, wrv[:, 0:NT], 0.0, r=wrk + ["p20sm"], w=urk)
        b.scan(uiv[:, 0:NT], rho_b, wiv[:, 0:NT], 0.0, r=wik + ["p20sm"], w=uik)
        cosc, sinc = tgc(g, 0), tgc(g, 1)
        tte = b.sb["tte"]
        b.tt(tte[:, 0:1], smc(g, 7), cosc, ALU.mult, r=["p20sm", "trig"], w=["tte"])
        b.tt(tte[:, 1:2], smc(g, 8), sinc, ALU.mult, r=["p20sm", "trig"], w=["tte"])
        b.tt(tte[:, 2:3], smc(g, 7), sinc, ALU.mult, r=["p20sm", "trig"], w=["tte"])
        b.tt(tte[:, 3:4], smc(g, 8), cosc, ALU.mult, r=["p20sm", "trig"], w=["tte"])
        b.tt(tte[:, 4:5], tte[:, 0:1], tte[:, 1:2], ALU.subtract, r=["tte"], w=["tte"])
        b.tt(tte[:, 5:6], tte[:, 2:3], tte[:, 3:4], ALU.add, r=["tte"], w=["tte"])
        b.ts(tte[:, 5:6], tte[:, 5:6], -1.0, ALU.mult, r=["tte"], w=["tte"])
        ver = urv[:, NT - 1:NT]
        vei = uiv[:, NT - 1:NT]
        b.tt(tte[:, 0:1], tte[:, 4:5], ver, ALU.mult, r=["tte"] + urk, w=["tte"])
        b.tt(tte[:, 1:2], tte[:, 5:6], vei, ALU.mult, r=["tte"] + uik, w=["tte"])
        b.tt(tte[:, 2:3], tte[:, 4:5], vei, ALU.mult, r=["tte"] + uik, w=["tte"])
        b.tt(tte[:, 3:4], tte[:, 5:6], ver, ALU.mult, r=["tte"] + urk, w=["tte"])
        b.tt(smc(g, 11), tte[:, 0:1], tte[:, 1:2], ALU.subtract, r=["tte"],
             w=["p20sm"])
        b.tt(smc(g, 12), tte[:, 2:3], tte[:, 3:4], ALU.add, r=["tte"], w=["p20sm"])
        b.dma("gpsimd", b.dr["sc_p20"][:, g * NT:(g + 1) * NT],
              urv[:, 0:NT], r=urk, w=["sc_p20"], tag=f"p20vr{g}")
        b.dma("gpsimd", b.dr["sc_p20"][:, (NG + g) * NT:(NG + g + 1) * NT],
              uiv[:, 0:NT], r=uik, w=["sc_p20"], tag=f"p20vi{g}")
        b.dma("gpsimd", b.dr["cc_p20_in"][0:1, g * 128:(g + 1) * 128],
              smc(g, 11), r=["p20sm"], w=["cc_p20_in"], tag="p20st")
        b.dma("gpsimd", b.dr["cc_p20_in"][1:2, g * 128:(g + 1) * 128],
              smc(g, 12), r=["p20sm"], w=["cc_p20_in"], tag="p20st")
    b.allgather(b.dr["cc_p20_in"], b.dr["cc_p20_out"], r=["cc_p20_in"],
                w=["cc_p20_out"], tag="ag_p20")
    agv, agk = b.A1.view(4 * NT, 64)
    agst = b.sb["sqb"][0:16, 0:NG * 128]
    b.dma("sync", agst, b.dr["cc_p20_out"], r=["sqb", "cc_p20_out"], w=["sqb"], tag="p20agl")
    for g in range(NG):
        bank = b.ps[g % 2]
        b.tp(bank[:, 0:16], agst[:, g * 128:(g + 1) * 128], r=["sqb"],
             w=[f"bk{g % 2}"])
        b.copy(agv[:, g * 16:(g + 1) * 16], bank[:, 0:16], r=[f"bk{g % 2}"],
               w=agk, eng="scalar")

    def agc(a, comp, g):
        j = g * 16 + a * 2 + comp
        return agv[:, j:j + 1]

    pm = b.sb["premask"]
    tte = b.sb["tte"]
    ta = b.sb["tta"]
    e_ = 16 * (NG - 1) + 1

    def smv(j):
        return sm[:, j:j + e_:16]

    cr, ci_ = smv(11), smv(12)
    b.memset(cr, 0.0, w=["p20sm"])
    b.memset(ci_, 0.0, w=["p20sm"])
    for a in range(cfg.N_CORES):
        mc = pm[:, a:a + 1]
        agr = agv[:, a * 2:a * 2 + 16 * (NG - 1) + 1:16]
        agi = agv[:, a * 2 + 1:a * 2 + 1 + 16 * (NG - 1) + 1:16]
        b.tt(tte[:, 0:NG], smv(4), smv(5), ALU.mult, r=["p20sm"], w=["tte"])
        b.tt(tte[:, 4:4 + NG], smv(4), smv(6), ALU.mult, r=["p20sm"],
             w=["tte"])
        b.ts(tte[:, 0:NG], tte[:, 0:NG], -1.0, ALU.add, r=["tte"], w=["tte"])
        b.ts(tte[:, 0:NG], tte[:, 0:NG], mc[:, 0:1], ALU.mult,
             r=["tte", "premask"], w=["tte"])
        b.ts(tte[:, 0:NG], tte[:, 0:NG], 1.0, ALU.add, r=["tte"], w=["tte"])
        b.ts(tte[:, 4:4 + NG], tte[:, 4:4 + NG], mc[:, 0:1], ALU.mult,
             r=["tte", "premask"], w=["tte"])
        b.tt(tte[:, 8:8 + NG], cr, tte[:, 0:NG], ALU.mult,
             r=["p20sm", "tte"], w=["tte"])
        b.tt(tte[:, 12:12 + NG], ci_, tte[:, 4:4 + NG], ALU.mult,
             r=["p20sm", "tte"], w=["tte"])
        b.tt(ta[:, 0:NG], cr, tte[:, 4:4 + NG], ALU.mult,
             r=["p20sm", "tte"], w=["tta"])
        b.tt(ta[:, 4:4 + NG], ci_, tte[:, 0:NG], ALU.mult,
             r=["p20sm", "tte"], w=["tta"])
        b.tt(cr, tte[:, 8:8 + NG], tte[:, 12:12 + NG], ALU.subtract,
             r=["tte"], w=["p20sm"])
        b.tt(ci_, ta[:, 0:NG], ta[:, 4:4 + NG], ALU.add, r=["tta"],
             w=["p20sm"])
        b.stt(cr, agr, mc[:, 0:1], cr, ALU.mult, ALU.add,
              r=agk + ["premask", "p20sm"], w=["p20sm"])
        b.stt(ci_, agi, mc[:, 0:1], ci_, ALU.mult, ALU.add,
              r=agk + ["premask", "p20sm"], w=["p20sm"])
    e8 = 8 * (NG - 1) + 1
    cosc, sinc = tg[:, 0:e8:8], tg[:, 1:1 + e8:8]
    b.tt(tte[:, 0:NG], cr, cosc, ALU.mult, r=["p20sm", "trig"], w=["tte"])
    b.tt(tte[:, 4:4 + NG], ci_, sinc, ALU.mult, r=["p20sm", "trig"],
         w=["tte"])
    b.tt(tte[:, 8:8 + NG], cr, sinc, ALU.mult, r=["p20sm", "trig"],
         w=["tte"])
    b.tt(tte[:, 12:12 + NG], ci_, cosc, ALU.mult, r=["p20sm", "trig"],
         w=["tte"])
    b.tt(smv(9), tte[:, 0:NG], tte[:, 4:4 + NG], ALU.subtract, r=["tte"],
         w=["p20sm"])
    b.tt(smv(10), tte[:, 8:8 + NG], tte[:, 12:12 + NG], ALU.add, r=["tte"],
         w=["p20sm"])
    b.tt(smv(9), smv(9), smv(0), ALU.mult, r=["p20sm"], w=["p20sm"])
    b.tt(smv(10), smv(10), smv(0), ALU.mult, r=["p20sm"], w=["p20sm"])
    for g in range(NG):
        b.dma("sync", urv[:, 0:NT], b.dr["sc_p20"][:, g * NT:(g + 1) * NT],
              r=["sc_p20"], w=urk, tag=f"p20rr{g}")
        b.dma("sync", uiv[:, 0:NT],
              b.dr["sc_p20"][:, (NG + g) * NT:(NG + g + 1) * NT],
              r=["sc_p20"], w=uik, tag=f"p20ri{g}")
        bpt = b.sb["bpt"]
        bptr, bpti, rt = bpt[:, 0:16], bpt[:, 16:32], bpt[:, 32:48]
        _rpow_table(b, g, smc, rt, ntile)
        ta2, tb2 = b.sb["tta"], b.sb["ttb"]
        b.ts(ta2[:, 0:ntile], rt[:, 0:ntile], smc(g, 9)[:, 0:1], ALU.mult,
             r=["bpt", "p20sm"], w=["tta"])
        b.ts(tb2[:, 0:ntile], rt[:, 0:ntile], smc(g, 10)[:, 0:1], ALU.mult,
             r=["bpt", "p20sm"], w=["ttb"])
        ssl = slice(g * 128, (g + 1) * 128)
        ur3 = urv[:, 0:NT].rearrange("q (t c) -> q t c", c=128)
        ui3 = uiv[:, 0:NT].rearrange("q (t c) -> q t c", c=128)
        wr3 = wrv[:, 0:NT].rearrange("q (t c) -> q t c", c=128)
        wi3 = wiv[:, 0:NT].rearrange("q (t c) -> q t c", c=128)
        rpb = rp[:, ssl].unsqueeze(1).broadcast_to([128, ntile, 128])
        zzr = ta2[:, 0:ntile].unsqueeze(2).broadcast_to([128, ntile, 128])
        zzi = tb2[:, 0:ntile].unsqueeze(2).broadcast_to([128, ntile, 128])
        b.tt(wr3, rpb, zzr, ALU.mult, r=["rp", "tta"], w=wrk)
        b.tt(ur3, ur3, wr3, ALU.add, r=urk + wrk, w=urk)
        b.tt(wi3, rpb, zzi, ALU.mult, r=["rp", "ttb"], w=wik)
        b.tt(ui3, ui3, wi3, ALU.add, r=uik + wik, w=uik)
        _cpow_table(b, g, smc, bptr, bpti, ntile)
        srb = subr[:, ssl].unsqueeze(1).broadcast_to([128, ntile, 128])
        sib = subi[:, ssl].unsqueeze(1).broadcast_to([128, ntile, 128])
        b7 = bptr[:, 0:ntile].unsqueeze(2).broadcast_to([128, ntile, 128])
        b8 = bpti[:, 0:ntile].unsqueeze(2).broadcast_to([128, ntile, 128])
        b.tt(wr3, ur3, srb, ALU.mult, r=urk + ["subr"], w=wrk)
        b.tt(wi3, ui3, sib, ALU.mult, r=uik + ["subi"], w=wik)
        b.tt(wr3, wr3, wi3, ALU.add, r=wrk + wik, w=wrk)
        b.tt(wi3, ui3, srb, ALU.mult, r=uik + ["subr"], w=wik)
        b.tt(ui3, ur3, sib, ALU.mult, r=urk + ["subi"], w=uik)
        b.tt(wi3, wi3, ui3, ALU.subtract, r=wik + uik, w=wik)
        b.tt(ur3, wi3, b8, ALU.mult, r=wik + ["bpt"], w=urk)
        b.tt(ui3, wr3, b8, ALU.mult, r=wrk + ["bpt"], w=uik)
        b.tt(wr3, wr3, b7, ALU.mult, r=wrk + ["bpt"], w=wrk)
        b.tt(wi3, wi3, b7, ALU.mult, r=wik + ["bpt"], w=wik)
        b.tt(ur3, wr3, ur3, ALU.add, r=urk + wrk, w=urk)
        b.tt(ui3, wi3, ui3, ALU.subtract, r=uik + wik, w=uik)
        sq = b.sb["sqb"]
        nss = max(1, NT // 512)
        ssz = min(512, NT)
        for comp, buf, kk in ((0, urv, urk), (1, uiv, uik)):
            for s in range(nss):
                b.act(sq[:, 0:ssz], buf[:, s * ssz:(s + 1) * ssz], AF.Square,
                      r=kk, w=["sqb"])
                bank = b.ps[4 + s % 4]
                b.mm(bank[0:1, 0:ssz], b.const("ones_blk")[:, 0:1],
                     sq[:, 0:ssz],
                     start=(g == 0 and comp == 0),
                     stop=(g == NG - 1 and comp == 1),
                     r=["sqb", "consts"], w=[f"bk{4 + s % 4}"], tag="p20ssq")
        b.dma("gpsimd", b.dr["sc_p20"][:, g * NT:(g + 1) * NT],
              urv[:, 0:NT], r=urk, w=["sc_p20"], tag=f"p20yr{g}")
        b.dma("gpsimd", b.dr["sc_p20"][:, (NG + g) * NT:(NG + g + 1) * NT],
              uiv[:, 0:NT], r=uik, w=["sc_p20"], tag=f"p20yi{g}")
    rs = b.sb["rsrow"]
    nss = max(1, NT // 512)
    ssz = min(512, NT)
    for s in range(nss):
        bank = b.ps[4 + s % 4]
        bkk = f"bk{4 + s % 4}"
        b.act(rs[0:1, s * ssz:(s + 1) * ssz], bank[0:1, 0:ssz], AF.Ln,
              bias=b.const("rms_eps")[0:1, 0:1], scale=1.0 / cfg.D,
              r=[bkk, "consts"], w=["rsrow"])
    b.act(rs[0:1, 0:NT], rs[0:1, 0:NT], AF.Exp, scale=-0.5, r=["rsrow"],
          w=["rsrow"])


def _cpow_table(b, g, smc, tr, ti, n):
    tte = b.sb["tte"]
    ta, tb = b.sb["tta"], b.sb["ttb"]
    b.memset(tr[:, 0:1], 1.0, w=["bpt"])
    b.memset(ti[:, 0:1], 0.0, w=["bpt"])
    b.copy(tte[:, 8:9], smc(g, 2), r=["p20sm"], w=["tte"], eng="vector")
    b.copy(tte[:, 9:10], smc(g, 3), r=["p20sm"], w=["tte"], eng="vector")
    s = 1
    while s < n:
        cr, ci_ = tte[:, 8:9][:, 0:1], tte[:, 9:10][:, 0:1]
        b.ts(ta[:, 0:s], tr[:, 0:s], cr, ALU.mult, r=["bpt", "tte"],
             w=["tta"])
        b.ts(tb[:, 0:s], ti[:, 0:s], ci_, ALU.mult, r=["bpt", "tte"],
             w=["ttb"])
        b.tt(tr[:, s:2 * s], ta[:, 0:s], tb[:, 0:s], ALU.subtract,
             r=["tta", "ttb"], w=["bpt"])
        b.ts(ta[:, 0:s], ti[:, 0:s], cr, ALU.mult, r=["bpt", "tte"],
             w=["tta"])
        b.ts(tb[:, 0:s], tr[:, 0:s], ci_, ALU.mult, r=["bpt", "tte"],
             w=["ttb"])
        b.tt(ti[:, s:2 * s], ta[:, 0:s], tb[:, 0:s], ALU.add,
             r=["tta", "ttb"], w=["bpt"])
        b.tt(tte[:, 10:11], cr, cr, ALU.mult, r=["tte"], w=["tte"])
        b.tt(tte[:, 11:12], ci_, ci_, ALU.mult, r=["tte"], w=["tte"])
        b.tt(tte[:, 14:15], cr, ci_, ALU.mult, r=["tte"], w=["tte"])
        b.tt(tte[:, 8:9], tte[:, 10:11], tte[:, 11:12], ALU.subtract,
             r=["tte"], w=["tte"])
        b.tt(tte[:, 9:10], tte[:, 14:15], tte[:, 14:15], ALU.add,
             r=["tte"], w=["tte"])
        s *= 2
    b.copy(smc(g, 7), tte[:, 8:9], r=["tte"], w=["p20sm"], eng="vector")
    b.copy(smc(g, 8), tte[:, 9:10], r=["tte"], w=["p20sm"], eng="vector")


def _rpow_table(b, g, smc, rt, n):
    tte = b.sb["tte"]
    b.memset(rt[:, 0:1], 1.0, w=["bpt"])
    b.copy(tte[:, 8:9], smc(g, 1), r=["p20sm"], w=["tte"], eng="vector")
    s = 1
    while s < n:
        b.ts(rt[:, s:2 * s], rt[:, 0:s], tte[:, 8:9][:, 0:1], ALU.mult,
             r=["bpt", "tte"], w=["bpt"])
        b.tt(tte[:, 8:9], tte[:, 8:9], tte[:, 8:9], ALU.mult, r=["tte"],
             w=["tte"])
        s *= 2



CGW = 2
SEGW = CGW * 128 + 3


def ph_attn(b, cfg):
    b.memset(b.sb["tte"][:, 15:16], 0.0, r=["subr", "subi", "rp"],
             w=["A3.0", "A3.1", "tte"], tag="tbl_fence")
    for cg in range(cfg.NTILE // CGW):
        ph_ctrl_chunk(b, cfg, cg)
        for sub in range(CGW):
            ph_delta1(b, cfg, cg * CGW + sub)


def _ci(b, m):
    v, k = b.A1.view(m * SEGW, SEGW)
    return v, k


def ph_ctrl_chunk(b, cfg, cg):
    NT, D, C, QCH, NG = cfg.NT, cfg.D, cfg.C, cfg.QCH, cfg.NG
    W = CGW * 128
    t0 = cg * W
    xcb = b.sb["xcb"]
    par = cg % 2
    xck = f"xcb{par}"
    segw = QCH * (W + 3)
    xc = xcb[:, par * segw:(par + 1) * segw]
    CW = W + 3

    for tt_ in range(CGW):
        xt, xtk = b.A2.view(3072 + (tt_ % 2) * 1024, 1024)
        b.dma("sync", xt[:, 0:D],
              b.dr["x_tm"][t0 + tt_ * 128: t0 + (tt_ + 1) * 128, :], r=xtk,
              w=xtk, tag=f"xld{cg}_{tt_}")
        for f in range(QCH):
            bank = b.ps[(tt_ + f) % 2]
            bkk = f"bk{(tt_ + f) % 2}"
            b.tp(bank[:, 0:128], xt[:, f * 128:(f + 1) * 128], r=xtk, w=[bkk])
            b.copy(xc[:, f * CW + 3 + tt_ * 128: f * CW + 3 + (tt_ + 1) * 128],
                   bank[:, 0:128], r=[bkk], w=[xck], eng="scalar")
    xh, xhk = b.A2.view(5120, 1024)
    if cg == 0:
        b.dma("sync", xh[:3, 0:D], b.dr["x_halo"], r=xhk, w=xhk, tag="xhalo")
    else:
        b.dma("sync", xh[:3, 0:D], b.dr["x_tm"][t0 - 3:t0, :], r=xhk, w=xhk,
              tag=f"xhalo{cg}")
    for f in range(QCH):
        bank = b.ps[f % 2]
        bkk = f"bk{f % 2}"
        b.tp(bank[:, 0:3], xh[:3, f * 128:(f + 1) * 128], r=xhk, w=[bkk])
        b.copy(xc[:, f * CW: f * CW + 3], bank[:, 0:3], r=[bkk], w=[xck],
               eng="scalar")

    n = W + 3
    tails = b.sb["tails"]
    for m in range(26):
        if m < 24:
            mc, w0, wn, pbase = 128, 128 * m, 128, 0
        elif m == 24:
            mc, w0, wn, pbase = 16, 3072, 16, 0
        else:
            mc, w0, wn, pbase = 16, 3088, 16, 32
        wv, wk0 = b.A3.view((m % 4) * 1024, 1024)
        b.dma("sync", wv[:, 0:QCH * mc],
              b.dr["ctrl_W"][:, w0:w0 + wn]
              .rearrange("(k p) n -> p k n", p=128),
              r=wk0, w=wk0, tag=f"cw{cg}_{m}")
        bank = b.ps[2 + m % 4]
        bkk = f"bk{2 + m % 4}"
        items = []
        for kc in range(QCH):
            items.append((bank[pbase:pbase + mc, :n], wv[:, kc * mc:(kc + 1) * mc],
                          xc[:, kc * CW: kc * CW + n], kc == 0, kc == QCH - 1,
                          None if pbase == 0 else (0, pbase)))
        b.mmg(items, r=wk0 + [xck], w=[bkk], tag=f"cmm{cg}_{m}")
        if m < 24:
            seg, segk = _ci(b, m)
            if cg > 0:
                b.copy(seg[:, 0:3], tails[:, m * 3:(m + 1) * 3],
                       r=["tails"], w=segk, eng="vector")
            b.act(seg[:, 3:3 + W] if cg > 0 else seg[:, 0:n],
                  bank[:, 3:3 + W] if cg > 0 else bank[:, 0:n], AF.Identity,
                  bias=b.sb["ctrl_b"][:, m:m + 1], r=[bkk, "ctrl_b"], w=segk)
            b.copy(tails[:, m * 3:(m + 1) * 3], seg[:, W:W + 3], r=segk,
                   w=["tails"], eng="vector")
        else:
            b.act(b.sb["decb"][pbase:pbase + 16, t0:t0 + W],
                  bank[pbase:pbase + 16, 3:3 + W], AF.Identity,
                  bias=b.sb["ctrl_b"][pbase:pbase + 16, m:m + 1],
                  r=[bkk, "ctrl_b"], w=["decb"])
    dsl = slice(t0, t0 + W)
    gt = b.sb["sqb"]
    b.act(gt[:16, 0:W], b.sb["decb"][0:16, dsl], AF.Exp, scale=-1.0,
          r=["decb"], w=["sqb"])
    b.act(b.sb["g_all"][:16, dsl], gt[:16, 0:W], AF.Ln, bias=1.0, r=["sqb"],
          w=["g_all"])
    b.ts(b.sb["g_all"][:16, dsl], b.sb["g_all"][:16, dsl], -1.0, ALU.mult,
         r=["g_all"], w=["g_all"])
    b.act(b.sb["beta_all"][32:48, dsl], b.sb["decb"][32:48, dsl], AF.Sigmoid,
          r=["decb"], w=["beta_all"])

    gout = b.sb["outt"]
    for q4 in range(4):
        gwv, gwk = b.A3.view(2048 + (q4 % 2) * 2048, 2048)
        b.dma("sync", gwv[:, 0:QCH * 256],
              b.dr["ctrl_W"][:, 3104 + q4 * 256:3104 + (q4 + 1) * 256]
              .rearrange("(k p) n -> p k n", p=128),
              r=gwk, w=gwk, tag=f"gw{cg}_{q4}")
        for tt_ in range(CGW):
            bank = b.ps[4 + tt_ % 2]
            bkk = f"bk{4 + tt_ % 2}"
            items = [(bank[:, q4 % 2 * 256:(q4 % 2) * 256 + 256],
                      xc[:, kc * CW + 3 + tt_ * 128: kc * CW + 3 + (tt_ + 1) * 128],
                      gwv[:, kc * 256:(kc + 1) * 256], kc == 0, False, None)
                     for kc in range(QCH)]
            items.append((bank[:, q4 % 2 * 256:(q4 % 2) * 256 + 256],
                          b.const("ones_blk")[0:1, :],
                          b.sb["gate_b"][0:1, q4 * 256:(q4 + 1) * 256],
                          False, True, None))
            b.mmg(items, r=gwk + [xck, "gate_b", "consts"], w=[bkk],
                  tag=f"gmm{cg}_{q4}_{tt_}")
            b.copy(gout[:, (tt_ % 2) * 512 + (q4 % 2) * 256:
                        (tt_ % 2) * 512 + (q4 % 2) * 256 + 256],
                   bank[:, q4 % 2 * 256:(q4 % 2) * 256 + 256],
                   r=[bkk], w=["outt"], eng="scalar")
            b.dma("gpsimd",
                  b.dr["sc_gate"][t0 + tt_ * 128: t0 + (tt_ + 1) * 128,
                                  q4 * 256:(q4 + 1) * 256],
                  gout[:, (tt_ % 2) * 512 + (q4 % 2) * 256:
                       (tt_ % 2) * 512 + (q4 % 2) * 256 + 256],
                  r=["outt"], w=["sc_gate"], tag=f"gsp{cg}_{q4}_{tt_}")

    npv, npk = b.A2.view(0, QCH * 256)
    b.dma("sync", npv,
          b.dr["sc_p20"].rearrange("p (k t) -> p k t", t=NT)[:, :, t0:t0 + W],
          r=["sc_p20"], w=npk, tag=f"npl{cg}")
    bank_rs = b.ps[6]
    b.mm(bank_rs[:, 0:W], b.const("ones_blk")[0:1, :],
         b.sb["rsrow"][0:1, t0:t0 + W], r=["rsrow", "consts"],
         w=["bk6"], tag="rse")
    for f in range(QCH):
        vn = b.sb["vnorm"][:, f:f + 1]
        b.stt(npv[:, f * W:(f + 1) * W], npv[:, f * W:(f + 1) * W], vn[:, 0:1],
              bank_rs[:, 0:W], ALU.mult, ALU.mult,
              r=npk + ["vnorm", "bk6"], w=npk)
    convw, ramp = b.sb["convw"], b.sb["ramp"]
    for m in range(24):
        wpv, wpk = b.A3.view((m % 4) * 1024, 1024)
        b.dma("sync", wpv[:, 0:QCH * 128],
              b.dr["p20c_W"][:, 128 * m:128 * (m + 1)]
              .rearrange("(k p) n -> p k n", p=128),
              r=wpk, w=wpk, tag=f"pw{cg}_{m}")
        bank = b.ps[2 + m % 4]
        bkk = f"bk{2 + m % 4}"
        items = []
        for kc in range(QCH):
            items.append((bank[:, :W], wpv[:, kc * 128:(kc + 1) * 128],
                          npv[:, kc * W:(kc + 1) * W], kc == 0, kc == QCH - 1,
                          None))
        b.mmg(items, r=wpk + npk, w=[bkk], tag=f"pmm{cg}_{m}")
        seg, segk = _ci(b, m)
        tmp = b.sb["sqb"][:, (m % 2) * 512:(m % 2) * 512 + W]
        b.ts(tmp, seg[:, 0:W], convw[:, 4 * m:4 * m + 1][:, 0:1], ALU.mult,
             r=segk + ["convw"], w=["sqb"])
        for j in range(1, 4):
            b.stt(tmp, seg[:, j:j + W], convw[:, 4 * m + j:4 * m + j + 1][:, 0:1],
                  tmp, ALU.mult, ALU.add, r=segk + ["convw", "sqb"], w=["sqb"])
        b.act(tmp, tmp, AF.Silu, r=["sqb"], w=["sqb"])
        rcol = ramp[:, m % QCH: m % QCH + 1]
        b.stt(seg[:, 3:3 + W], bank[:, :W], rcol[:, 0:1], tmp,
              ALU.mult, ALU.add, r=[bkk, "ramp", "sqb"], w=segk)
    for part in range(2):
        sq = b.sb["sqb"]
        bank = b.ps[5 + part]
        bkk = f"bk{5 + part}"
        for f in range(QCH):
            m = part * QCH + f
            seg, segk = _ci(b, m)
            b.act(sq[:, 0:W], seg[:, 3:3 + W], AF.Square, r=segk, w=["sqb"])
            b.mm(bank[0:2, (f % 2) * W:(f % 2) * W + W],
                 b.const("R2"), sq[:, 0:W], r=["sqb", "consts"], w=[bkk],
                 tag="l2s")
            if f % 2 == 1 or f == QCH - 1:
                n4 = (f % 2) + 1
                lrs = b.sb["sqb"][:2, 512:512 + n4 * W]
                b.act(lrs, bank[0:2, 0:n4 * W], AF.Ln,
                      bias=b.const("l2_eps")[:2, 0:1], r=[bkk, "consts"],
                      w=["sqb"])
                b.act(lrs, lrs, AF.Exp, scale=-0.5, r=["sqb"], w=["sqb"])
                for ff in range(f - n4 + 1, f + 1):
                    mm_ = part * QCH + ff
                    seg2, seg2k = _ci(b, mm_)
                    bank2 = b.ps[7]
                    b.mm(bank2[:, 0:W],
                         b.const("E2")[0:2, :],
                         b.sb["sqb"][0:2, 512 + (ff % 2) * W:512 + (ff % 2) * W + W],
                         r=["sqb", "consts"], w=["bk7"], tag="l2e")
                    b.tt(seg2[:, 3:3 + W], seg2[:, 3:3 + W], bank2[:, 0:W],
                         ALU.mult, r=seg2k + ["bk7"], w=seg2k)


NEUM = 6


def ph_delta1(b, cfg, c):
    C, HP, HD = cfg.C, cfg.HP, cfg.HD
    colb, tte = b.sb["colb"], b.sb["tte"]
    sub = c % CGW
    o0 = 3 + sub * 128
    dsl = slice(c * C, (c + 1) * C)
    gc = b.sb["sqb"][:16, 0:C]
    ones_b = b.const("ones_blk")[:16, 0:1].broadcast_to([16, C])
    b.scan(gc, ones_b, b.sb["g_all"][:16, dsl], 0.0, r=["g_all", "consts"],
           w=["sqb"], tag="gcum")
    bv = b.sb["sqb"][:16, 2 * C:3 * C]
    stk = b.sb["sqb"][:48, 3 * C:4 * C]
    b.act(stk[0:16, :], gc, AF.Exp, r=["sqb"], w=["sqb"])
    b.act(bv, gc, AF.Exp, scale=-1.0, r=["sqb"], w=["sqb"])
    b.copy(b.sb["b_all"][:16, dsl], stk[0:16, :], r=["sqb"], w=["b_all"],
           eng="vector")
    b.dma("gpsimd", stk[16:32, :], bv, r=["sqb"], w=["sqb"], tag="binvmv")
    b.copy(stk[32:48, :], b.sb["beta_all"][32:48, dsl], r=["beta_all"], w=["sqb"],
           eng="vector")
    bank = b.ps[4]
    b.tp(bank[:, 0:48], stk, r=["sqb"], w=["bk4"])
    b.copy(colb[:, 0:48], bank[:, 0:48], r=["bk4"], w=["colb"], eng="scalar")

    GH = b.sb["GHst"]
    for pp in range(HP // 4):
        pinf = []
        for pi in range(4):
            p = 4 * pp + pi
            if pi < 2:
                ps_ = "" if p % 2 == 0 else "2"
                ktm, vtm = b.sb["ktm" + ps_], b.sb["vtm" + ps_]
                ktmk, vtmk = "ktm" + ps_, "vtm" + ps_
                kh, khk = b.sb["khtm" + ps_], "khtm" + ps_
                khfm, khfk = b.sb["khfm" + ps_], "khfm" + ps_
            elif pi == 2:
                ktm, k1_ = b.A3.view(4096, 128)
                vtm = b.A3.view(4224, 128)[0]
                kh = b.A3.view(4352, 128)[0]
                khfm = b.A3.view(4480, 128)[0]
                ktmk = vtmk = khk = khfk = k1_[0]
            else:
                ktm, k1_ = b.A2.view(3072, 128)
                vtm = b.A2.view(3200, 128)[0]
                kh = b.A2.view(3328, 128)[0]
                khfm = b.A2.view(3456, 128)[0]
                ktmk = vtmk = khk = khfk = k1_[0]
            mK, mV, mQ = 8 + p, 16 + p, p
            ksg, ksgk = _ci(b, mK)
            vsg, vsgk = _ci(b, mV)
            qsg, qsgk = _ci(b, mQ)
            bank = b.ps[5]
            b.tp(bank[:, 0:128], ksg[:, o0:o0 + C], r=ksgk, w=["bk5"])
            b.copy(ktm, bank[:, 0:128], r=["bk5"], w=[ktmk], eng="scalar")
            bank = b.ps[6]
            b.tp(bank[:, 0:128], vsg[:, o0:o0 + C], r=vsgk, w=["bk6"])
            b.copy(vtm, bank[:, 0:128], r=["bk6"], w=[vtmk], eng="scalar")
            for hh in range(2):
                h = 2 * p + hh
                b.ts(kh[:, hh * 64:(hh + 1) * 64], ktm[:, hh * 64:(hh + 1) * 64],
                     colb[:, 32 + h:32 + h + 1][:, 0:1], ALU.mult,
                     r=[ktmk, "colb"], w=[khk])
            bank = b.ps[7]
            b.tp(bank[:, 0:128], kh, r=[khk], w=["bk7"])
            b.copy(khfm, bank[:, 0:128], r=["bk7"], w=[khfk], eng="scalar")
            pinf.append(dict(p=p, ksg=ksg, ksgk=ksgk, qsg=qsg, qsgk=qsgk,
                             ktm=ktm, ktmk=ktmk, vtm=vtm, vtmk=vtmk,
                             kh=kh, khk=khk, khfm=khfm, khfk=khfk))
        ap_banks = ((b.ps[5], "bk5"), (b.ps[6], "bk6"),
                    (b.ps[7], "bk7"), (b.ps[4], "bk4"),
                    (b.ps[0], "bk0"), (b.ps[1], "bk1"),
                    (b.ps[2], "bk2"), (b.ps[3], "bk3"))
        cxs = []
        for pi in range(4):
            inf = pinf[pi]
            for hh in range(2):
                ci4 = 2 * pi + hh
                base = hh * 64
                if pi == 0:
                    s_ = "" if hh == 0 else "2"
                    bufs = {nm: (b.sb[nm + s_], nm + s_) for nm in
                            ("Y0", "Y1", "SqA", "SqB", "OqA", "OqB", "MQ")}
                elif pi == 1:
                    bufs = {}
                    for i3, nm in enumerate(
                            ("Y0", "Y1", "SqA", "SqB", "OqA", "OqB", "MQ")):
                        v, kk = b.A3.view(hh * 1024 + i3 * 128, 128)
                        bufs[nm] = (v, kk[0])
                elif pi == 2:
                    bufs = {}
                    for i3, nm in enumerate(
                            ("Y0", "Y1", "SqA", "SqB", "OqA", "OqB", "MQ")):
                        v, kk = b.A3.view(2048 + hh * 1024 + i3 * 128, 128)
                        bufs[nm] = (v, kk[0])
                else:
                    bufs = {}
                    for i3, nm in enumerate(
                            ("Y0", "Y1", "SqA", "SqB", "OqA", "OqB", "MQ")):
                        v, kk = b.A2.view(hh * 1024 + i3 * 128, 128)
                        bufs[nm] = (v, kk[0])
                cx = dict(
                    pi=pi, hh=hh, base=base, p=inf["p"], inf=inf,
                    kf=inf["ksg"][base:base + 64, o0:o0 + C],
                    qf=inf["qsg"][base:base + 64, o0:o0 + C],
                    khf=inf["khfm"][base:base + 64, :],
                    Y=[bufs["Y0"][0], bufs["Y1"][0]],
                    ky=[bufs["Y0"][1], bufs["Y1"][1]], cur=0,
                    MQ=bufs["MQ"][0], MQk=bufs["MQ"][1],
                    SqA=bufs["SqA"], SqB=bufs["SqB"],
                    OqA=bufs["OqA"], OqB=bufs["OqB"],
                    apb=ap_banks[ci4][0], apk=ap_banks[ci4][1],
                    sqb=ap_banks[ci4][0], sqk=ap_banks[ci4][1],
                    tp1=None if hh == 0 else (0, 64), tcol=8 + ci4,
                )
                cxs.append(cx)
        for cx in cxs:
            h = 2 * cx["p"] + cx["hh"]
            b.tt(tte[:, cx["tcol"]:cx["tcol"] + 1], colb[:, 32 + h:33 + h],
                 colb[:, 16 + h:17 + h], ALU.mult, r=["colb"], w=["tte"])
        for cx in cxs:
            ba = cx["base"]
            b.ts(cx["Y"][0][:, 0:64], cx["inf"]["vtm"][:, ba:ba + 64],
                 tte[:, cx["tcol"]:cx["tcol"] + 1][:, 0:1], ALU.mult,
                 r=[cx["inf"]["vtmk"], "tte"], w=[cx["ky"][0]])
        for cx in cxs:
            ba = cx["base"]
            b.copy(cx["Y"][0][:, 64:128], cx["inf"]["kh"][:, ba:ba + 64],
                   r=[cx["inf"]["khk"]], w=[cx["ky"][0]], eng="vector")
        for tag, lf, mname, dst in (
                ("grS", lambda c_: (c_["kf"], c_["khf"], [c_["inf"]["khfk"]]),
                 "mask_sT_neg", "SqA"),
                ("grO", lambda c_: (c_["khf"], c_["kf"], [c_["inf"]["khfk"]]),
                 "mask_s_neg", "OqA"),
                ("grMQ", lambda c_: (c_["kf"], c_["qf"], c_["inf"]["qsgk"]),
                 "mask_iT", "MQ")):
            for cx in cxs:
                l_, r_, xk = lf(cx)
                b.mm(cx["apb"][:, 0:128], l_, r_, r=cx["inf"]["ksgk"] + xk,
                     w=[cx["apk"]], tag=tag)
            for cx in cxs:
                dv, dk = (cx["MQ"], cx["MQk"]) if dst == "MQ" else cx[dst]
                b.tt(dv, cx["apb"][:, 0:128], b.const(mname),
                     ALU.mult, r=[cx["apk"], "consts"], w=[dk])
        for cx in cxs:
            cx["Sq"], cx["Sqk"] = cx["SqA"]
            cx["Oq"], cx["Oqk"] = cx["OqA"]
            cx["Sqn"], cx["Sqnk"] = cx["SqB"]
            cx["Oqn"], cx["Oqnk"] = cx["OqB"]
        for lev in range(NEUM + 1):
            for cx in cxs:
                cur = cx["cur"]
                b.mm(cx["apb"][:, 0:128], cx["Sq"], cx["Y"][cur],
                     r=[cx["ky"][cur], cx["Sqk"]], w=[cx["apk"]],
                     tag=f"ap{lev}")
            for cx in cxs:
                cur = cx["cur"]
                b.tt(cx["Y"][1 - cur], cx["apb"][:, 0:128], cx["Y"][cur],
                     ALU.add, r=[cx["apk"], cx["ky"][cur]],
                     w=[cx["ky"][1 - cur]], eng="vector")
                cx["cur"] = 1 - cur
            if lev == NEUM:
                break
            for cx in cxs:
                b.mm(cx["sqb"][:, 128:256], cx["Oq"], cx["Sq"],
                     r=[cx["Sqk"], cx["Oqk"]], w=[cx["sqk"]], tag="sqS")
            for cx in cxs:
                b.copy(cx["Sqn"], cx["sqb"][:, 128:256], r=[cx["sqk"]],
                       w=[cx["Sqnk"]], eng="scalar")
            if lev < NEUM - 1:
                for cx in cxs:
                    b.mm(cx["sqb"][:, 256:384], cx["Sq"], cx["Oq"],
                         r=[cx["Sqk"], cx["Oqk"]], w=[cx["sqk"]], tag="sqO")
                for cx in cxs:
                    b.copy(cx["Oqn"], cx["sqb"][:, 256:384], r=[cx["sqk"]],
                           w=[cx["Oqnk"]], eng="scalar")
                    cx["Oq"], cx["Oqn"], cx["Oqk"], cx["Oqnk"] = \
                        cx["Oqn"], cx["Oq"], cx["Oqnk"], cx["Oqk"]
            for cx in cxs:
                cx["Sq"], cx["Sqn"], cx["Sqk"], cx["Sqnk"] = \
                    cx["Sqn"], cx["Sq"], cx["Sqnk"], cx["Sqk"]
        for cx in cxs:
            cx["Yf"], cx["kyf"] = cx["Y"][cx["cur"]], cx["ky"][cx["cur"]]
            cx["fb"], cx["fbk"] = cx["apb"], cx["apk"]
            cx["kcol"] = cx["inf"]["ktm"][:, cx["base"]:cx["base"] + 64]

        def fod(cx, c0, c1):
            return (cx["fb"][0:64, c0:c1] if cx["hh"] == 0
                    else cx["fb"][64:128, c0:c1])
        for cx in cxs:
            cx["od"] = fod(cx, 0, 128)
            b.mm(cx["od"], cx["Yf"][:, 0:64], cx["MQ"],
                 r=[cx["kyf"], cx["MQk"]], w=[cx["fbk"]], tp=cx["tp1"],
                 tag="O1")
        for cx in cxs:
            ba, p = cx["base"], cx["p"]
            b.copy(b.sb["O1st"][ba:ba + 64, p * C:(p + 1) * C], cx["od"],
                   r=[cx["fbk"]], w=["O1st"], eng="scalar")
        for cx in cxs:
            b.mm(cx["od"], cx["Yf"][:, 64:128], cx["MQ"],
                 r=[cx["kyf"], cx["MQk"]], w=[cx["fbk"]], tp=cx["tp1"],
                 tag="W3")
        for cx in cxs:
            ba, p = cx["base"], cx["p"]
            b.tt(b.sb["Qpst"][ba:ba + 64, p * C:(p + 1) * C], cx["qf"],
                 cx["od"], ALU.subtract,
                 r=[cx["fbk"]] + cx["inf"]["qsgk"], w=["Qpst"])
        for cx in cxs:
            cx["od"] = fod(cx, 128, 192)
            b.mm(cx["od"], cx["Yf"][:, 64:128], cx["kcol"],
                 r=[cx["kyf"], cx["inf"]["ktmk"]], w=[cx["fbk"]],
                 tp=cx["tp1"], tag="Gt")
        for cx in cxs:
            ba, p = cx["base"], cx["p"]
            b.tt(GH[ba:ba + 64, p * 64:(p + 1) * 64],
                 b.const("I64pair")[ba:ba + 64, 0:64], cx["od"], ALU.subtract,
                 r=[cx["fbk"], "consts"], w=["GHst"])
        for cx in cxs:
            cx["od"] = fod(cx, 192, 256)
            b.mm(cx["od"], cx["kcol"], cx["Yf"][:, 64:128],
                 r=[cx["kyf"], cx["inf"]["ktmk"]], w=[cx["fbk"]],
                 tp=cx["tp1"], tag="Gn")
        for cx in cxs:
            ba, p = cx["base"], cx["p"]
            b.tt(b.sb["cmpb"][ba:ba + 64, p * 64:(p + 1) * 64],
                 b.const("I64pair")[ba:ba + 64, 0:64], cx["od"], ALU.subtract,
                 r=[cx["fbk"], "consts"], w=["cmpb"])
        for cx in cxs:
            cx["od"] = fod(cx, 256, 320)
            b.mm(cx["od"], cx["kcol"], cx["Yf"][:, 0:64],
                 r=[cx["kyf"], cx["inf"]["ktmk"]], w=[cx["fbk"]],
                 tp=cx["tp1"], tag="Hm")
        for cx in cxs:
            ba, p = cx["base"], cx["p"]
            b.copy(GH[ba:ba + 64, (HP + p) * 64:(HP + p + 1) * 64], cx["od"],
                   r=[cx["fbk"]], w=["GHst"], eng="scalar")
        for pi in range(4):
            p = 4 * pp + pi
            bank = b.ps[6]
            b.mm(bank[:, 0:1], b.const(f"E16_{p}")[0:16, :],
                 b.sb["b_all"][0:16, c * C + C - 1:c * C + C],
                 r=["b_all", "consts"], w=["bk6"], tag="bl")
            for which in ("gt", "gn", "h"):
                if which == "gt":
                    dst, dk = GH[:, p * 64:(p + 1) * 64], "GHst"
                elif which == "gn":
                    dst, dk = b.sb["cmpb"][:, p * 64:(p + 1) * 64], "cmpb"
                else:
                    dst, dk = GH[:, (HP + p) * 64:(HP + p + 1) * 64], "GHst"
                b.ts(dst, dst, bank[:, 0:1], ALU.mult, r=[dk, "bk6"], w=[dk])
            ph_compose(b, cfg, c, p)
    b.dma("gpsimd", b.dr["sc_O1"][c], b.sb["O1st"][:, :], r=["O1st"],
          w=["sc_O1"], tag=f"sO1_{c}")
    b.dma("gpsimd", b.dr["sc_Qp"][c], b.sb["Qpst"][:, :], r=["Qpst"],
          w=["sc_Qp"], tag=f"sQp_{c}")
    b.dma("gpsimd", b.dr["sc_GH"][c], b.sb["GHst"][:, :], r=["GHst"],
          w=["sc_GH"], tag=f"sGH_{c}")


def ph_compose(b, cfg, c, p):
    HP = cfg.HP
    GH = b.sb["GHst"]
    sl = slice(p * 64, (p + 1) * 64)
    gt_c = GH[:, sl]
    gn_c = b.sb["cmpb"][:, sl]
    h_c = GH[:, (HP + p) * 64:(HP + p + 1) * 64]
    Gs0, Gs1, Hs = b.sb["Gs0"], b.sb["Gs1"], b.sb["Hs"]
    if c == 0:
        b.copy(Gs0[:, sl], gt_c, r=["GHst"], w=["Gs0"], eng="vector")
        b.copy(Gs1[:, sl], gn_c, r=["cmpb"], w=["Gs1"], eng="vector")
        b.copy(Hs[:, sl], h_c, r=["GHst"], w=["Hs"], eng="vector")
        return
    cmpa = b.sb["cmpa"]
    for hh in range(2):
        base = hh * 64
        tp1 = None if hh == 0 else (hh * 64, 64)
        bank = b.ps[6]
        od = bank[0:64, 192:256] if hh == 0 else bank[64:128, 192:256]
        b.mm(od, Gs1[base:base + 64, sl], gt_c[base:base + 64, :],
             r=["Gs1", "GHst"], w=["bk6"], tp=tp1, tag="cGt")
        b.copy(cmpa[base:base + 64, sl], od, r=["bk6"], w=["cmpa"], eng="scalar")
        bank = b.ps[7]
        od = bank[0:64, 64:128] if hh == 0 else bank[64:128, 64:128]
        b.mm(od, gt_c[base:base + 64, :], Gs1[base:base + 64, sl],
             r=["Gs1", "GHst"], w=["bk7"], tp=None if hh == 0 else (64, 64),
             tag="cGn")
        bank = b.ps[4]
        od2 = bank[0:64, 192:256] if hh == 0 else bank[64:128, 192:256]
        b.mm(od2, gt_c[base:base + 64, :], Hs[base:base + 64, sl],
             r=["Hs", "GHst"], w=["bk4"], tp=None if hh == 0 else (64, 64),
             tag="cH")
        b.tt(Hs[base:base + 64, sl], od2, h_c[base:base + 64, :], ALU.add,
             r=["bk4", "GHst"], w=["Hs"])
        b.copy(Gs1[base:base + 64, sl], od, r=["bk7"], w=["Gs1"], eng="scalar")
    b.copy(Gs0[:, sl], cmpa[:, sl], r=["cmpa"], w=["Gs0"], eng="vector")


def ph_gh_ag(b, cfg):
    HP = cfg.HP
    gh = b.sb["ghin"]
    b.copy(gh[:, 0:HP * 64], b.sb["Gs0"][:, :], r=["Gs0"], w=["ghin"],
           eng="vector")
    b.copy(gh[:, HP * 64:], b.sb["Hs"][:, :], r=["Hs"], w=["ghin"], eng="vector")
    b.dma("gpsimd", b.dr["cc_gh_in"], gh[:, :], r=["ghin"], w=["cc_gh_in"],
          tag="ghst")
    b.allgather(b.dr["cc_gh_in"], b.dr["cc_gh_out"], r=["cc_gh_in"],
                w=["cc_gh_out"], tag="ag_gh")
    S0 = b.sb["S0"]
    b.memset(S0[:, :], 0.0, w=["S0"] + [f"S0.{p}" for p in range(HP)])
    pm = b.sb["premask"]
    cmpa, cmpb = b.sb["cmpa"], b.sb["cmpb"]
    for a in range(cfg.N_CORES):
        b.dma("sync", gh[:, :], b.dr["cc_gh_out"][a * 128:(a + 1) * 128, :],
              r=["cc_gh_out"], w=["ghin"], tag=f"ghl{a}")
        for p in range(HP):
            sl = slice(p * 64, (p + 1) * 64)
            gt_a = gh[:, sl]
            h_a = gh[:, (HP + p) * 64:(HP + p + 1) * 64]
            b.ts(cmpa[:, sl], gt_a, pm[:, a:a + 1][:, 0:1], ALU.mult,
                 r=["ghin", "premask"], w=["cmpa"])
            b.ts(cmpb[:, sl], b.const("I64pair")[:, 0:64],
                 pm[:, a:a + 1][:, 0:1], ALU.mult, r=["consts", "premask"],
                 w=["cmpb"])
            b.tt(cmpb[:, sl], b.const("I64pair")[:, 0:64], cmpb[:, sl],
                 ALU.subtract, r=["cmpb", "consts"], w=["cmpb"])
            b.tt(cmpa[:, sl], cmpa[:, sl], cmpb[:, sl], ALU.add,
                 r=["cmpa", "cmpb"], w=["cmpa"])
            for hh in range(2):
                base = hh * 64
                bank = b.ps[6]
                od = bank[0:64, 256:320] if hh == 0 else bank[64:128, 256:320]
                b.mm(od, cmpa[base:base + 64, sl], S0[base:base + 64, sl],
                     r=["cmpa", f"S0.{p}"], w=["bk6"],
                     tp=None if hh == 0 else (64, 64), tag="s0c")
                b.stt(S0[base:base + 64, sl], h_a[base:base + 64, :],
                      pm[base:base + 64, a:a + 1][:, 0:1], od,
                      ALU.mult, ALU.add, r=["ghin", "premask", "bk6"],
                      w=[f"S0.{p}"])


def ph_pass2(b, cfg):
    C, HP, HD = cfg.C, cfg.HP, cfg.HD
    S0 = b.sb["S0"]
    sets = []
    for par in range(2):
        O1v, O1k = b.A2.view(par * 3072, 1024)
        Qpv, Qpk = b.A2.view(par * 3072 + 1024, 1024)
        GHv, GHk = b.A2.view(par * 3072 + 2048, 1024)
        sets.append((O1v, O1k, Qpv, Qpk, GHv, GHk))
    for c in range(cfg.NTILE):
        O1, O1k, Qp, Qpk, GH, GHk = sets[c % 2]
        b.dma("sync", O1[:, :], b.dr["sc_O1"][c], r=O1k + ["sc_O1"], w=O1k,
              tag=f"lO1_{c}")
        b.dma("sync", Qp[:, :], b.dr["sc_Qp"][c], r=Qpk + ["sc_Qp"], w=Qpk,
              tag=f"lQp_{c}")
        b.dma("gpsimd", GH[:, :], b.dr["sc_GH"][c], r=GHk + ["sc_GH"], w=GHk,
              tag=f"lGH_{c}")
        pcx = []
        for p in range(HP):
            sc, sck = b.A1.view(p * 1024, 384)
            pcx.append((p, b.ps[p], f"bk{p}", sc, sck,
                        slice(p * 64, (p + 1) * 64)))
        for p, bank, kb, sc, sck, sl in pcx:
            b.mm(bank[:, 0:C], b.const(f"E16_{p}")[0:16, :],
                 b.sb["b_all"][0:16, c * C:(c + 1) * C],
                 r=["b_all", "consts"], w=[kb], tag="be")
        for p, bank, kb, sc, sck, sl in pcx:
            items = []
            for hh in range(2):
                base = hh * 64
                od = bank[0:64, 128:128 + C] if hh == 0 \
                    else bank[64:128, 128:128 + C]
                items.append((od, S0[base:base + 64, sl],
                              Qp[base:base + 64, p * C:(p + 1) * C],
                              True, True, None if hh == 0 else (64, 64)))
            b.mmg(items, r=[f"S0.{p}"] + Qpk, w=[kb], tag="op2")
        for p, bank, kb, sc, sck, sl in pcx:
            ot = sc[:, 0:C]
            b.tt(ot, bank[:, 128:128 + C], O1[:, p * C:(p + 1) * C], ALU.add,
                 r=[kb] + O1k, w=sck)
            b.tt(ot, ot, bank[:, 0:C], ALU.mult, r=sck + [kb], w=sck)
            b.act(sc[:, C:2 * C], ot, AF.Square, r=sck, w=sck)
        for p, bank, kb, sc, sck, sl in pcx:
            b.mm(bank[0:2, 256:256 + C], b.const("R2"), sc[:, C:2 * C],
                 r=sck + ["consts"], w=[kb], tag="on")
        for p, bank, kb, sc, sck, sl in pcx:
            ors = sc[:2, 2 * C:3 * C]
            b.act(ors, bank[0:2, 256:256 + C], AF.Ln,
                  bias=b.const("rms_eps_hd")[:2, 0:1], scale=1.0 / HD,
                  r=[kb, "consts"], w=sck)
            b.act(ors, ors, AF.Exp, scale=-0.5, r=sck, w=sck)
        for p, bank, kb, sc, sck, sl in pcx:
            b.mm(bank[:, 256:256 + C], b.const("E2")[0:2, :],
                 sc[0:2, 2 * C:3 * C], r=sck + ["consts"], w=[kb], tag="oe")
        for p, bank, kb, sc, sck, sl in pcx:
            ot = sc[:, 0:C]
            b.stt(ot, ot, b.sb["mnorm"][:, 0:1], bank[:, 256:256 + C],
                  ALU.mult, ALU.mult, r=sck + ["mnorm", kb], w=sck)
            b.dma("gpsimd",
                  b.dr["sc_reads"][:, (c * HP + p) * C:(c * HP + p + 1) * C],
                  ot, r=sck, w=["sc_reads"], tag="rsp")
        for p, bank, kb, sc, sck, sl in pcx:
            items = []
            for hh in range(2):
                base = hh * 64
                od = bank[0:64, 384:448] if hh == 0 else bank[64:128, 448:512]
                items.append((od, GH[base:base + 64, sl],
                              S0[base:base + 64, sl],
                              True, True, None if hh == 0 else (64, 64)))
            b.mmg(items, r=GHk + [f"S0.{p}"], w=[kb], tag="sw")
        for p, bank, kb, sc, sck, sl in pcx:
            for hh in range(2):
                base = hh * 64
                od = bank[0:64, 384:448] if hh == 0 else bank[64:128, 448:512]
                b.tt(S0[base:base + 64, sl], od,
                     GH[base:base + 64, (HP + p) * 64:(HP + p + 1) * 64],
                     ALU.add, r=[kb] + GHk, w=[f"S0.{p}"])


def ph_outproj(b, cfg):
    NT, D, C, QCH = cfg.NT, cfg.D, cfg.C, cfg.QCH
    gsl = b.sb["outt"]
    lnst = b.sb["lnst"]
    for nh in range(2):
        wob, wobk = b.A1.view(0, 2 * QCH * 512)
        for half in range(2):
            b.dma("sync", wob[:, half * QCH * 512:(half + 1) * QCH * 512],
                  b.dr["out_W"][half * D:(half + 1) * D, nh * 512:(nh + 1) * 512]
                  .rearrange("(k p) n -> p k n", p=128),
                  r=wobk, w=wobk, tag=f"owl{nh}_{half}")
        for t in range(cfg.NTILE):
            fuse, fusek = b.A2.view((t % 2) * 2 * QCH * 128, 2 * QCH * 128)
            b.dma("sync", fuse[:, 0:QCH * 128],
                  b.dr["sc_reads"][:, t * QCH * 128:(t + 1) * QCH * 128],
                  r=["sc_reads"] + fusek, w=fusek, tag=f"fr{t}_{nh}")
            b.dma("gpsimd", fuse[:, QCH * 128:2 * QCH * 128],
                  b.dr["sc_p20"].rearrange("p (k n) -> p k n", n=NT)[:, :, t * C:(t + 1) * C],
                  r=["sc_p20"] + fusek, w=fusek, tag=f"fp{t}_{nh}")
            bank_rs = b.ps[4 + t % 2]
            krs = f"bk{4 + t % 2}"
            b.mm(bank_rs[:, 0:C], b.const("ones_blk")[0:1, :],
                 b.sb["rsrow"][0:1, t * C:(t + 1) * C], r=["rsrow", "consts"],
                 w=[krs], tag="rse2")
            for f in range(QCH):
                vn = b.sb["vnorm"][:, f:f + 1]
                sl_ = slice(QCH * 128 + f * 128, QCH * 128 + (f + 1) * 128)
                b.stt(fuse[:, sl_], fuse[:, sl_], vn[:, 0:1], bank_rs[:, 0:C],
                      ALU.mult, ALU.mult, r=fusek + ["vnorm", krs], w=fusek)
            if nh == 0:
                gpart = gsl[:, (t % 2) * 512:(t % 2) * 512 + 512]
            else:
                gpart = gsl[:, 512:1024]
                b.dma("scalar", gsl[:, 0:512],
                      b.dr["sc_x1"][t * C:(t + 1) * C, 0:512],
                      r=["sc_x1", "outt"], w=["outt"], tag=f"lnl{t}")
            b.dma("scalar", gpart,
                  b.dr["sc_gate"][t * C:(t + 1) * C, nh * 512:(nh + 1) * 512],
                  r=["sc_gate", "outt"], w=["outt"], tag=f"gl{t}_{nh}")
            b.act(gpart, gpart, AF.Sigmoid, r=["outt"], w=["outt"])
            xt, xtk = b.A2.view(4096 + (t % 2) * 1024, 512)
            b.dma("scalar", xt,
                  b.dr["x_tm"][t * C:(t + 1) * C, nh * 512:(nh + 1) * 512],
                  r=xtk, w=xtk, tag=f"xr{t}_{nh}")
            bank = b.ps[6 + t % 2]
            bkk = f"bk{6 + t % 2}"
            items = []
            for half in range(2):
                woff = half * QCH * 512
                items += [(bank[:, 0:512],
                           fuse[:, half * QCH * 128 + kc * 128:
                                half * QCH * 128 + (kc + 1) * 128],
                           wob[:, woff + kc * 512: woff + (kc + 1) * 512],
                           (half == 0 and kc == 0), False, None)
                          for kc in range(QCH)]
            items.append((bank[:, 0:512], b.const("ones_blk")[0:1, :],
                          b.sb["out_b"][0:1, nh * 512:(nh + 1) * 512],
                          False, True, None))
            b.mmg(items, r=fusek + wobk + ["out_b", "consts"], w=[bkk],
                  tag=f"om{t}_{nh}")
            b.tt(gpart, gpart, bank[:, 0:512], ALU.mult, r=["outt", bkk],
                 w=["outt"])
            b.tt(gpart, gpart, xt, ALU.add, r=["outt"] + xtk, w=["outt"])
            if nh == 0:
                b.dma("sync",
                      b.dr["sc_x1"][t * C:(t + 1) * C, 0:512],
                      gpart, r=["outt"], w=["sc_x1"], tag=f"x1s{t}_{nh}")
            else:
                _layernorm_tm(b, cfg, gsl[:, 0:D], lnst, 0)
                b.dma("sync", b.dr["sc_x1"][t * C:(t + 1) * C, :],
                      gsl[:, 0:D], r=["outt"], w=["sc_x1"], tag=f"x1s{t}")


def _layernorm_tm(b, cfg, row, lnst, base, rowk=("outt",)):
    D = cfg.D
    rk = list(rowk)
    sq = b.sb["sqb"]
    b.red(lnst[:, base:base + 1], row, ALU.add, r=rk, w=["lnst"])
    b.act(sq[:, 0:D], row, AF.Square, r=rk, w=["sqb"])
    b.red(lnst[:, base + 1:base + 2], sq[:, 0:D], ALU.add, r=["sqb"], w=["lnst"])
    b.ts(lnst[:, base:base + 1], lnst[:, base:base + 1], 1.0 / D, ALU.mult,
         r=["lnst"], w=["lnst"])
    b.ts(lnst[:, base + 1:base + 2], lnst[:, base + 1:base + 2], 1.0 / D,
         ALU.mult, r=["lnst"], w=["lnst"])
    b.tt(lnst[:, base + 2:base + 3], lnst[:, base:base + 1],
         lnst[:, base:base + 1], ALU.mult, r=["lnst"], w=["lnst"])
    b.tt(lnst[:, base + 2:base + 3], lnst[:, base + 1:base + 2],
         lnst[:, base + 2:base + 3], ALU.subtract, r=["lnst"], w=["lnst"])
    b.act(lnst[:, base + 3:base + 4], lnst[:, base + 2:base + 3], AF.Ln,
          bias=b.const("ln_eps")[:, 0:1], r=["lnst", "consts"], w=["lnst"])
    b.act(lnst[:, base + 3:base + 4], lnst[:, base + 3:base + 4], AF.Exp,
          scale=-0.5, r=["lnst"], w=["lnst"])
    b.ts(row, row, lnst[:, base:base + 1], ALU.subtract, r=rk + ["lnst"],
         w=rk)
    b.ts(row, row, lnst[:, base + 3:base + 4], ALU.mult, r=rk + ["lnst"],
         w=rk)


def ph_ffn(b, cfg):
    D, C, QCH, ICH = cfg.D, cfg.C, cfg.QCH, cfg.ICH
    I = cfg.I_FFN
    scl = b.sb["scl"]
    lnst = b.sb["lnst"]
    outt = b.sb["outt"]
    TG = min(4, cfg.NTILE)
    lanes = [b.A2.view(i * 512, 512) for i in range(4)]
    lk = [ln[1][0] for ln in lanes]

    def quant_cols(out_ap, in_ap, ncols, rkeys, wkeys, tag):
        nblk = ncols // 16
        sc = scl[:, 0:nblk]
        b.red(sc, in_ap.rearrange("p (g k) -> p g k", k=16), ALU.max,
              absval=True, r=rkeys, w=["scl"], tag=tag)
        b.ts(sc, sc, 1.0 / 6.0, ALU.mult, 1e-12, ALU.max, r=["scl"], w=["scl"])
        rsc = scl[:, 176:176 + nblk]
        b.act(rsc, sc, AF.Ln, r=["scl"], w=["scl"])
        b.act(rsc, rsc, AF.Exp, scale=-1.0, r=["scl"], w=["scl"])
        hsc = scl[:, 352:352 + nblk]
        b.ts(hsc, sc, 0.5, ALU.mult, r=["scl"], w=["scl"])
        rexp = rsc.unsqueeze(2).broadcast_to([128, nblk, 16])
        hexp = hsc.unsqueeze(2).broadcast_to([128, nblk, 16])
        lv = [ln[0][:, 0:ncols].rearrange("p (g k) -> p g k", k=16)
              for ln in lanes]
        b.quant(out_ap.rearrange("p (g k) -> p g k", k=16),
                in_ap.rearrange("p (g k) -> p g k", k=16), rexp, hexp, lv, lk,
                list(rkeys) + ["scl"], wkeys)

    NSL2 = (I + 511) // 512
    wslots = [b.A3.view(0, 2048), b.A3.view(2048, 2048),
              b.A3.view(4096, 2048), b.A2.view(2048, 2048),
              b.A2.view(4096, 2048)]
    wcnt = 0
    for tg in range(cfg.NTILE // TG):
        for j in range(TG):
            t = tg * TG + j
            x1v, x1k = b.A1.view(5120, 1024)
            b.dma("sync" if j % 2 == 0 else "gpsimd", x1v,
                  b.dr["sc_x1"][t * C:(t + 1) * C, :],
                  r=x1k + ["sc_x1"], w=x1k, tag=f"x1l{t}")
            xq, xqk = b.A1.view(0, 1024)
            for hx in range(2):
                quant_cols(xq[:, hx * 512:(hx + 1) * 512],
                           x1v[:, hx * 512:(hx + 1) * 512], 512, x1k, xqk,
                           f"xq{t}_{hx}")
            xqf, xqfk = b.A1.view(1024 + j * 1024, 1024)
            for f in range(QCH):
                bank = b.ps[6 + f % 2]
                bkk = f"bk{6 + f % 2}"
                b.tp(bank[:, 0:128], xq[:, f * 128:(f + 1) * 128], r=xqk, w=[bkk])
                b.copy(xqf[:, f * 128:(f + 1) * 128], bank[:, 0:128], r=[bkk],
                       w=xqfk, eng="scalar")
        for n2 in range(NSL2):
            w2 = min(512, I - n2 * 512)
            nsl = w2 // 256
            gsl_, usl_ = [], []
            for sh in range(nsl):
                ns = n2 * 2 + sh
                wgv, wgk = wslots[wcnt % 5]
                wcnt += 1
                b.dma("sync", wgv[:, 0:QCH * 256],
                      b.dr["cc_wg_out"][:, ns * 256:(ns + 1) * 256]
                      .rearrange("(k p) n -> p k n", p=128),
                      r=wgk + ["cc_wg_out"], w=wgk, tag=f"fwg{tg}_{ns}")
                gsl_.append((wgv, wgk))
            for sh in range(nsl):
                ns = n2 * 2 + sh
                wuv, wuk = wslots[wcnt % 5]
                wcnt += 1
                b.dma("gpsimd", wuv[:, 0:QCH * 256],
                      b.dr["cc_wu_out"][:, ns * 256:(ns + 1) * 256]
                      .rearrange("(k p) n -> p k n", p=128),
                      r=wuk + ["cc_wu_out"], w=wuk, tag=f"fwu{tg}_{ns}")
                usl_.append((wuv, wuk))
            gks = sum((g[1] for g in gsl_), [])
            uks = sum((u[1] for u in usl_), [])
            for j in range(TG):
                t = tg * TG + j
                xqf, xqfk = b.A1.view(1024 + j * 1024, 1024)
                bkg = b.ps[j]
                bku = b.ps[4 + j]
                kg, ku = f"bk{j}", f"bk{4 + j}"
                b.mmg([(bkg[:, sh * 256:(sh + 1) * 256],
                        xqf[:, kc * 128:(kc + 1) * 128],
                        gsl_[sh][0][:, kc * 256:(kc + 1) * 256],
                        kc == 0, kc == QCH - 1, None)
                       for sh in range(nsl) for kc in range(QCH)],
                      r=xqfk + gks, w=[kg], tag=f"fmg{t}_{n2}")
                b.mmg([(bku[:, sh * 256:(sh + 1) * 256],
                        xqf[:, kc * 128:(kc + 1) * 128],
                        usl_[sh][0][:, kc * 256:(kc + 1) * 256],
                        kc == 0, kc == QCH - 1, None)
                       for sh in range(nsl) for kc in range(QCH)],
                      r=xqfk + uks, w=[ku], tag=f"fmu{t}_{n2}")
            for j in range(TG):
                t = tg * TG + j
                bkg, bku = b.ps[j], b.ps[4 + j]
                kg, ku = f"bk{j}", f"bk{4 + j}"
                jo = (0, 5120, 6144, 7168)[j]
                hsl, hslk = b.A1.view(jo, 512)
                b.act(hsl[:, 0:w2], bkg[:, 0:w2], AF.Silu, r=[kg], w=hslk)
                b.tt(hsl[:, 0:w2], hsl[:, 0:w2], bku[:, 0:w2], ALU.mult,
                     r=hslk + [ku], w=hslk)
                hqs, hqsk = b.A1.view(jo + 512, 512)
                quant_cols(hqs[:, 0:w2], hsl[:, 0:w2], w2, hslk, hqsk,
                           f"hq{t}_{n2}")
            for j in range(TG):
                t = tg * TG + j
                bku, ku = b.ps[4 + j], f"bk{4 + j}"
                jo = (0, 5120, 6144, 7168)[j]
                hsl, hslk = b.A1.view(jo, 512)
                hqs, hqsk = b.A1.view(jo + 512, 512)
                stg, stgk = hsl, hslk
                for k2 in range(w2 // 128):
                    b.tp(bku[:, k2 * 128:(k2 + 1) * 128],
                         hqs[:, k2 * 128:(k2 + 1) * 128], r=hqsk, w=[ku])
                for k2 in range(nsl):
                    b.copy(stg[:, k2 * 256:(k2 + 1) * 256],
                           bku[:, k2 * 256:(k2 + 1) * 256],
                           r=[ku], w=stgk, eng="scalar")
                b.dma("sync" if j % 2 == 0 else "gpsimd",
                      b.dr["sc_hq"][t][:, n2 * 512:n2 * 512 + w2],
                      stg[:, 0:w2], r=stgk, w=[f"sc_hq{t}"],
                      tag=f"hqsp{t}_{n2}")
    hqslots = [b.A1.view(0, ICH * 128), b.A1.view(ICH * 128, ICH * 128),
               b.A2.view(0, ICH * 128)]
    x1v, x1k = b.A2.view(3072, 2048)
    outb, outbk = b.A2.view(5120, 1024)
    for pp in range(cfg.NTILE // 2):
        ta, tb = 2 * pp, 2 * pp + 1
        hqA, hqAk = hqslots[(2 * pp) % 3]
        hqB, hqBk = hqslots[(2 * pp + 1) % 3]
        b.dma("scalar", hqA, b.dr["sc_hq"][ta], r=hqAk + [f"sc_hq{ta}"],
              w=hqAk, tag=f"hql{ta}")
        b.dma("gpsimd", hqB, b.dr["sc_hq"][tb], r=hqBk + [f"sc_hq{tb}"],
              w=hqBk, tag=f"hql{tb}")
        b.dma("scalar", x1v[:, 0:1024], b.dr["sc_x1"][ta * C:(ta + 1) * C, :],
              r=x1k + ["sc_x1"], w=x1k, tag=f"x1r{ta}")
        b.dma("sync", x1v[:, 1024:2048],
              b.dr["sc_x1"][tb * C:(tb + 1) * C, :],
              r=x1k + ["sc_x1"], w=x1k, tag=f"x1r{tb}")
        for nh in range(4):
            bkA, bkAk = b.ps[4 + nh % 2], f"bk{4 + nh % 2}"
            bkB, bkBk = b.ps[6 + nh % 2], f"bk{6 + nh % 2}"
            for hf in range(2):
                k0 = hf * 11
                k1 = 11 if hf == 0 else ICH - 11
                wdv, wdk = b.A3.view(hf * 2816, 2816)
                b.dma("sync" if hf == 0 else "gpsimd", wdv[:, 0:k1 * 256],
                      b.dr["cc_wd_out"][k0 * 128:(k0 + k1) * 128,
                                        nh * 256:(nh + 1) * 256]
                      .rearrange("(k p) n -> p k n", p=128),
                      r=wdk + ["cc_wd_out"], w=wdk, tag=f"dwl{pp}_{nh}_{hf}")
                b.mmg([(bkA[:, 0:256],
                        hqA[:, (k0 + kc) * 128:(k0 + kc + 1) * 128],
                        wdv[:, kc * 256:(kc + 1) * 256],
                        (hf == 0 and kc == 0), (hf == 1 and kc == k1 - 1),
                        None) for kc in range(k1)],
                      r=hqAk + wdk, w=[bkAk], tag=f"dmA{pp}_{nh}_{hf}")
                b.mmg([(bkB[:, 0:256],
                        hqB[:, (k0 + kc) * 128:(k0 + kc + 1) * 128],
                        wdv[:, kc * 256:(kc + 1) * 256],
                        (hf == 0 and kc == 0), (hf == 1 and kc == k1 - 1),
                        None) for kc in range(k1)],
                      r=hqBk + wdk, w=[bkBk], tag=f"dmB{pp}_{nh}_{hf}")
            b.tt(outt[:, nh * 256:(nh + 1) * 256], bkA[:, 0:256],
                 x1v[:, nh * 256:(nh + 1) * 256], ALU.add, r=[bkAk] + x1k,
                 w=["outt"], tag=f"dresA{pp}_{nh}")
            b.tt(outb[:, nh * 256:(nh + 1) * 256], bkB[:, 0:256],
                 x1v[:, 1024 + nh * 256:1024 + (nh + 1) * 256], ALU.add,
                 r=[bkBk] + x1k, w=outbk, tag=f"dresB{pp}_{nh}")
        _layernorm_tm(b, cfg, outt[:, 0:D], lnst, 4)
        b.dma("gpsimd", b.dr["y"][ta * C:(ta + 1) * C, :], outt[:, 0:D],
              r=["outt"], w=["y_out"], tag=f"yst{ta}")
        _layernorm_tm(b, cfg, outb, lnst, 4, rowk=outbk)
        b.dma("sync", b.dr["y"][tb * C:(tb + 1) * C, :], outb,
              r=outbk, w=["y_out"], tag=f"yst{tb}")



def kernel(**inputs):
    cfg = CFG()
    in_maps, cnames = prep_inputs(cfg, inputs)
    nc = build_nc(cfg, cnames)
    res = run_bass_kernel_spmd(nc, in_maps, core_ids=list(range(cfg.N_CORES)))
    span = cfg.S // cfg.SPANS
    out = np.empty((cfg.B, cfg.S, cfg.D), np.float32)
    for c in range(cfg.N_CORES):
        bb = c // cfg.SPANS
        s0 = (c % cfg.SPANS) * span
        out[bb, s0:s0 + span] = res.results[c]["y"]
    return out

